# version 15
# baseline (speedup 1.0000x reference)
import numpy as np
import ml_dtypes
from contextlib import ExitStack

import concourse.bass as bass
import concourse.mybir as mybir
from concourse.bass_utils import run_bass_kernel_spmd

F32 = mybir.dt.float32
BF16 = mybir.dt.bfloat16
U32 = mybir.dt.uint32
AF = mybir.ActivationFunctionType
ALU = mybir.AluOpType
AX = mybir.AxisListType.X

T = 4096
C = 1024
NCORES = 8

N_PHASES = 4
DEV_DUMP = False
EMIT_NOPS = True


class Env:
    def __init__(self, nc, top):
        self.nc = nc
        self.top = top
        self.d = {}
        self.pool = []
        self.cursor = 0

    def sem(self, name):
        if self.cursor < len(self.pool):
            h = self.pool[self.cursor]
        else:
            h = self.top.enter_context(self.nc.semaphore(f"pool{len(self.pool)}"))
            self.pool.append(h)
        self.cursor += 1
        return h

    def persistent_sem(self, name):
        return self.top.enter_context(self.nc.semaphore(name))

    def phase_boundary(self, fence):
        n_used = self.cursor
        self.cursor = 0
        self._bnd_n = getattr(self, "_bnd_n", 0) + 1
        bnd = self.persistent_sem(f"bnd{self._bnd_n}")
        with self.nc.Block(f"boundary{self._bnd_n}", no_gpsimd_drain=True) as blk:
            @blk.gpsimd
            def _(g):
                for sem_, val_ in fence:
                    g.wait_ge(sem_, val_)
                g.dma_reset()
                for h in self.pool[:n_used]:
                    g.sem_clear(h)
                g.sem_inc(bnd, 1)
        return [(bnd, 1)]


def _tnop(eng):
    if EMIT_NOPS:
        eng.nop(cycle_cnt=512, nofuse=True)


def _ln_stats(v, a, sx, ssum, ssq, mcols, e2, msq, var, sqj, std, rstd, nsub, sem_sq, sq_target, sem_std, std_target, inv_n):
    raise NotImplementedError


def phase_l0_attn(env, x_in, x_out, fence):
    nc = env.nc
    d = env.d

    def fw(eng):
        for sem_, val_ in fence:
            eng.wait_ge(sem_, val_)

    lnx16 = nc.dram_tensor("l0a_lnx16", [8 * T, 128], BF16, kind="Internal").ap()
    qfm = nc.dram_tensor("l0a_qfm", [C, T], BF16, kind="Internal").ap()
    kfm = nc.dram_tensor("l0a_kfm", [C, T], BF16, kind="Internal").ap()
    vtm = nc.dram_tensor("l0a_vtm", [T, C], BF16, kind="Internal").ap()

    S = env.sem
    pL = S("l0a_pL"); pLs = S("l0a_pLs"); pV = S("l0a_pV")
    pTwc = S("l0a_pTwc"); pVwc = S("l0a_pVwc"); pTbc = S("l0a_pTbc")
    pAbc = S("l0a_pAbc"); pTbp = S("l0a_pTbp"); pVbp = S("l0a_pVbp")
    aLx = S("l0a_aLx"); aLt = S("l0a_aLt"); aAsq = S("l0a_aAsq")
    aVst1 = S("l0a_aVst1"); aAstd = S("l0a_aAstd"); aVln = S("l0a_aVln")
    aLst = S("l0a_aLst"); aLtr = S("l0a_aLtr")
    aTq = S("l0a_aTq"); aTk = S("l0a_aTk"); aTv = S("l0a_aTv")
    aVq = S("l0a_aVq"); aAk = S("l0a_aAk"); aPk = S("l0a_aPk"); aVv = S("l0a_aVv")
    aSt = S("l0a_aSt")
    bLqv = S("l0a_bLqv"); bLk = S("l0a_bLk"); bLx0 = S("l0a_bLx0")
    bTw = S("l0a_bTw"); bAsp = S("l0a_bAsp"); bVmk = S("l0a_bVmk")
    bToj = S("l0a_bToj"); bVep = S("l0a_bVep"); bSx1 = S("l0a_bSx1")

    NT = T // 512
    NI = T // 256
    jts = [2 * it + 2 for it in range(NI)]
    cum0 = [sum(jts[:it]) for it in range(NI)]
    CUMJ = sum(jts)
    dg0 = [2 * it for it in range(NI)]

    with ExitStack() as ph:
        sb = lambda n, s, dt=F32: ph.enter_context(nc.sbuf_tensor(n, s, dt))
        smaskA = sb("l0a_maskA", [128, 256])
        smaskB = sb("l0a_maskB", [128, 256])
        sbpb = sb("l0a_bpb", [128, C])
        seps = sb("l0a_eps", [128, 1])
        sonec = sb("l0a_onec", [128, 1])

        saw = ExitStack()
        sbw = lambda n, s, dt=F32: saw.enter_context(nc.sbuf_tensor(n, s, dt))
        sones16 = sbw("l0a_ones16", [1, 128], BF16)
        sones512 = sbw("l0a_ones512", [1, 512], BF16)
        swcomb = sbw("l0a_wcomb", [128, 8, C], BF16)
        sbcomb = sbw("l0a_bcomb", [1, C], BF16)
        sbq = sbw("l0a_bq", [1, C], BF16)
        sbk = sbw("l0a_bk", [1, C], BF16)
        swq = sbw("l0a_wq", [128, 8, C], BF16)
        swk = sbw("l0a_wk", [128, 8, C], BF16)

        with ExitStack() as pp:
            psb = lambda n, s, dt=F32: pp.enter_context(nc.sbuf_tensor(n, s, dt))
            pps = lambda n, s, dt=F32: pp.enter_context(nc.psum_tensor(n, s, dt))
            swvT = psb("l0a_wvT", [128, 8, C], BF16)
            swp = psb("l0a_wp", [128, 8, C], BF16)
            sbv = psb("l0a_bv", [128, 8], BF16)
            sbpr = psb("l0a_bpr", [1, C])
            sonesf = psb("l0a_onesf", [1, 128])
            ppsA = pps("l0a_ppsA", [128, 512])
            ppsB = pps("l0a_ppsB", [128, 512])
            ppsR = pps("l0a_ppsR", [1, 512])

            blk = pp.enter_context(nc.Block("l0a_prep", no_gpsimd_drain=True))

            @blk.gpsimd
            def _(g):
                fw(g)
                g.dma_start(out=swq[:], in_=d["l0_wq"].rearrange("(k p) c -> p k c", p=128)).then_inc(pL, 16)
                g.dma_start(out=swk[:], in_=d["l0_wk"].rearrange("(k p) c -> p k c", p=128)).then_inc(pL, 16)
                g.dma_start(out=swvT[:], in_=d["l0_wvT"].rearrange("(k p) c -> p k c", p=128)).then_inc(pL, 16)
                g.dma_start(out=swp[:], in_=d["l0_wp"].rearrange("(k p) c -> p k c", p=128)).then_inc(pL, 16)
                g.dma_start(out=sbq[:], in_=d["l0_bq"]).then_inc(pL, 16)
                g.dma_start(out=sbk[:], in_=d["l0_bk"]).then_inc(pL, 16)
                g.dma_start(out=sbv[:], in_=d["l0_bv128"]).then_inc(pL, 16)

            @blk.sync
            def _(s):
                fw(s)
                s.dma_start(out=sbpr[:], in_=d["l0_bp"]).then_inc(pLs, 16)
                s.dma_start(out=smaskA[:], in_=d["l0_maskA"]).then_inc(pLs, 16)
                s.dma_start(out=smaskB[:], in_=d["l0_maskB"]).then_inc(pLs, 16)

            @blk.vector
            def _(v):
                fw(v)
                v.memset(sones16[:], 1.0)
                v.memset(sones512[:], 1.0)
                v.memset(seps[:], 1e-5)
                v.memset(sonec[:], 1.0)
                v.memset(sonesf[:], 1.0).then_inc(pV, 1)
                for g_i in range(16):
                    v.wait_ge(pTwc, g_i + 1)
                    kk, hh = divmod(g_i, 2)
                    src = ppsA if g_i % 2 == 0 else ppsB
                    v.tensor_copy(swcomb[:, kk, hh * 512:(hh + 1) * 512], src[:]).then_inc(pVwc, 1)
                for hh in range(2):
                    v.wait_ge(pTbp, hh + 1)
                    src = ppsA if hh == 0 else ppsB
                    v.tensor_copy(sbpb[:, hh * 512:(hh + 1) * 512], src[:]).then_inc(pVbp, 1)

            @blk.tensor
            def _(t):
                t.wait_ge(pL, 16 * 7)
                t.wait_ge(pV, 1)
                for g_i in range(16):
                    kk, hh = divmod(g_i, 2)
                    dst = ppsA if g_i % 2 == 0 else ppsB
                    if g_i >= 2:
                        t.wait_ge(pVwc, g_i - 1)
                    for m in range(8):
                        ins = t.matmul(dst[:], swvT[:, m, kk * 128:(kk + 1) * 128],
                                       swp[:, m, hh * 512:(hh + 1) * 512],
                                       start=(m == 0), stop=(m == 7))
                    ins.then_inc(pTwc, 1)
                for hh in range(2):
                    if hh >= 1:
                        t.wait_ge(pAbc, hh)
                    for m in range(8):
                        ins = t.matmul(ppsR[:], sbv[:, m:m + 1],
                                       swp[:, m, hh * 512:(hh + 1) * 512],
                                       start=(m == 0), stop=(m == 7))
                    ins.then_inc(pTbc, 1)
                t.wait_ge(pLs, 48)
                t.wait_ge(pVwc, 16)
                for hh in range(2):
                    dst = ppsA if hh == 0 else ppsB
                    t.matmul(dst[:], sonesf[:], sbpr[:, hh * 512:(hh + 1) * 512],
                             start=True, stop=True).then_inc(pTbp, 1)

            @blk.scalar
            def _(s):
                fw(s)
                for hh in range(2):
                    s.wait_ge(pTbc, hh + 1)
                    s.copy(sbcomb[:, hh * 512:(hh + 1) * 512], ppsR[:]).then_inc(pAbc, 1)

        with ExitStack() as sa:
            asb = lambda n, s, dt=F32: sa.enter_context(nc.sbuf_tensor(n, s, dt))
            aps = lambda n, s, dt=F32: sa.enter_context(nc.psum_tensor(n, s, dt))
            sx = [asb(f"l0a_sx{i}", [128, 4, C]) for i in range(2)]
            scos = [asb(f"l0a_cos{i}", [128, 4, 512], BF16) for i in range(2)]
            ssin = [asb(f"l0a_sin{i}", [128, 4, 512], BF16) for i in range(2)]
            slnx = [asb(f"l0a_slnx{i}", [128, 4, C], BF16) for i in range(2)]
            slnf = [asb(f"l0a_slnf{i}", [128, 8, 512], BF16) for i in range(2)]
            sqst = [asb(f"l0a_sqst{i}", [128, 8, 512], BF16) for i in range(2)]
            skst = [asb(f"l0a_skst{i}", [128, 8, 512], BF16) for i in range(2)]
            svst = [asb(f"l0a_svst{i}", [128, 4, C], BF16) for i in range(2)]
            skf32 = [asb(f"l0a_skf32{i}", [128, 2, 512]) for i in range(2)]
            sqact = asb("l0a_sqact", [128, C])
            sqdve = asb("l0a_sqdve", [128, C])
            spool = asb("l0a_spool", [128, 512])
            sstat = asb("l0a_sstat", [128, 28])
            sstd = asb("l0a_sstd", [128, 4])
            pq = [aps(f"l0a_pq{i}", [128, 512]) for i in range(4)]
            pk = [aps(f"l0a_pk{i}", [128, 512]) for i in range(2)]
            pv = [aps(f"l0a_pv{i}", [128, 512]) for i in range(2)]

            ssum = sstat[:, 0:4]; ssq = sstat[:, 4:8]; smc = sstat[:, 8:12]
            se2 = sstat[:, 12:16]; svar = sstat[:, 16:20]; srstd = sstat[:, 20:24]
            smsq = sstat[:, 24:28]

            blk = sa.enter_context(nc.Block("l0a_stageA", no_gpsimd_drain=True))

            @blk.sync
            def _(s):
                for it in range(NT):
                    t0 = it * 512
                    if it >= 2:
                        s.wait_ge(aVln, it - 1)
                        s.wait_ge(aAsq, it - 1)
                    s.dma_start(out=sx[it % 2][:],
                                in_=x_in[t0:t0 + 512, :].rearrange("(s p) c -> p s c", p=128)
                                ).then_inc(aLx, 16)
                    s.wait_ge(aVln, it + 1)
                    for kk in range(8):
                        s.dma_start(out=lnx16[kk * T + t0: kk * T + t0 + 512, :].rearrange("(s2 p) c -> p s2 c", p=128),
                                    in_=slnx[it % 2][:, :, kk * 128:(kk + 1) * 128]
                                    ).then_inc(aLst, 16)

            @blk.scalar
            def _(s):
                for it in range(NT):
                    t0 = it * 512
                    s.wait_ge(aLx, 16 * (it + 1))
                    if it >= 1:
                        s.wait_ge(aVst1, it)
                    for ss in range(4):
                        ins = s.activation(sqact[:], sx[it % 2][:, ss, :], AF.Square,
                                           accum_out=ssq[:, ss:ss + 1])
                    ins.then_inc(aAsq, 1)
                    s.wait_ge(aVst1, it + 1)
                    s.activation(sstd[:], svar[:], AF.Sqrt, bias=seps[:], scale=-1.0).then_inc(aAstd, 1)
                    s.wait_ge(aLst, 128 * (it + 1))
                    if it >= 2:
                        s.wait_ge(aTv, 4 * (it - 1))
                    for kk in range(8):
                        s.dma_start_transpose(slnf[it % 2][:, kk, :],
                                              lnx16[kk * T + t0: kk * T + t0 + 512, :]
                                              ).then_inc(aLtr, 16)
                    for p_i in range(4):
                        g = 4 * it + p_i
                        s.wait_ge(aTk, g + 1)
                        if g >= 2:
                            s.wait_ge(aPk, g - 1)
                        s.copy(skf32[g % 2][:, 0, :], pk[0][:])
                        s.copy(skf32[g % 2][:, 1, :], pk[1][:]).then_inc(aAk, 1)

            @blk.vector
            def _(v):
                for it in range(NT):
                    v.wait_ge(aLx, 16 * (it + 1))
                    if it >= 2:
                        v.wait_ge(aLst, 128 * (it - 1))
                    for ss in range(4):
                        v.reduce_sum(ssum[:, ss:ss + 1], sx[it % 2][:, ss, :], axis=AX)
                    v.wait_ge(aAsq, it + 1)
                    v.tensor_scalar_mul(smc[:], ssum[:], 1.0 / C)
                    v.tensor_scalar_mul(se2[:], ssq[:], 1.0 / C)
                    _tnop(v)
                    for ss in range(3):
                        v.scalar_tensor_tensor(svar[:, ss:ss + 1], smc[:, ss:ss + 1],
                                               smc[:, ss:ss + 1], se2[:, ss:ss + 1],
                                               ALU.mult, ALU.subtract)
                    ins = v.scalar_tensor_tensor(svar[:, 3:4], smc[:, 3:4],
                                                 smc[:, 3:4], se2[:, 3:4],
                                                 ALU.mult, ALU.subtract)
                    ins.then_inc(aVst1, 1)
                    v.wait_ge(aAstd, it + 1)
                    v.reciprocal(srstd[:], sstd[:])
                    _tnop(v)
                    for ss in range(4):
                        ins = v.tensor_scalar(slnx[it % 2][:, ss, :], sx[it % 2][:, ss, :],
                                              smc[:, ss:ss + 1], srstd[:, ss:ss + 1],
                                              ALU.subtract, ALU.mult)
                    ins.then_inc(aVln, 1)
                    for p_i in range(4):
                        g = 4 * it + p_i
                        v.wait_ge(aTq, g + 1)
                        v.wait_ge(aLt, 32 * (it + 1))
                        if it >= 2:
                            v.wait_ge(aSt, 48 * (it - 1))
                        cosp = scos[it % 2][:, p_i, :]
                        sinp = ssin[it % 2][:, p_i, :]
                        qs = sqst[it % 2]
                        z0 = pq[2 * (g % 2)]
                        z1 = pq[2 * (g % 2) + 1]
                        v.tensor_mul(sqdve[:, 0:512], cosp, z0[:])
                        v.tensor_mul(sqdve[:, 512:1024], sinp, z1[:])
                        v.tensor_add(qs[:, p_i, :], sqdve[:, 0:512], sqdve[:, 512:1024])
                        v.tensor_mul(sqdve[:, 0:512], cosp, z1[:])
                        v.tensor_mul(sqdve[:, 512:1024], sinp, z0[:])
                        v.tensor_sub(qs[:, p_i + 4, :], sqdve[:, 0:512], sqdve[:, 512:1024]).then_inc(aVq, 1)
                    for p_i in range(4):
                        gv = 4 * it + p_i
                        if it >= 2:
                            v.wait_ge(aSt, 48 * (it - 1))
                        v.wait_ge(aTv, gv + 1)
                        v.tensor_copy(svst[it % 2][:, p_i, 0:512], pv[0][:])
                        v.tensor_copy(svst[it % 2][:, p_i, 512:1024], pv[1][:]).then_inc(aVv, 1)

            @blk.gpsimd
            def _(g):
                for it in range(NT):
                    t0 = it * 512
                    if it >= 2:
                        g.wait_ge(aVq, 4 * (it - 1))
                        g.wait_ge(aPk, 4 * (it - 1))
                    g.dma_start(out=scos[it % 2][:],
                                in_=d["cosfm"][:, t0:t0 + 512].rearrange("(j p) t -> p j t", p=128)
                                ).then_inc(aLt, 16)
                    g.dma_start(out=ssin[it % 2][:],
                                in_=d["sinfm"][:, t0:t0 + 512].rearrange("(j p) t -> p j t", p=128)
                                ).then_inc(aLt, 16)
                    for p_i in range(4):
                        g_i = 4 * it + p_i
                        g.wait_ge(aAk, g_i + 1)
                        if it >= 2:
                            g.wait_ge(aSt, 48 * (it - 1))
                        cosp = scos[it % 2][:, p_i, :]
                        sinp = ssin[it % 2][:, p_i, :]
                        ks = skst[it % 2]
                        z0 = skf32[g_i % 2][:, 0, :]
                        z1 = skf32[g_i % 2][:, 1, :]
                        g.tensor_mul(ks[:, p_i, :], cosp, z0)
                        g.tensor_mul(spool[:], sinp, z1)
                        g.tensor_add(ks[:, p_i, :], ks[:, p_i, :], spool[:])
                        g.tensor_mul(ks[:, p_i + 4, :], cosp, z1)
                        g.tensor_mul(spool[:], sinp, z0)
                        g.tensor_sub(ks[:, p_i + 4, :], ks[:, p_i + 4, :], spool[:]).then_inc(aPk, 1)
                    g.wait_ge(aVq, 4 * (it + 1))
                    g.dma_start(out=qfm.rearrange("(k p) t -> p k t", p=128)[:, :, t0:t0 + 512],
                                in_=sqst[it % 2][:]).then_inc(aSt, 16)
                    g.wait_ge(aPk, 4 * (it + 1))
                    g.dma_start(out=kfm.rearrange("(k p) t -> p k t", p=128)[:, :, t0:t0 + 512],
                                in_=skst[it % 2][:]).then_inc(aSt, 16)
                    g.wait_ge(aVv, 4 * (it + 1))
                    g.dma_start(out=vtm[t0:t0 + 512, :].rearrange("(s p) c -> p s c", p=128),
                                in_=svst[it % 2][:]).then_inc(aSt, 16)

            @blk.tensor
            def _(t):
                for it in range(NT):
                    t.wait_ge(aLtr, 128 * (it + 1))
                    lnf = slnf[it % 2]
                    for p_i in range(4):
                        g_i = 4 * it + p_i
                        if g_i >= 2:
                            t.wait_ge(aVq, g_i - 1)
                        for half, co in ((0, p_i), (1, p_i + 4)):
                            dst = pq[2 * (g_i % 2) + half]
                            t.matmul(dst[:], sbq[:, co * 128:(co + 1) * 128], sones512[:],
                                     start=True, stop=False)
                            for kk in range(8):
                                ins = t.matmul(dst[:], swq[:, kk, co * 128:(co + 1) * 128],
                                               lnf[:, kk, :], start=False, stop=(kk == 7))
                        ins.then_inc(aTq, 1)
                    for p_i in range(4):
                        g_i = 4 * it + p_i
                        if g_i >= 1:
                            t.wait_ge(aAk, g_i)
                        for half, co in ((0, p_i), (1, p_i + 4)):
                            dst = pk[half]
                            t.matmul(dst[:], sbk[:, co * 128:(co + 1) * 128], sones512[:],
                                     start=True, stop=False)
                            for kk in range(8):
                                ins = t.matmul(dst[:], swk[:, kk, co * 128:(co + 1) * 128],
                                               lnf[:, kk, :], start=False, stop=(kk == 7))
                        ins.then_inc(aTk, 1)
                    for sub in range(4):
                        gv = 4 * it + sub
                        if gv >= 1:
                            t.wait_ge(aVv, gv)
                        for half in range(2):
                            dst = pv[half]
                            t.matmul(dst[:], sones16[:], sbcomb[:, half * 512:(half + 1) * 512],
                                     start=True, stop=False)
                            for kk in range(8):
                                ins = t.matmul(dst[:], lnf[:, kk, sub * 128:(sub + 1) * 128],
                                               swcomb[:, kk, half * 512:(half + 1) * 512],
                                               start=False, stop=(kk == 7))
                        ins.then_inc(aTv, 1)
        saw.close()

        with ExitStack() as sbk_:
            asb = lambda n, s, dt=F32: sbk_.enter_context(nc.sbuf_tensor(n, s, dt))
            aps = lambda n, s, dt=F32: sbk_.enter_context(nc.psum_tensor(n, s, dt))
            q_sb = asb("l0a_qsb", [128, 8, T], BF16)
            v_sb = asb("l0a_vsb", [128, T // 128, C], BF16)
            sk = [asb(f"l0a_sk{i}", [128, 8, 256], BF16) for i in range(2)]
            sew = [asb(f"l0a_sew{i}", [128, 256]) for i in range(2)]
            swei = [asb(f"l0a_swei{i}", [128, 256], BF16) for i in range(2)]
            snl = [asb(f"l0a_snl{i}", [128, 256]) for i in range(2)]
            sx0 = [asb(f"l0a_sx0_{i}", [128, 2, C]) for i in range(2)]
            sx1 = [asb(f"l0a_sx1_{i}", [128, 2, C]) for i in range(2)]
            pw = [aps(f"l0a_pw{i}", [128, 256]) for i in range(2)]
            po = [aps(f"l0a_po{i}", [128, 512]) for i in range(4)]

            blk = sbk_.enter_context(nc.Block("l0a_stageB", no_gpsimd_drain=True))

            @blk.sync
            def _(s):
                s.wait_ge(aSt, 48 * NT)
                s.dma_start(out=q_sb[:], in_=qfm.rearrange("(k p) t -> p k t", p=128)).then_inc(bLqv, 16)
                s.dma_start(out=v_sb[:], in_=vtm.rearrange("(j p) c -> p j c", p=128)).then_inc(bLqv, 16)
                for it in range(NI):
                    i0 = it * 256
                    if it >= 2:
                        s.wait_ge(bTw, cum0[it - 1])
                    s.dma_start(out=sk[it % 2][:],
                                in_=kfm.rearrange("(k p) t -> p k t", p=128)[:, :, i0:i0 + 256]
                                ).then_inc(bLk, 16)

            @blk.gpsimd
            def _(g):
                for it in range(NI):
                    i0 = it * 256
                    if it >= 2:
                        g.wait_ge(bVep, it - 1)
                    g.dma_start(out=sx0[it % 2][:],
                                in_=x_in[i0:i0 + 256, :].rearrange("(s p) c -> p s c", p=128)
                                ).then_inc(bLx0, 16)
                    g.wait_ge(bVep, it + 1)
                    g.dma_start(out=x_out[i0:i0 + 256, :].rearrange("(s p) c -> p s c", p=128),
                                in_=sx1[it % 2][:]).then_inc(bSx1, 16)
                g.wait_ge(bSx1, 16 * NI)

            @blk.tensor
            def _(t):
                t.wait_ge(bLqv, 32)
                for it in range(NI):
                    njt = jts[it]
                    t.wait_ge(bLk, 16 * (it + 1))
                    for jt in range(njt + 1):
                        if jt < njt:
                            cj = cum0[it] + jt
                            if cj >= 2:
                                t.wait_ge(bAsp, cj - 1)
                            for kk in range(8):
                                ins = t.matmul(pw[cj % 2][:], q_sb[:, kk, jt * 128:(jt + 1) * 128],
                                               sk[it % 2][:, kk, :], start=(kk == 0), stop=(kk == 7))
                            ins.then_inc(bTw, 1)
                        if jt >= 1:
                            jo = jt - 1
                            cj = cum0[it] + jo
                            t.wait_ge(bAsp, cj + 1)
                            if jo in (2 * it, 2 * it + 1):
                                t.wait_ge(bVmk, dg0[it] + (jo - 2 * it) + 1)
                            if jo == 0 and it >= 1:
                                t.wait_ge(bVep, it)
                            for sub in range(2):
                                for half in range(2):
                                    ins = t.matmul(po[2 * sub + half][:],
                                                   swei[cj % 2][:, sub * 128:(sub + 1) * 128],
                                                   v_sb[:, jo, half * 512:(half + 1) * 512],
                                                   start=(jo == 0), stop=(jo == njt - 1))
                            ins.then_inc(bToj, 1)

            @blk.scalar
            def _(s):
                for it in range(NI):
                    for jt in range(jts[it]):
                        cj = cum0[it] + jt
                        s.wait_ge(bTw, cj + 1)
                        if cj >= 2:
                            s.wait_ge(bToj, cj - 1)
                        s.activation(sew[cj % 2][:], pw[cj % 2][:], AF.Exp, scale=1.0 / 32.0)
                        if jt in (2 * it, 2 * it + 1):
                            dcnt = dg0[it] + (jt - 2 * it)
                            if dcnt >= 2:
                                s.wait_ge(bVmk, dcnt - 1)
                            s.activation(snl[dcnt % 2][:], sew[cj % 2][:], AF.Ln, bias=sonec[:]).then_inc(bAsp, 1)
                        else:
                            s.activation(swei[cj % 2][:], sew[cj % 2][:], AF.Ln, bias=sonec[:]).then_inc(bAsp, 1)

            @blk.vector
            def _(v):
                for it in range(NI):
                    for dd in range(2):
                        jt = 2 * it + dd
                        cj = cum0[it] + jt
                        dcnt = dg0[it] + dd
                        v.wait_ge(bAsp, cj + 1)
                        v.tensor_mul(swei[cj % 2][:], snl[dcnt % 2][:],
                                     smaskA[:] if dd == 0 else smaskB[:]).then_inc(bVmk, 1)
                    v.wait_ge(bToj, cum0[it] + jts[it])
                    v.wait_ge(bLx0, 16 * (it + 1))
                    if it >= 2:
                        v.wait_ge(bSx1, 16 * (it - 1))
                    for sub in range(2):
                        for half in range(2):
                            dst = sx1[it % 2][:, sub, half * 512:(half + 1) * 512]
                            v.tensor_add(dst, po[2 * sub + half][:], sbpb[:, half * 512:(half + 1) * 512])
                            ins = v.tensor_add(dst, dst, sx0[it % 2][:, sub, half * 512:(half + 1) * 512])
                    ins.then_inc(bVep, 1)
    return [(bSx1, 16 * NI)]


def phase_l0_ffn(env, x_in, x_out, fence):
    nc = env.nc
    d = env.d
    E = 8
    NT = T // 512

    f2cm = nc.dram_tensor("l0f_f2cm", [8 * T, 128], BF16, kind="Internal").ap()
    if DEV_DUMP:
        dbg_sc = nc.dram_tensor("dbg_sc", [128, T // 128, 8], F32, kind="ExternalOutput").ap()
        dbg_rt = nc.dram_tensor("dbg_rt", [128, T // 128, 8], F32, kind="ExternalOutput").ap()
        dbg_tr = nc.dram_tensor("dbg_tr", [128, T // 128, 8], F32, kind="ExternalOutput").ap()
        dbg_f2 = nc.dram_tensor("dbg_f2", [8 * T, 128], BF16, kind="ExternalOutput").ap()
        dbg_xa = nc.dram_tensor("dbg_xa", [T, C], F32, kind="ExternalOutput").ap()
        dbg_st = nc.dram_tensor("dbg_st", [128, T // 512, 32], F32, kind="ExternalOutput").ap()

    S = env.sem
    c0Lx = S("l0f_c0Lx"); c0A1 = S("l0f_c0A1"); c0V1 = S("l0f_c0V1")
    c0A1s = S("l0f_c0A1s"); c0Vf = S("l0f_c0Vf"); c0A2 = S("l0f_c0A2")
    c0V2 = S("l0f_c0V2"); c0A2s = S("l0f_c0A2s"); c0Vln = S("l0f_c0Vln")
    c0Sf2 = S("l0f_c0Sf2"); c0Sxa = S("l0f_c0Sxa")
    c0Ttr = S("l0f_c0Ttr"); c0Vtr = S("l0f_c0Vtr"); c0Tsc = S("l0f_c0Tsc")
    c0Vam = S("l0f_c0Vam"); c0Aal = S("l0f_c0Aal"); c0Vrt = S("l0f_c0Vrt")
    c0Lc = S("l0f_c0Lc")
    fLw = S("l0f_fLw"); fLw2 = S("l0f_fLw2"); fVw = S("l0f_fVw"); fTbc = S("l0f_fTbc")
    fAb = S("l0f_fAb"); fTtr2 = S("l0f_fTtr2")
    fVb = S("l0f_fVb"); fTh = S("l0f_fTh"); fAh = S("l0f_fAh")
    fTy = S("l0f_fTy"); fVy = S("l0f_fVy")
    fLf2 = S("l0f_fLf2"); fLxa = S("l0f_fLxa"); fSxa = S("l0f_fSxa")
    fLb = S("l0f_fLb"); fLbg = S("l0f_fLbg")

    def fw(eng):
        for sem_, val_ in fence:
            eng.wait_ge(sem_, val_)

    with ExitStack() as ph:
        sb = lambda n, s, dt=F32: ph.enter_context(nc.sbuf_tensor(n, s, dt))
        sroute = sb("l0f_route", [128, T // 128, E])
        seps2 = sb("l0f_eps", [128, 1])

        with ExitStack() as f0:
            asb = lambda n, s, dt=F32: f0.enter_context(nc.sbuf_tensor(n, s, dt))
            aps = lambda n, s, dt=F32: f0.enter_context(nc.psum_tensor(n, s, dt))
            sx1f = [asb(f"l0f_sx1f{i}", [128, 4, C]) for i in range(2)]
            sfeat = [asb(f"l0f_sfeat{i}", [128, 4, C]) for i in range(2)]
            sf2 = [asb(f"l0f_sf2_{i}", [128, 4, C], BF16) for i in range(2)]
            sxai = [asb(f"l0f_sxai{i}", [128, 4, C]) for i in range(2)]
            strx = [asb(f"l0f_strx{i}", [128, 128]) for i in range(2)]
            scent = asb("l0f_cent", [128, 8, E])
            sidf = asb("l0f_idf", [128, 128])
            ssc = asb("l0f_ssc", [128, E])
            samax = asb("l0f_amax", [128, 1])
            salpha = asb("l0f_alpha", [128, 1])
            swm = asb("l0f_wm", [128, E])
            sqact = asb("l0f_sqact", [128, C])
            sqd = asb("l0f_sqd", [128, C])
            sstat = asb("l0f_sstat", [128, 28])
            sstd = asb("l0f_sstd", [128, 4])
            if DEV_DUMP:
                sdbg_sc = asb("l0f_dbgsc", [128, T // 128, 8])
                sdbg_tr = asb("l0f_dbgtr", [128, T // 128, 8])
                sdbg_st = asb("l0f_dbgst", [128, T // 512, 32])
            ptr = [aps(f"l0f_ptr{i}", [128, 128]) for i in range(2)]
            psc = aps("l0f_psc", [128, E])

            ssum = sstat[:, 0:4]; ssq = sstat[:, 4:8]; smc = sstat[:, 8:12]
            se2 = sstat[:, 12:16]; svar = sstat[:, 16:20]; srstd = sstat[:, 20:24]
            smsq = sstat[:, 24:28]

            blk = f0.enter_context(nc.Block("l0f_F0", no_gpsimd_drain=True))

            @blk.sync
            def _(s):
                fw(s)
                s.dma_start(out=scent[:], in_=d["l0f_centT"].rearrange("(k p) e -> p k e", p=128),
                            allow_slow_non_contiguous=True).then_inc(c0Lc, 16)
                s.dma_start(out=sidf[:], in_=d["idf32"]).then_inc(c0Lc, 16)
                for it in range(NT):
                    t0 = it * 512
                    if it >= 2:
                        s.wait_ge(c0Vln, it - 1)
                        s.wait_ge(c0A1, it - 1)
                    s.dma_start(out=sx1f[it % 2][:],
                                in_=x_in[t0:t0 + 512, :].rearrange("(s p) c -> p s c", p=128)
                                ).then_inc(c0Lx, 16)
                    s.wait_ge(c0Vln, it + 1)
                    for kk in range(8):
                        s.dma_start(out=f2cm[kk * T + t0: kk * T + t0 + 512, :].rearrange("(s2 p) c -> p s2 c", p=128),
                                    in_=sf2[it % 2][:, :, kk * 128:(kk + 1) * 128]
                                    ).then_inc(c0Sf2, 16)

            @blk.gpsimd
            def _(g):
                fw(g)
                for it in range(NT):
                    t0 = it * 512
                    g.wait_ge(c0Vln, it + 1)
                    g.dma_start(out=x_out[t0:t0 + 512, :].rearrange("(s p) c -> p s c", p=128),
                                in_=sxai[it % 2][:]).then_inc(c0Sxa, 16)
                if DEV_DUMP:
                    g.wait_ge(c0Vrt, 4 * NT)
                    g.wait_ge(c0Sf2, 128 * NT)
                    g.wait_ge(c0Sxa, 16 * NT)
                    g.dma_start(out=dbg_sc, in_=sdbg_sc[:]).then_inc(c0Sxa, 16)
                    g.dma_start(out=dbg_rt, in_=sroute[:]).then_inc(c0Sxa, 16)
                    g.dma_start(out=dbg_tr, in_=sdbg_tr[:]).then_inc(c0Sxa, 16)
                    g.dma_start(out=dbg_f2, in_=f2cm).then_inc(c0Sxa, 16)
                    g.dma_start(out=dbg_xa, in_=x_out).then_inc(c0Sxa, 16)
                    g.dma_start(out=dbg_st, in_=sdbg_st[:]).then_inc(c0Sxa, 16)
                    g.wait_ge(c0Sxa, 16 * NT + 96)

            @blk.scalar
            def _(s):
                fw(s)
                for it in range(NT):
                    s.wait_ge(c0Lx, 16 * (it + 1))
                    if it >= 1:
                        s.wait_ge(c0V2, it)
                    for ss in range(4):
                        ins = s.activation(sqact[:], sx1f[it % 2][:, ss, :], AF.Square,
                                           accum_out=ssq[:, ss:ss + 1])
                    ins.then_inc(c0A1, 1)
                    s.wait_ge(c0V1, it + 1)
                    s.activation(sstd[:], svar[:], AF.Sqrt, bias=seps2[:], scale=-1.0).then_inc(c0A1s, 1)
                    s.wait_ge(c0V2, it + 1)
                    s.activation(sstd[:], svar[:], AF.Sqrt, bias=seps2[:], scale=-1.0).then_inc(c0A2s, 1)
                    for su in range(4):
                        s.wait_ge(c0Vam, 4 * it + su + 1)
                        s.activation(salpha[:], samax[:], AF.Sigmoid).then_inc(c0Aal, 1)

            @blk.vector
            def _(v):
                fw(v)
                v.memset(seps2[:], 1e-5)
                for it in range(NT):
                    xb = sx1f[it % 2]
                    fb = sfeat[it % 2]
                    v.wait_ge(c0Lx, 16 * (it + 1))
                    if it >= 2:
                        v.wait_ge(c0Tsc, 4 * (it - 1))
                        v.wait_ge(c0Sf2, 128 * (it - 1))
                        v.wait_ge(c0Sxa, 16 * (it - 1))
                    for ss in range(4):
                        v.reduce_sum(ssum[:, ss:ss + 1], xb[:, ss, :], axis=AX)
                    v.wait_ge(c0A1, it + 1)
                    v.tensor_scalar_mul(smc[:], ssum[:], 1.0 / C)
                    v.tensor_scalar_mul(se2[:], ssq[:], 1.0 / C)
                    _tnop(v)
                    for ss in range(3):
                        v.scalar_tensor_tensor(svar[:, ss:ss + 1], smc[:, ss:ss + 1],
                                               smc[:, ss:ss + 1], se2[:, ss:ss + 1],
                                               ALU.mult, ALU.subtract)
                    ins = v.scalar_tensor_tensor(svar[:, 3:4], smc[:, 3:4],
                                                 smc[:, 3:4], se2[:, 3:4],
                                                 ALU.mult, ALU.subtract)
                    ins.then_inc(c0V1, 1)
                    v.wait_ge(c0A1s, it + 1)
                    v.reciprocal(srstd[:], sstd[:])
                    _tnop(v)
                    for ss in range(4):
                        ins = v.tensor_scalar(fb[:, ss, :], xb[:, ss, :],
                                              smc[:, ss:ss + 1], srstd[:, ss:ss + 1],
                                              ALU.subtract, ALU.mult)
                    ins.then_inc(c0Vf, 1)
                    for ss in range(4):
                        v.tensor_add(sxai[it % 2][:, ss, :], xb[:, ss, :], fb[:, ss, :])
                    for ss in range(4):
                        v.reduce_sum(ssum[:, ss:ss + 1], fb[:, ss, :], axis=AX)
                        v.tensor_mul(sqd[:], fb[:, ss, :], fb[:, ss, :])
                        v.reduce_sum(ssq[:, ss:ss + 1], sqd[:], axis=AX)
                    v.tensor_scalar_mul(smc[:], ssum[:], 1.0 / C)
                    v.tensor_scalar_mul(se2[:], ssq[:], 1.0 / C)
                    _tnop(v)
                    for ss in range(3):
                        v.scalar_tensor_tensor(svar[:, ss:ss + 1], smc[:, ss:ss + 1],
                                               smc[:, ss:ss + 1], se2[:, ss:ss + 1],
                                               ALU.mult, ALU.subtract)
                    ins = v.scalar_tensor_tensor(svar[:, 3:4], smc[:, 3:4],
                                                 smc[:, 3:4], se2[:, 3:4],
                                                 ALU.mult, ALU.subtract)
                    ins.then_inc(c0V2, 1)
                    v.wait_ge(c0A2s, it + 1)
                    v.reciprocal(srstd[:], sstd[:])
                    _tnop(v)
                    if DEV_DUMP:
                        v.tensor_copy(sdbg_st[:, it, 0:28], sstat[:])
                        v.tensor_copy(sdbg_st[:, it, 28:32], sstd[:])
                    for ss in range(4):
                        ins = v.tensor_scalar(sf2[it % 2][:, ss, :], fb[:, ss, :],
                                              smc[:, ss:ss + 1], srstd[:, ss:ss + 1],
                                              ALU.subtract, ALU.mult)
                    ins.then_inc(c0Vln, 1)
                    for su in range(4):
                        gidx = 4 * it + su
                        for kk in range(8):
                            v.wait_ge(c0Ttr, 8 * gidx + kk + 1)
                            v.tensor_copy(strx[kk % 2][:], ptr[kk % 2][:]).then_inc(c0Vtr, 1)
                        v.wait_ge(c0Tsc, gidx + 1)
                        v.tensor_copy(ssc[:], psc[:])
                        _tnop(v)
                        if DEV_DUMP:
                            v.tensor_copy(sdbg_sc[:, gidx, :], ssc[:])
                            v.tensor_copy(sdbg_tr[:, gidx, :], strx[1][:, 0:8])
                        v.reduce_max(samax[:], ssc[:], axis=AX).then_inc(c0Vam, 1)
                        v.wait_ge(c0Aal, gidx + 1)
                        v.tensor_scalar(swm[:], ssc[:], samax[:], None, ALU.is_equal)
                        _tnop(v)
                        v.tensor_scalar(sroute[:, gidx, :], swm[:], salpha[:], None,
                                        ALU.mult).then_inc(c0Vrt, 1)

            @blk.tensor
            def _(t):
                fw(t)
                t.wait_ge(c0Lc, 32)
                for it in range(NT):
                    t.wait_ge(c0Vf, it + 1)
                    for su in range(4):
                        gidx = 4 * it + su
                        if gidx >= 1:
                            t.wait_ge(c0Vrt, gidx)
                        for kk in range(8):
                            if 8 * gidx + kk >= 2:
                                t.wait_ge(c0Vtr, 8 * gidx + kk - 1)
                            t.transpose(ptr[kk % 2][:], sfeat[it % 2][:, su, kk * 128:(kk + 1) * 128],
                                        sidf[:]).then_inc(c0Ttr, 1)
                            t.wait_ge(c0Vtr, 8 * gidx + kk + 1)
                            ins = t.matmul(psc[:], strx[kk % 2][:], scent[:, kk, :],
                                           start=(kk == 0), stop=(kk == 7))
                        ins.then_inc(c0Tsc, 1)

        NIT = T // 256
        with ExitStack() as f1:
            asb = lambda n, s, dt=F32: f1.enter_context(nc.sbuf_tensor(n, s, dt))
            aps = lambda n, s, dt=F32: f1.enter_context(nc.psum_tensor(n, s, dt))
            sW1 = asb("l0f_w1", [128, 8, 4096], BF16)
            sW2 = asb("l0f_w2", [128, 32, C], BF16)
            sh = asb("l0f_h", [128, 32, 256], BF16)
            sf2t = [asb(f"l0f_f2t{i}", [128, 8, 256], BF16) for i in range(2)]
            sxa = [asb(f"l0f_sxa{i}", [128, 2, C]) for i in range(2)]
            sgam = asb("l0f_gam", [128, E])
            sbet = asb("l0f_bet", [128, E], BF16)
            sb1c = asb("l0f_b1c", [128, 32])
            sb2r = asb("l0f_b2r", [1, C], BF16)
            sbias = asb("l0f_bias", [128, 32])
            sbwrow = asb("l0f_bwrow", [1, 4096], BF16)
            sidf1 = asb("l0f_id16", [128, 128], BF16)
            sones16 = asb("l0f_ones16", [1, 128], BF16)
            ph_ = [aps(f"l0f_ph{i}", [128, 256])[:] for i in range(2)]
            po = [aps(f"l0f_po{i}", [128, 512]) for i in range(4)]
            prow_t = aps("l0f_prow", [1, 512])
            prow = [prow_t, prow_t]
            pcol = aps("l0f_pcol", [128, 64], BF16)

            blk = f1.enter_context(nc.Block("l0f_F1", no_gpsimd_drain=True))
            F0FENCE = [(c0Sf2, 128 * NT), (c0Sxa, 16 * NT), (c0Vrt, 4 * NT)]

            def f0w(eng):
                for sem_, val_ in F0FENCE:
                    eng.wait_ge(sem_, val_)

            @blk.vector
            def _(v):
                f0w(v)
                v.memset(sones16[:], 1.0)
                for e in range(E):
                    v.wait_ge(fLb, 32 * (e + 1))
                    v.wait_ge(fTtr2, e + 1)
                    v.tensor_add(sbias[:], pcol[:, 0:64:2], sb1c[:]).then_inc(fVb, 1)
                    for q in range(4):
                        v.wait_ge(fTbc, 8 * e + 2 * (q + 1))
                        v.wait_ge(fLw, 64 * e + 16 * (q + 1))
                        for kk in range(8):
                            ins = v.tensor_scalar_mul(sW1[:, kk, q * 1024:(q + 1) * 1024],
                                                      sW1[:, kk, q * 1024:(q + 1) * 1024],
                                                      sgam[:, kk:kk + 1])
                        ins.then_inc(fVw, 1)
                    for it in range(NIT):
                        n = NIT * e + it
                        v.wait_ge(fTy, 4 * n + 4)
                        v.wait_ge(fLxa, 16 * (n + 1))
                        for sub in range(2):
                            mcol = sroute[:, 2 * it + sub, e:e + 1]
                            for half in range(2):
                                ins = v.scalar_tensor_tensor(
                                    sxa[it % 2][:, sub, half * 512:(half + 1) * 512],
                                    po[2 * sub + half][:], mcol,
                                    sxa[it % 2][:, sub, half * 512:(half + 1) * 512],
                                    ALU.mult, ALU.add)
                        ins.then_inc(fVy, 1)

            @blk.gpsimd
            def _(g):
                f0w(g)
                for e in range(E):
                    if e >= 1:
                        g.wait_ge(fTbc, 8 * e)
                        g.wait_ge(fTy, 4 * (NIT * e - 1) + 1)
                    g.dma_start(out=sbet[:], in_=d["l0f_beta128"][e * 128:(e + 1) * 128, :]).then_inc(fLbg, 16)
                    g.dma_start(out=sb2r[:], in_=d["l0f_b2"][e:e + 1, :]).then_inc(fLbg, 16)
                    if e == 0:
                        g.dma_start(out=sidf1[:], in_=d["idf32"]).then_inc(fLbg, 16)
                    for q in range(4):
                        if e >= 1:
                            g.wait_ge(fTh, 32 * (NIT * e - 1) + 8 * (q + 1))
                        g.dma_start(out=sW1[:, :, q * 1024:(q + 1) * 1024],
                                    in_=d["l0f_w1T"][e * C:(e + 1) * C, :].rearrange("(k p) f -> p k f", p=128)[:, :, q * 1024:(q + 1) * 1024]
                                    ).then_inc(fLw, 16)
                    for q in range(4):
                        if e >= 1:
                            g.wait_ge(fTy, 4 * (NIT * e - 1) + q + 1)
                        g.dma_start(out=sW2[:, 8 * q:8 * (q + 1), :],
                                    in_=d["l0f_w2T"][e * 4096 + 1024 * q: e * 4096 + 1024 * (q + 1), :].rearrange("(k p) dd -> p k dd", p=128)
                                    ).then_inc(fLw2, 16)

            @blk.sync
            def _(s):
                f0w(s)
                NTOT = NIT * E

                def f2load(n):
                    ee, ii = divmod(n, NIT)
                    t0 = ii * 256
                    if n >= 2:
                        s.wait_ge(fTh, 32 * (n - 1))
                    for kk in range(8):
                        s.dma_start_transpose(sf2t[ii % 2][:, kk, :],
                                              f2cm[kk * T + t0: kk * T + t0 + 256, :]
                                              ).then_inc(fLf2, 16)

                for e in range(E):
                    if e >= 1:
                        s.wait_ge(fVw, 4 * e)
                        s.wait_ge(fVb, e)
                    s.dma_start(out=sgam[:], in_=d["l0f_gam128"][e * 128:(e + 1) * 128, :]).then_inc(fLb, 16)
                    s.dma_start(out=sb1c[:], in_=d["l0f_b1c128"][e * 128:(e + 1) * 128, :]).then_inc(fLb, 16)
                    for it in range(NIT):
                        n = NIT * e + it
                        t0 = it * 256
                        if e == 0 and it == 0:
                            f2load(0)
                            f2load(1)
                        if n >= 1:
                            s.wait_ge(fSxa, 16 * n)
                        s.dma_start(out=sxa[it % 2][:],
                                    in_=x_out[t0:t0 + 256, :].rearrange("(s p) c -> p s c", p=128)
                                    ).then_inc(fLxa, 16)
                        if n + 2 < NTOT:
                            f2load(n + 2)
                        s.wait_ge(fVy, n + 1)
                        s.dma_start(out=x_out[t0:t0 + 256, :].rearrange("(s p) c -> p s c", p=128),
                                    in_=sxa[it % 2][:]).then_inc(fSxa, 16)
                s.wait_ge(fSxa, 16 * NIT * E)

            @blk.tensor
            def _(t):
                f0w(t)
                for e in range(E):
                    t.wait_ge(fLb, 32 * (e + 1))
                    t.wait_ge(fLbg, 32 * e + 48)
                    for fg in range(8):
                        q = fg // 2
                        t.wait_ge(fLw, 64 * e + 16 * (q + 1))
                        if 8 * e + fg >= 1:
                            t.wait_ge(fAb, 8 * e + fg)
                        for kk in range(8):
                            ins = t.matmul(prow[fg % 2][:], sbet[:, kk:kk + 1],
                                           sW1[:, kk, fg * 512:(fg + 1) * 512],
                                           start=(kk == 0), stop=(kk == 7))
                        ins.then_inc(fTbc, 1)
                    t.wait_ge(fAb, 8 * (e + 1))
                    if e >= 1:
                        t.wait_ge(fVb, e)
                    for fc in range(32):
                        ins = t.transpose(pcol[:, 2 * fc:2 * fc + 1], sbwrow[0:1, fc * 128:(fc + 1) * 128],
                                          sidf1[0:1, 0:1])
                    ins.then_inc(fTtr2, 1)
                    for it in range(NIT):
                        n = NIT * e + it
                        t.wait_ge(fLf2, 128 * (n + 1))
                        for fc in range(32):
                            q = fc // 8
                            if fc % 8 == 0:
                                t.wait_ge(fVw, 4 * e + q + 1)
                            if fc % 2 == 0 and 32 * n + fc >= 1:
                                t.wait_ge(fAh, 32 * n + fc)
                            for kk in range(8):
                                ins = t.matmul(ph_[fc % 2], sW1[:, kk, fc * 128:(fc + 1) * 128],
                                               sf2t[it % 2][:, kk, :],
                                               start=(kk == 0), stop=(kk == 7))
                            ins.then_inc(fTh, 1)
                        if n >= 1:
                            t.wait_ge(fVy, n)
                        for sub in range(2):
                            for half in range(2):
                                t.matmul(po[2 * sub + half][:], sones16[:],
                                         sb2r[:, half * 512:(half + 1) * 512],
                                         start=True, stop=False)
                        for fc in range(32):
                            if fc % 2 == 0:
                                t.wait_ge(fAh, min(32 * n + fc + 2, 32 * n + 32))
                            if fc % 8 == 0:
                                t.wait_ge(fLw2, 64 * e + 16 * (fc // 8 + 1))
                            for sub in range(2):
                                for half in range(2):
                                    ins = t.matmul(po[2 * sub + half][:],
                                                   sh[:, fc, sub * 128:(sub + 1) * 128],
                                                   sW2[:, fc, half * 512:(half + 1) * 512],
                                                   start=False, stop=(fc == 31))
                            if fc % 8 == 7:
                                ins.then_inc(fTy, 1)

            @blk.scalar
            def _(s):
                f0w(s)
                for e in range(E):
                    for fg in range(8):
                        s.wait_ge(fTbc, 8 * e + fg + 1)
                        s.copy(sbwrow[0:1, fg * 512:(fg + 1) * 512], prow[fg % 2][:]).then_inc(fAb, 1)
                    for it in range(NIT):
                        n = NIT * e + it
                        s.wait_ge(fVb, e + 1)
                        for fc in range(32):
                            s.wait_ge(fTh, 32 * n + fc + 1)
                            s.activation(sh[:, fc, :], ph_[fc % 2], AF.Relu,
                                         bias=sbias[:, fc:fc + 1]).then_inc(fAh, 1)
    return [(fSxa, 16 * (T // 256) * E)]


def phase_l1_attn(env, x_in, x_out, fence):
    nc = env.nc
    d = env.d
    NT = T // 512
    NP = 8

    ln3cm = nc.dram_tensor("l1a_ln3cm", [8 * T, 128], BF16, kind="Internal").ap()
    qfm3 = nc.dram_tensor("l1a_qfm", [C, T], BF16, kind="Internal").ap()
    xb3 = nc.dram_tensor("l1a_xb", [T, C], F32, kind="Internal").ap()

    S = env.sem
    gLx = S("l1a_gLx"); gAsq = S("l1a_gAsq"); gVst = S("l1a_gVst")
    gAstd = S("l1a_gAstd"); gVln = S("l1a_gVln"); gVxb = S("l1a_gVxb")
    gSln = S("l1a_gSln"); gSxb = S("l1a_gSxb")
    hLtr = S("l1a_hLtr"); hLw = S("l1a_hLw"); hLwg = S("l1a_hLwg")
    hTq = S("l1a_hTq"); hAe = S("l1a_hAe"); hTst = S("l1a_hTst")
    hVr = S("l1a_hVr"); hAsd = S("l1a_hAsd"); hTb = S("l1a_hTb")
    hVn = S("l1a_hVn"); hTtr = S("l1a_hTtr"); hVtr = S("l1a_hVtr")
    hVrc = S("l1a_hVrc")
    hSq = S("l1a_hSq"); hTkv = S("l1a_hTkv"); hVkv = S("l1a_hVkv")
    iLq = S("l1a_iLq"); iLqs = S("l1a_iLqs"); iLtr = S("l1a_iLtr"); iLxb = S("l1a_iLxb")
    iTat = S("l1a_iTat"); iVat = S("l1a_iVat"); iTp1 = S("l1a_iTp1")
    iAh = S("l1a_iAh"); iTp2 = S("l1a_iTp2"); iVep = S("l1a_iVep")
    iSx3 = S("l1a_iSx3")

    def fw(eng):
        for sem_, val_ in fence:
            eng.wait_ge(sem_, val_)

    with ExitStack() as ph:
        sb = lambda n, s, dt=F32: ph.enter_context(nc.sbuf_tensor(n, s, dt))
        seps3 = sb("l1a_eps", [128, 1])
        skv16 = sb("l1a_skv16", [128, 8, 128], BF16)

        with ExitStack() as g0:
            asb = lambda n, s, dt=F32: g0.enter_context(nc.sbuf_tensor(n, s, dt))
            sx = [asb(f"l1a_sx{i}", [128, 4, C]) for i in range(2)]
            slnf = [asb(f"l1a_slnf{i}", [128, 4, C]) for i in range(2)]
            sln16 = [asb(f"l1a_sln16{i}", [128, 4, C], BF16) for i in range(2)]
            sxb = [asb(f"l1a_sxb{i}", [128, 4, C]) for i in range(2)]
            sqact = asb("l1a_sqact", [128, C])
            sstat = asb("l1a_sstat", [128, 28])
            sstd = asb("l1a_sstd", [128, 4])
            ssum = sstat[:, 0:4]; ssq = sstat[:, 4:8]; smc = sstat[:, 8:12]
            se2 = sstat[:, 12:16]; svar = sstat[:, 16:20]; srstd = sstat[:, 20:24]

            blk = g0.enter_context(nc.Block("l1a_G0", no_gpsimd_drain=True))

            @blk.sync
            def _(s):
                fw(s)
                for it in range(NT):
                    t0 = it * 512
                    if it >= 2:
                        s.wait_ge(gVln, it - 1)
                        s.wait_ge(gAsq, it - 1)
                    s.dma_start(out=sx[it % 2][:],
                                in_=x_in[t0:t0 + 512, :].rearrange("(s p) c -> p s c", p=128)
                                ).then_inc(gLx, 16)
                    s.wait_ge(gVln, it + 1)
                    for kk in range(8):
                        s.dma_start(out=ln3cm[kk * T + t0: kk * T + t0 + 512, :].rearrange("(s2 p) c -> p s2 c", p=128),
                                    in_=sln16[it % 2][:, :, kk * 128:(kk + 1) * 128]
                                    ).then_inc(gSln, 16)

            @blk.gpsimd
            def _(g):
                fw(g)
                for it in range(NT):
                    t0 = it * 512
                    g.wait_ge(gVxb, it + 1)
                    g.dma_start(out=xb3[t0:t0 + 512, :].rearrange("(s p) c -> p s c", p=128),
                                in_=sxb[it % 2][:]).then_inc(gSxb, 16)

            @blk.scalar
            def _(s):
                fw(s)
                for it in range(NT):
                    s.wait_ge(gLx, 16 * (it + 1))
                    if it >= 1:
                        s.wait_ge(gVst, it)
                    for ss in range(4):
                        ins = s.activation(sqact[:], sx[it % 2][:, ss, :], AF.Square,
                                           accum_out=ssq[:, ss:ss + 1])
                    ins.then_inc(gAsq, 1)
                    s.wait_ge(gVst, it + 1)
                    s.activation(sstd[:], svar[:], AF.Sqrt, bias=seps3[:], scale=-1.0).then_inc(gAstd, 1)

            @blk.vector
            def _(v):
                fw(v)
                v.memset(seps3[:], 1e-5)
                for it in range(NT):
                    xb = sx[it % 2]
                    v.wait_ge(gLx, 16 * (it + 1))
                    if it >= 2:
                        v.wait_ge(gSln, 128 * (it - 1))
                        v.wait_ge(gSxb, 16 * (it - 1))
                    for ss in range(4):
                        v.reduce_sum(ssum[:, ss:ss + 1], xb[:, ss, :], axis=AX)
                    v.wait_ge(gAsq, it + 1)
                    v.tensor_scalar_mul(smc[:], ssum[:], 1.0 / C)
                    v.tensor_scalar_mul(se2[:], ssq[:], 1.0 / C)
                    _tnop(v)
                    for ss in range(4):
                        ins = v.scalar_tensor_tensor(svar[:, ss:ss + 1], smc[:, ss:ss + 1],
                                                     smc[:, ss:ss + 1], se2[:, ss:ss + 1],
                                                     ALU.mult, ALU.subtract)
                    ins.then_inc(gVst, 1)
                    v.wait_ge(gAstd, it + 1)
                    v.reciprocal(srstd[:], sstd[:])
                    _tnop(v)
                    for ss in range(4):
                        v.tensor_scalar(slnf[it % 2][:, ss, :], xb[:, ss, :],
                                        smc[:, ss:ss + 1], srstd[:, ss:ss + 1],
                                        ALU.subtract, ALU.mult)
                    for ss in range(4):
                        ins = v.tensor_copy(sln16[it % 2][:, ss, :], slnf[it % 2][:, ss, :])
                    ins.then_inc(gVln, 1)
                    for ss in range(4):
                        ins = v.tensor_add(sxb[it % 2][:, ss, :], xb[:, ss, :], slnf[it % 2][:, ss, :])
                    ins.then_inc(gVxb, 1)

        with ExitStack() as g1:
            asb = lambda n, s, dt=F32: g1.enter_context(nc.sbuf_tensor(n, s, dt))
            aps = lambda n, s, dt=F32: g1.enter_context(nc.psum_tensor(n, s, dt))
            sktm = asb("l1a_sktm", [128, T // 128, C], BF16)
            svtm = asb("l1a_svtm", [128, T // 128, C], BF16)
            sqkvw = asb("l1a_qkvw", [128, 8, 3072], BF16)
            sqkvb = asb("l1a_sqkvb", [128, 24])
            slnxf = [asb("l1a_lnxf", [128, 8, 512], BF16)] * 2
            skf = [asb("l1a_skf", [128, 512])] * 2
            sksq = asb("l1a_sksq", [128, 512])
            srows = asb("l1a_srows", [2, 4, 512])
            sk16 = [asb(f"l1a_sk16{i}", [128, 512], BF16) for i in range(2)]
            sq16h = asb("l1a_sq16", [128, 4, 512], BF16)
            ssel2a = asb("l1a_ssel2a", [128, 2])
            ssel2b = asb("l1a_ssel2b", [2, 128])
            skgb = asb("l1a_skgb", [128, 4])
            sid16 = asb("l1a_id16", [128, 128], BF16)
            stemp = asb("l1a_stemp", [128, 8])
            satnb = asb("l1a_satnb", [128, 8, 128], BF16)
            seps_r = asb("l1a_epsr", [2, 1])
            pqkv = [aps(f"l1a_pqkv{i}", [128, 512]) for i in range(2)]
            pst = [aps(f"l1a_pst{i}", [2, 512]) for i in range(2)]
            pmb = aps("l1a_pmb", [128, 512])
            prb = aps("l1a_prb", [128, 512])
            ptr = [aps(f"l1a_ptr{i}", [128, 128], BF16) for i in range(2)]

            blk = g1.enter_context(nc.Block("l1a_G1", no_gpsimd_drain=True))
            G0F = [(gSln, 128 * NT), (gSxb, 16 * NT)]

            def g0w(eng):
                for sem_, val_ in G0F:
                    eng.wait_ge(sem_, val_)

            @blk.gpsimd
            def _(g):
                g0w(g)
                g.dma_start(out=sqkvw[:], in_=d["l1a_qkvwT"].rearrange("(k p) r -> p k r", p=128)).then_inc(hLwg, 16)
                g.dma_start(out=sid16[:], in_=d["idf32"]).then_inc(hLwg, 16)
                g.dma_start(out=satnb[:], in_=d["l1a_atnb"].rearrange("p (q e) -> p q e", q=8)).then_inc(hLwg, 16)
                for it in range(NT):
                    t0 = it * 512
                    for hf in range(2):
                        g.wait_ge(hAe, 24 * it + 4 * (hf + 1))
                        g.dma_start(out=qfm3.rearrange("(k p) t -> p k t", p=128)[:, 4 * hf:4 * (hf + 1), t0:t0 + 512],
                                    in_=sq16h[:]).then_inc(hSq, 16)

            @blk.sync
            def _(s):
                g0w(s)
                s.dma_start(out=sqkvb[:], in_=d["l1a_qkvb128"]).then_inc(hLw, 16)
                s.dma_start(out=ssel2a[:], in_=d["l1a_sel2a"]).then_inc(hLw, 16)
                s.dma_start(out=ssel2b[:], in_=d["l1a_sel2b"]).then_inc(hLw, 16)
                s.dma_start(out=skgb[:], in_=d["l1a_kgb"]).then_inc(hLw, 16)
                s.dma_start(out=stemp[:], in_=d["l1a_temp128"]).then_inc(hLw, 16)
                for it in range(NT):
                    t0 = it * 512
                    if it >= 1:
                        s.wait_ge(hTq, 24 * it)
                    for kk in range(8):
                        s.dma_start_transpose(slnxf[it % 2][:, kk, :],
                                              ln3cm[kk * T + t0: kk * T + t0 + 512, :]
                                              ).then_inc(hLtr, 16)

            @blk.tensor
            def _(t):
                g0w(t)
                t.wait_ge(hLwg, 48)
                t.wait_ge(hLw, 80)
                for it in range(NT):
                    t.wait_ge(hLtr, 128 * (it + 1))
                    lnf = slnxf[it % 2]
                    for rc in range(24):
                        cnt = 24 * it + rc
                        if cnt >= 2:
                            t.wait_ge(hAe, cnt - 1)
                        for kk in range(8):
                            ins = t.matmul(pqkv[cnt % 2][:], sqkvw[:, kk, rc * 128:(rc + 1) * 128],
                                           lnf[:, kk, :], start=(kk == 0), stop=(kk == 7))
                        ins.then_inc(hTq, 1)
                        if rc >= 8:
                            pcnt = it * 16 + (rc - 8)
                            t.wait_ge(hAe, cnt + 1)
                            if pcnt >= 1:
                                t.wait_ge(hVr, pcnt)
                            t.matmul(pst[0][:], ssel2a[:], skf[pcnt % 2][:], start=True, stop=True)
                            t.matmul(pst[1][:], ssel2a[:], sksq[:], start=True, stop=True).then_inc(hTst, 1)
                            t.wait_ge(hVr, pcnt + 1)
                            if pcnt >= 1:
                                t.wait_ge(hVn, pcnt)
                            t.matmul(pmb[:], ssel2b[:], srows[:, 0, :], start=True, stop=True)
                            t.wait_ge(hVrc, pcnt + 1)
                            t.matmul(prb[:], ssel2b[:], srows[:, 1, :], start=True, stop=True).then_inc(hTb, 1)
                            t.wait_ge(hVn, pcnt + 1)
                            for sub in range(4):
                                tcnt = 4 * pcnt + sub
                                if tcnt >= 2:
                                    t.wait_ge(hVtr, tcnt - 1)
                                t.transpose(ptr[tcnt % 2][:], sk16[pcnt % 2][:, sub * 128:(sub + 1) * 128],
                                            sid16[:]).then_inc(hTtr, 1)
                t.wait_ge(hAe, 24 * NT)
                t.wait_ge(hVtr, 64 * NT)
                for p_i in range(NP):
                    if p_i >= 2:
                        t.wait_ge(hVkv, p_i - 1)
                    dst = pqkv[p_i % 2]
                    for tc in range(T // 128):
                        ins = t.matmul(dst[0:128, 0:128], sktm[:, tc, p_i * 128:(p_i + 1) * 128],
                                       svtm[:, tc, p_i * 128:(p_i + 1) * 128],
                                       start=(tc == 0), stop=(tc == T // 128 - 1))
                    ins.then_inc(hTkv, 1)

            @blk.scalar
            def _(s):
                g0w(s)
                for it in range(NT):
                    for rc in range(24):
                        cnt = 24 * it + rc
                        s.wait_ge(hTq, cnt + 1)
                        if rc < 8:
                            if rc % 4 == 0 and (it + rc) > 0:
                                s.wait_ge(hSq, 32 * it + (16 if rc == 4 else 0))
                            s.activation(sq16h[:, rc % 4, :], pqkv[cnt % 2][:], AF.Identity,
                                         bias=sqkvb[:, rc:rc + 1]).then_inc(hAe, 1)
                        else:
                            pcnt = it * 16 + (rc - 8)
                            if pcnt >= 1:
                                s.wait_ge(hTst, pcnt)
                                s.wait_ge(hVn, pcnt)
                            s.activation(skf[pcnt % 2][:], pqkv[cnt % 2][:], AF.Identity,
                                         bias=sqkvb[:, rc:rc + 1])
                            s.activation(sksq[:], skf[pcnt % 2][:], AF.Square).then_inc(hAe, 1)
                            s.wait_ge(hVr, pcnt + 1)
                            s.activation(srows[:, 3, :], srows[:, 2, :], AF.Sqrt, bias=seps_r[:],
                                         scale=-1.0).then_inc(hAsd, 1)

            @blk.vector
            def _(v):
                g0w(v)
                v.memset(seps_r[:], 1e-5)
                v.wait_ge(hLwg, 48)
                v.wait_ge(hLw, 80)
                for it in range(NT):
                    for rc in range(8, 24):
                        cnt = 24 * it + rc
                        pcnt = it * 16 + (rc - 8)
                        is_v = rc >= 16
                        gcol = skgb[:, 2:3] if is_v else skgb[:, 0:1]
                        bcol = skgb[:, 3:4] if is_v else skgb[:, 1:2]
                        v.wait_ge(hTst, pcnt + 1)
                        v.tensor_scalar_mul(srows[:, 0, :], pst[0][:], 1.0 / 64.0)
                        v.tensor_scalar_mul(srows[:, 1, :], pst[1][:], 1.0 / 64.0)
                        v.tensor_mul(srows[:, 2, :], srows[:, 0, :], srows[:, 0, :])
                        ins = v.tensor_sub(srows[:, 2, :], srows[:, 2, :], srows[:, 1, :])
                        ins.then_inc(hVr, 1)
                        v.wait_ge(hAsd, pcnt + 1)
                        v.reciprocal(srows[:, 1, :], srows[:, 3, :]).then_inc(hVrc, 1)
                        v.wait_ge(hTb, pcnt + 1)
                        if pcnt >= 2:
                            v.wait_ge(hTtr, 4 * (pcnt - 1))
                        v.tensor_sub(skf[pcnt % 2][:], skf[pcnt % 2][:], pmb[:])
                        v.tensor_mul(skf[pcnt % 2][:], skf[pcnt % 2][:], prb[:])
                        ins = v.tensor_scalar(sk16[pcnt % 2][:], skf[pcnt % 2][:], gcol, bcol,
                                              ALU.mult, ALU.add)
                        ins.then_inc(hVn, 1)
                        for sub in range(4):
                            tcnt = 4 * pcnt + sub
                            v.wait_ge(hTtr, tcnt + 1)
                            dst = sktm if not is_v else svtm
                            col0 = (rc - 8 if not is_v else rc - 16) * 128
                            ins = v.tensor_copy(dst[:, 4 * it + sub, col0:col0 + 128],
                                                ptr[tcnt % 2][:])
                            ins.then_inc(hVtr, 1)
                for p_i in range(NP):
                    v.wait_ge(hTkv, p_i + 1)
                    v.tensor_scalar(sksq[:, 0:128], pqkv[p_i % 2][0:128, 0:128],
                                    stemp[:, p_i:p_i + 1], 1.0 / T, ALU.mult, ALU.mult)
                    _tnop(v)
                    ins = v.tensor_add(skv16[:, p_i, :], sksq[:, 0:128], satnb[:, p_i, :])
                    ins.then_inc(hVkv, 1)

        with ExitStack() as g2:
            asb = lambda n, s, dt=F32: g2.enter_context(nc.sbuf_tensor(n, s, dt))
            aps = lambda n, s, dt=F32: g2.enter_context(nc.psum_tensor(n, s, dt))
            sp1w = asb("l1a_p1w", [128, 8, C], BF16)
            sp2w = asb("l1a_p2w", [128, 8, C], BF16)
            sp1b = asb("l1a_sp1b", [128, 8])
            sp2b = asb("l1a_p2br", [1, C], BF16)
            sones16 = asb("l1a_ones16", [1, 128], BF16)
            sq16g = [asb(f"l1a_q16g{i}", [128, 8, 512], BF16) for i in range(2)]
            slnxg = [asb(f"l1a_lnxg{i}", [128, 8, 512], BF16) for i in range(2)]
            satt2 = asb("l1a_satt2", [128, 8, 512], BF16)
            sh1 = asb("l1a_sh1", [128, 8, 512], BF16)
            sxb2 = [asb(f"l1a_sxb2{i}", [128, 4, C]) for i in range(2)]
            sx3 = [asb(f"l1a_sx3{i}", [128, 4, C]) for i in range(2)]
            pat = [aps(f"l1a_pat{i}", [128, 512]) for i in range(2)]
            pp1 = [aps(f"l1a_pp1{i}", [128, 512]) for i in range(2)]
            po = [aps(f"l1a_po{i}", [128, 512]) for i in range(2)]

            blk = g2.enter_context(nc.Block("l1a_G2", no_gpsimd_drain=True))
            G1F = [(hVkv, NP), (hSq, 32 * NT)]

            def g1w(eng):
                for sem_, val_ in G1F:
                    eng.wait_ge(sem_, val_)

            @blk.gpsimd
            def _(g):
                g1w(g)
                g.dma_start(out=sp1w[:], in_=d["l1a_p1wT"].rearrange("(k p) r -> p k r", p=128)).then_inc(iLq, 16)
                g.dma_start(out=sp2w[:], in_=d["l1a_p2wT"].rearrange("(k p) r -> p k r", p=128)).then_inc(iLq, 16)
                g.dma_start(out=sp2b[:], in_=d["l1a_p2b"]).then_inc(iLq, 16)
                for it in range(NT):
                    t0 = it * 512
                    if it >= 2:
                        g.wait_ge(iVep, 4 * (it - 1))
                    g.dma_start(out=sxb2[it % 2][:],
                                in_=xb3[t0:t0 + 512, :].rearrange("(s p) c -> p s c", p=128)
                                ).then_inc(iLxb, 16)
                    g.wait_ge(iVep, 4 * (it + 1))
                    g.dma_start(out=x_out[t0:t0 + 512, :].rearrange("(s p) c -> p s c", p=128),
                                in_=sx3[it % 2][:]).then_inc(iSx3, 16)
                g.wait_ge(iSx3, 16 * NT)

            @blk.sync
            def _(s):
                g1w(s)
                s.dma_start(out=sp1b[:], in_=d["l1a_p1b128"]).then_inc(iLqs, 16)
                for it in range(NT):
                    t0 = it * 512
                    if it >= 2:
                        s.wait_ge(iTat, 8 * (it - 1))
                        s.wait_ge(iVat, 8 * (it - 1))
                    s.dma_start(out=sq16g[it % 2][:],
                                in_=qfm3.rearrange("(k p) t -> p k t", p=128)[:, :, t0:t0 + 512]
                                ).then_inc(iLqs, 16)
                    for kk in range(8):
                        s.dma_start_transpose(slnxg[it % 2][:, kk, :],
                                              ln3cm[kk * T + t0: kk * T + t0 + 512, :]
                                              ).then_inc(iLtr, 16)

            @blk.vector
            def _(v):
                g1w(v)
                v.memset(sones16[:], 1.0)
                for it in range(NT):
                    for p_i in range(NP):
                        acnt = 8 * it + p_i
                        v.wait_ge(iTat, acnt + 1)
                        if it >= 1:
                            v.wait_ge(iTp1, 8 * it)
                        v.wait_ge(iLtr, 128 * (it + 1))
                        ins = v.tensor_add(satt2[:, p_i, :], pat[acnt % 2][:],
                                           slnxg[it % 2][:, p_i, :])
                        ins.then_inc(iVat, 1)
                    v.wait_ge(iLxb, 16 * (it + 1))
                    if it >= 2:
                        v.wait_ge(iSx3, 16 * (it - 1))
                    for sub in range(4):
                        ocnt = 4 * it + sub
                        v.wait_ge(iTp2, ocnt + 1)
                        for half in range(2):
                            ins = v.tensor_add(sx3[it % 2][:, sub, half * 512:(half + 1) * 512],
                                               po[half][:],
                                               sxb2[it % 2][:, sub, half * 512:(half + 1) * 512])
                        ins.then_inc(iVep, 1)

            @blk.tensor
            def _(t):
                g1w(t)
                t.wait_ge(iLq, 48)
                t.wait_ge(iLqs, 16)
                for it in range(NT):
                    t.wait_ge(iLqs, 16 * (it + 2))
                    for p_i in range(NP):
                        acnt = 8 * it + p_i
                        if acnt >= 2:
                            t.wait_ge(iVat, acnt - 1)
                        dst = pat[acnt % 2]
                        t.matmul(dst[0:64, :], skv16[0:64, p_i, 0:64],
                                 sq16g[it % 2][0:64, p_i, :], start=True, stop=True)
                        t.matmul(dst[64:128, :], skv16[64:128, p_i, 64:128],
                                 sq16g[it % 2][64:128, p_i, :], start=True, stop=True,
                                 tile_position=(64, 64)).then_inc(iTat, 1)
                    for rc in range(8):
                        pcnt = 8 * it + rc
                        t.wait_ge(iVat, 8 * (it + 1))
                        if pcnt >= 2:
                            t.wait_ge(iAh, pcnt - 1)
                        for kk in range(8):
                            ins = t.matmul(pp1[pcnt % 2][:], sp1w[:, kk, rc * 128:(rc + 1) * 128],
                                           satt2[:, kk, :], start=(kk == 0), stop=(kk == 7))
                        ins.then_inc(iTp1, 1)
                    for sub in range(4):
                        ocnt = 4 * it + sub
                        t.wait_ge(iAh, 8 * (it + 1))
                        if ocnt >= 1:
                            t.wait_ge(iVep, ocnt)
                        for half in range(2):
                            t.matmul(po[half][:], sones16[:], sp2b[:, half * 512:(half + 1) * 512],
                                     start=True, stop=False)
                            for kk in range(8):
                                ins = t.matmul(po[half][:], sh1[:, kk, sub * 128:(sub + 1) * 128],
                                               sp2w[:, kk, half * 512:(half + 1) * 512],
                                               start=False, stop=(kk == 7))
                        ins.then_inc(iTp2, 1)

            @blk.scalar
            def _(s):
                g1w(s)
                for it in range(NT):
                    for rc in range(8):
                        pcnt = 8 * it + rc
                        s.wait_ge(iTp1, pcnt + 1)
                        s.activation(sh1[:, rc, :], pp1[pcnt % 2][:], AF.Gelu,
                                     bias=sp1b[:, rc:rc + 1]).then_inc(iAh, 1)
    return [(iSx3, 16 * NT)]


def phase_l1_ffn(env, x_in, x_out, fence):
    nc = env.nc
    d = env.d
    NT = T // 512
    NP = 8
    E = 8
    NI = T // 512

    lnacm = nc.dram_tensor("l1f_lnacm", [8 * T, 128], BF16, kind="Internal").ap()
    lnbcm = nc.dram_tensor("l1f_lnbcm", [8 * T, 128], BF16, kind="Internal").ap()
    q4fm = nc.dram_tensor("l1f_q4fm", [C, T], BF16, kind="Internal").ap()
    f4 = nc.dram_tensor("l1f_f4", [T, C], F32, kind="Internal").ap()

    S = env.sem
    jLx = S("l1f_jLx"); jAsq = S("l1f_jAsq"); jVst = S("l1f_jVst")
    jAstd = S("l1f_jAstd"); jV2 = S("l1f_jV2"); jA2s = S("l1f_jA2s")
    jVf = S("l1f_jVf"); jVln = S("l1f_jVln"); jSln = S("l1f_jSln"); jSf4 = S("l1f_jSf4")
    kLtr = S("l1f_kLtr"); kLw = S("l1f_kLw"); kTq = S("l1f_kTq")
    kAe = S("l1f_kAe"); kSq = S("l1f_kSq"); kTtr = S("l1f_kTtr"); kVtr = S("l1f_kVtr")
    lSq = S("l1f_lSq"); lTs = S("l1f_lTs"); lAe = S("l1f_lAe")
    lVmk = S("l1f_lVmk"); lTav = S("l1f_lTav"); lVr0 = S("l1f_lVr0")
    lVrr = S("l1f_lVrr")
    lTbr = S("l1f_lTbr"); lVav = S("l1f_lVav"); lTcp = S("l1f_lTcp")
    lVep = S("l1f_lVep"); lLf4 = S("l1f_lLf4"); lSa = S("l1f_lSa")
    lLw = S("l1f_lLw"); lLwg = S("l1f_lLwg")
    nLa = S("l1f_nLa"); nLx3 = S("l1f_nLx3"); nAsq = S("l1f_nAsq")
    nVst = S("l1f_nVst"); nAstd = S("l1f_nAstd"); nVln = S("l1f_nVln")
    nSlb = S("l1f_nSlb"); nTtr = S("l1f_nTtr"); nVtr = S("l1f_nVtr")
    nTlg = S("l1f_nTlg"); nAex = S("l1f_nAex"); nVrt = S("l1f_nVrt")
    nVxa = S("l1f_nVxa"); nSxa = S("l1f_nSxa"); nLw = S("l1f_nLw"); nLwg = S("l1f_nLwg")
    mLw = S("l1f_mLw"); mTh = S("l1f_mTh"); mAh = S("l1f_mAh")
    mTy = S("l1f_mTy"); mVy = S("l1f_mVy"); mLf2 = S("l1f_mLf2")
    mLxa = S("l1f_mLxa"); mSxa = S("l1f_mSxa")

    def fw(eng):
        for sem_, val_ in fence:
            eng.wait_ge(sem_, val_)

    with ExitStack() as ph:
        psb = lambda n, s, dt=F32: ph.enter_context(nc.sbuf_tensor(n, s, dt))
        seps4 = psb("l1f_eps", [128, 1])
        sroute4 = psb("l1f_route", [128, T // 128, E])

        with ExitStack() as h0:
            asb = lambda n, s, dt=F32: h0.enter_context(nc.sbuf_tensor(n, s, dt))
            sx = [asb(f"l1f_sx{i}", [128, 4, C]) for i in range(2)]
            sfeat = [asb(f"l1f_sft{i}", [128, 4, C]) for i in range(2)]
            sln16 = [asb(f"l1f_sln16{i}", [128, 4, C], BF16) for i in range(2)]
            sln1b = asb("l1f_sln1b", [128, C])
            sqact = asb("l1f_sqact", [128, C])
            sqd = asb("l1f_sqd", [128, C])
            sstat = asb("l1f_sstat", [128, 28])
            sstd = asb("l1f_sstd", [128, 4])
            ssum = sstat[:, 0:4]; ssq = sstat[:, 4:8]; smc = sstat[:, 8:12]
            se2 = sstat[:, 12:16]; svar = sstat[:, 16:20]; srstd = sstat[:, 20:24]

            blk = h0.enter_context(nc.Block("l1f_H0", no_gpsimd_drain=True))

            @blk.sync
            def _(s):
                fw(s)
                s.dma_start(out=sln1b[:], in_=d["l1f_ln1b"]).then_inc(jSln, 16)
                for it in range(NT):
                    t0 = it * 512
                    if it >= 2:
                        s.wait_ge(jVln, it - 1)
                        s.wait_ge(jAsq, it - 1)
                    s.dma_start(out=sx[it % 2][:],
                                in_=x_in[t0:t0 + 512, :].rearrange("(s p) c -> p s c", p=128)
                                ).then_inc(jLx, 16)
                    s.wait_ge(jVln, it + 1)
                    for kk in range(8):
                        s.dma_start(out=lnacm[kk * T + t0: kk * T + t0 + 512, :].rearrange("(s2 p) c -> p s2 c", p=128),
                                    in_=sln16[it % 2][:, :, kk * 128:(kk + 1) * 128]
                                    ).then_inc(jSln, 16)

            @blk.gpsimd
            def _(g):
                fw(g)
                for it in range(NT):
                    t0 = it * 512
                    g.wait_ge(jVf, it + 1)
                    g.dma_start(out=f4[t0:t0 + 512, :].rearrange("(s p) c -> p s c", p=128),
                                in_=sfeat[it % 2][:]).then_inc(jSf4, 16)

            @blk.scalar
            def _(s):
                fw(s)
                for it in range(NT):
                    s.wait_ge(jLx, 16 * (it + 1))
                    if it >= 1:
                        s.wait_ge(jVst, it)
                    for ss in range(4):
                        ins = s.activation(sqact[:], sx[it % 2][:, ss, :], AF.Square,
                                           accum_out=ssq[:, ss:ss + 1])
                    ins.then_inc(jAsq, 1)
                    s.wait_ge(jVst, it + 1)
                    s.activation(sstd[:], svar[:], AF.Sqrt, bias=seps4[:], scale=-1.0).then_inc(jAstd, 1)
                    s.wait_ge(jV2, it + 1)
                    s.activation(sstd[:], svar[:], AF.Sqrt, bias=seps4[:], scale=-1.0).then_inc(jA2s, 1)

            @blk.vector
            def _(v):
                fw(v)
                v.memset(seps4[:], 1e-5)
                for it in range(NT):
                    xb = sx[it % 2]
                    fb = sfeat[it % 2]
                    v.wait_ge(jLx, 16 * (it + 1))
                    if it >= 2:
                        v.wait_ge(jSf4, 16 * (it - 1))
                        v.wait_ge(jSln, 16 + 128 * (it - 1))
                    for ss in range(4):
                        v.reduce_sum(ssum[:, ss:ss + 1], xb[:, ss, :], axis=AX)
                    v.wait_ge(jAsq, it + 1)
                    v.tensor_scalar_mul(smc[:], ssum[:], 1.0 / C)
                    v.tensor_scalar_mul(se2[:], ssq[:], 1.0 / C)
                    _tnop(v)
                    for ss in range(4):
                        ins = v.scalar_tensor_tensor(svar[:, ss:ss + 1], smc[:, ss:ss + 1],
                                                     smc[:, ss:ss + 1], se2[:, ss:ss + 1],
                                                     ALU.mult, ALU.subtract)
                    ins.then_inc(jVst, 1)
                    v.wait_ge(jAstd, it + 1)
                    v.reciprocal(srstd[:], sstd[:])
                    _tnop(v)
                    for ss in range(4):
                        ins = v.tensor_scalar(fb[:, ss, :], xb[:, ss, :],
                                              smc[:, ss:ss + 1], srstd[:, ss:ss + 1],
                                              ALU.subtract, ALU.mult)
                    ins.then_inc(jVf, 1)
                    for ss in range(4):
                        v.reduce_sum(ssum[:, ss:ss + 1], fb[:, ss, :], axis=AX)
                        v.tensor_mul(sqd[:], fb[:, ss, :], fb[:, ss, :])
                        v.reduce_sum(ssq[:, ss:ss + 1], sqd[:], axis=AX)
                    v.tensor_scalar_mul(smc[:], ssum[:], 1.0 / C)
                    v.tensor_scalar_mul(se2[:], ssq[:], 1.0 / C)
                    _tnop(v)
                    for ss in range(4):
                        ins = v.scalar_tensor_tensor(svar[:, ss:ss + 1], smc[:, ss:ss + 1],
                                                     smc[:, ss:ss + 1], se2[:, ss:ss + 1],
                                                     ALU.mult, ALU.subtract)
                    ins.then_inc(jV2, 1)
                    v.wait_ge(jA2s, it + 1)
                    v.reciprocal(srstd[:], sstd[:])
                    _tnop(v)
                    v.wait_ge(jSln, 16)
                    for ss in range(4):
                        v.tensor_scalar(sqd[:], fb[:, ss, :],
                                        smc[:, ss:ss + 1], srstd[:, ss:ss + 1],
                                        ALU.subtract, ALU.mult)
                        ins = v.tensor_mul(sln16[it % 2][:, ss, :], sqd[:], sln1b[:])
                    ins.then_inc(jVln, 1)

        with ExitStack() as h1o:
            osb = lambda n, s, dt=F32: h1o.enter_context(nc.sbuf_tensor(n, s, dt))
            sk_fm = osb("l1f_skfm", [128, 8, T], BF16)
            sv_ext = osb("l1f_svext", [128, T // 128, 16, 65], BF16)
            ssel2b4 = osb("l1f_sel2b", [2, 128])
            sid16 = osb("l1f_id16", [128, 128], BF16)

            with ExitStack() as h1a:
                asb = lambda n, s, dt=F32: h1a.enter_context(nc.sbuf_tensor(n, s, dt))
                aps = lambda n, s, dt=F32: h1a.enter_context(nc.psum_tensor(n, s, dt))
                scattn = asb("l1f_scattn", [128, 8, 3072], BF16)
                slnxf = [asb("l1f_lnxf", [128, 8, 512], BF16)] * 2
                sq16h = asb("l1f_sq16", [128, 4, 512], BF16)
                sv16 = [asb(f"l1f_sv16{i}", [128, 512], BF16) for i in range(2)]
                pqkv = [aps(f"l1f_pqkv{i}", [128, 512]) for i in range(2)]
                ptr = [aps(f"l1f_ptr{i}", [128, 128], BF16) for i in range(2)]

                blk = h1a.enter_context(nc.Block("l1f_H1a", no_gpsimd_drain=True))
                H0F = [(jSln, 16 + 128 * NT), (jSf4, 16 * NT)]

                def h0w(eng):
                    for sem_, val_ in H0F:
                        eng.wait_ge(sem_, val_)

                @blk.gpsimd
                def _(g):
                    h0w(g)
                    g.dma_start(out=scattn[:], in_=d["l1f_cattn"].rearrange("(k p) r -> p k r", p=128)).then_inc(kLw, 16)
                    g.dma_start(out=sid16[:], in_=d["idf32"]).then_inc(kLw, 16)
                    g.dma_start(out=ssel2b4[:], in_=d["l1a_sel2b"]).then_inc(kLw, 16)
                    for it in range(NT):
                        t0 = it * 512
                        for hf in range(2):
                            g.wait_ge(kAe, 24 * it + 4 * (hf + 1))
                            g.dma_start(out=q4fm.rearrange("(k p) t -> p k t", p=128)[:, 4 * hf:4 * (hf + 1), t0:t0 + 512],
                                        in_=sq16h[:]).then_inc(kSq, 16)

                @blk.sync
                def _(s):
                    h0w(s)
                    for it in range(NT):
                        t0 = it * 512
                        if it >= 1:
                            s.wait_ge(kTq, 24 * it)
                        for kk in range(8):
                            s.dma_start_transpose(slnxf[it % 2][:, kk, :],
                                                  lnacm[kk * T + t0: kk * T + t0 + 512, :]
                                                  ).then_inc(kLtr, 16)

                @blk.tensor
                def _(t):
                    h0w(t)
                    t.wait_ge(kLw, 48)
                    for it in range(NT):
                        t.wait_ge(kLtr, 128 * (it + 1))
                        lnf = slnxf[it % 2]
                        for rc in range(24):
                            cnt = 24 * it + rc
                            if cnt >= 2:
                                t.wait_ge(kAe, cnt - 1)
                            for kk in range(8):
                                ins = t.matmul(pqkv[cnt % 2][:], scattn[:, kk, rc * 128:(rc + 1) * 128],
                                               lnf[:, kk, :], start=(kk == 0), stop=(kk == 7))
                            ins.then_inc(kTq, 1)
                            if rc >= 16:
                                pcnt = it * 8 + (rc - 16)
                                t.wait_ge(kAe, cnt + 1)
                                for sub in range(4):
                                    tcnt = 4 * pcnt + sub
                                    if tcnt >= 2:
                                        t.wait_ge(kVtr, tcnt - 1)
                                    t.transpose(ptr[tcnt % 2][:], sv16[pcnt % 2][:, sub * 128:(sub + 1) * 128],
                                                sid16[:]).then_inc(kTtr, 1)

                @blk.scalar
                def _(s):
                    h0w(s)
                    for it in range(NT):
                        for rc in range(24):
                            cnt = 24 * it + rc
                            s.wait_ge(kTq, cnt + 1)
                            if rc < 8:
                                if rc % 4 == 0 and (it + rc) > 0:
                                    s.wait_ge(kSq, 32 * it + (16 if rc == 4 else 0))
                                s.copy(sq16h[:, rc % 4, :], pqkv[cnt % 2][:]).then_inc(kAe, 1)
                            elif rc < 16:
                                s.copy(sk_fm[:, rc - 8, it * 512:(it + 1) * 512],
                                       pqkv[cnt % 2][:]).then_inc(kAe, 1)
                            else:
                                pcnt = it * 8 + (rc - 16)
                                if pcnt >= 2:
                                    s.wait_ge(kTtr, 4 * (pcnt - 1))
                                s.copy(sv16[pcnt % 2][:], pqkv[cnt % 2][:]).then_inc(kAe, 1)

                @blk.vector
                def _(v):
                    h0w(v)
                    v.memset(sv_ext[:], 1.0)
                    for it in range(NT):
                        for pp in range(8):
                            pcnt = it * 8 + pp
                            for sub in range(4):
                                tcnt = 4 * pcnt + sub
                                v.wait_ge(kTtr, tcnt + 1)
                                ins = v.tensor_copy(
                                    sv_ext[:, 4 * it + sub, 2 * pp:2 * pp + 2, 0:64],
                                    ptr[tcnt % 2][:].rearrange("p (h e) -> p h e", h=2))
                                ins.then_inc(kVtr, 1)

            with ExitStack() as h1b:
                asb = lambda n, s, dt=F32: h1b.enter_context(nc.sbuf_tensor(n, s, dt))
                aps = lambda n, s, dt=F32: h1b.enter_context(nc.psum_tensor(n, s, dt))
                scproj = asb("l1f_scproj", [128, 8, C], BF16)
                smask = asb("l1f_smasks", [128, 4, 1024], BF16)
                sq4 = [asb("l1f_sq4", [128, 8, 512], BF16)] * 2
                spt = asb("l1f_spt", [128, 8, 512], BF16)
                sew = asb("l1f_sew", [128, 2, 1024], BF16)
                savp = asb("l1f_savp", [128, 512])
                srr0 = asb("l1f_srr0", [1, 512])
                srr1 = asb("l1f_srr1", [1, 512])
                sselB = asb("l1f_selB", [1, 128])
                savn = asb("l1f_savn", [128, 8, 512], BF16)
                sf4t = [asb("l1f_sf4t", [128, 4, C])] * 2
                psS = [aps(f"l1f_psS{i}", [128, 1024]) for i in range(2)]
                pav = [aps(f"l1f_pav{i}", [128, 512]) for i in range(2)]
                pbr = aps("l1f_pbr", [128, 512])
                pcp = aps("l1f_pcp", [128, 512])

                blk = h1b.enter_context(nc.Block("l1f_H1b", no_gpsimd_drain=True))
                H1AF = [(kSq, 32 * NT), (kVtr, 32 * NT), (kAe, 24 * NT)]

                def h1aw(eng):
                    for sem_, val_ in H1AF:
                        eng.wait_ge(sem_, val_)

                njts = [4 * (I + 1) for I in range(NI)]
                gcum = []
                gtot = 0
                for I in range(NI):
                    row = []
                    for p in range(NP):
                        row.append(gtot)
                        gtot += njts[I]
                    gcum.append(row)
                PGT = NI * NP
                SUBT = 4 * NI

                @blk.sync
                def _(s):
                    h1aw(s)
                    s.dma_start(out=sselB[:], in_=d["l1a_sel2b"][1:2, :]).then_inc(lLw, 16)
                    for I in range(NI):
                        if I >= 1:
                            s.wait_ge(lTs, gcum[I - 1][0] + njts[I - 1] * NP)
                        s.dma_start(out=sq4[I % 2][:],
                                    in_=q4fm.rearrange("(k p) t -> p k t", p=128)[:, :, I * 512:(I + 1) * 512]
                                    ).then_inc(lSq, 16)

                @blk.gpsimd
                def _(g):
                    h1aw(g)
                    g.dma_start(out=scproj[:], in_=d["l1f_cproj"].rearrange("(k p) r -> p k r", p=128)).then_inc(lLwg, 16)
                    g.dma_start(out=smask[:], in_=d["l1f_masks"].rearrange("p (s c) -> p s c", s=4)).then_inc(lLwg, 16)
                    for I in range(NI):
                        t0 = I * 512
                        if I >= 1:
                            g.wait_ge(lSa, 16 * I)
                        g.dma_start(out=sf4t[I % 2][:],
                                    in_=f4[t0:t0 + 512, :].rearrange("(s p) c -> p s c", p=128)
                                    ).then_inc(lLf4, 16)
                        g.wait_ge(lVep, 4 * (I + 1))
                        g.dma_start(out=f4[t0:t0 + 512, :].rearrange("(s p) c -> p s c", p=128),
                                    in_=sf4t[I % 2][:]).then_inc(lSa, 16)
                    g.wait_ge(lSa, 16 * NI)

                @blk.tensor
                def _(t):
                    h1aw(t)
                    t.wait_ge(lLw, 16)
                    t.wait_ge(lLwg, 32)
                    pend = []

                    def emit_pbr(t, pgx):
                        t.wait_ge(lVrr, pgx + 1)
                        t.matmul(pbr[:], ssel2b4[0:1, :], srr0[:], start=True, stop=False)
                        t.matmul(pbr[:], sselB[:], srr1[:], start=False, stop=True).then_inc(lTbr, 1)

                    for I in range(NI):
                        t.wait_ge(lSq, 16 * (I + 1))
                        njt = njts[I]
                        for p in range(NP):
                            pg = 8 * I + p
                            g0 = gcum[I][p]

                            def do_av(jt):
                                g_i = g0 + jt
                                t.wait_ge(lAe, g_i + 1)
                                if jt >= 4 * I:
                                    t.wait_ge(lVmk, 8 * pg + 2 * (jt - 4 * I + 1))
                                if jt == 0 and pg >= 1:
                                    t.wait_ge(lVr0, pg)
                                t.matmul(pav[0][0:65, :], sv_ext[:, jt, 2 * p, :],
                                         spt[:, 2 * (g_i % 4), :], start=(jt == 0), stop=(jt == njt - 1))
                                t.matmul(pav[1][0:65, :], sv_ext[:, jt, 2 * p + 1, :],
                                         spt[:, 2 * (g_i % 4) + 1, :],
                                         start=(jt == 0), stop=(jt == njt - 1)).then_inc(lTav, 1)

                            for jt in range(njt):
                                g_i = g0 + jt
                                if g_i >= 2:
                                    t.wait_ge(lAe, g_i - 1)
                                t.matmul(psS[g_i % 2][:, 0:512], sk_fm[0:64, p, jt * 128:(jt + 1) * 128],
                                         sq4[I % 2][0:64, p, :], start=True, stop=True)
                                t.matmul(psS[g_i % 2][:, 512:1024], sk_fm[64:128, p, jt * 128:(jt + 1) * 128],
                                         sq4[I % 2][64:128, p, :], start=True, stop=True,
                                         tile_position=(64, 0)).then_inc(lTs, 1)
                                if jt >= 1:
                                    do_av(jt - 1)
                                if jt == 1 and pend:
                                    emit_pbr(t, pend.pop())
                            do_av(njt - 1)
                            pend.append(pg)
                        while pend:
                            emit_pbr(t, pend.pop())
                        t.wait_ge(lVav, 8 * (I + 1))
                        for sub in range(4):
                            scnt = 4 * I + sub
                            if scnt >= 1:
                                t.wait_ge(lVep, scnt)
                            for half in range(2):
                                for p in range(NP):
                                    ins = t.matmul(pcp[:] if half == 0 else pbr[:],
                                                   savn[:, p, sub * 128:(sub + 1) * 128],
                                                   scproj[:, p, half * 512:(half + 1) * 512],
                                                   start=(p == 0), stop=(p == 7))
                            ins.then_inc(lTcp, 1)

                @blk.scalar
                def _(s):
                    h1aw(s)
                    for I in range(NI):
                        njt = njts[I]
                        for p in range(NP):
                            pg = 8 * I + p
                            g0 = gcum[I][p]
                            for jt in range(njt):
                                g_i = g0 + jt
                                s.wait_ge(lTs, g_i + 1)
                                if g_i >= 4:
                                    s.wait_ge(lTav, g_i - 3)
                                diag = jt >= 4 * I
                                if diag:
                                    if jt - 4 * I >= 2:
                                        s.wait_ge(lVmk, 8 * (8 * I + p) + 2 * (jt - 4 * I - 1))
                                    ins = s.activation(sew[:, g_i % 2, :], psS[g_i % 2][:], AF.Exp, scale=0.125)
                                else:
                                    ins = s.activation(spt[:, 2 * (g_i % 4):2 * (g_i % 4) + 2, :],
                                                       psS[g_i % 2][:], AF.Exp, scale=0.125)
                                ins.then_inc(lAe, 1)

                @blk.vector
                def _(v):
                    h1aw(v)
                    v.wait_ge(lLwg, 32)
                    pendv = []

                    def emit_savn(v, pp_, ppg_):
                        v.wait_ge(lTbr, ppg_ + 1)
                        ins = v.tensor_mul(savn[:, pp_, :], savp[:], pbr[:])
                        ins.then_inc(lVav, 1)

                    for I in range(NI):
                        njt = njts[I]
                        for p in range(NP):
                            g0 = gcum[I][p]
                            for jt in range(njt):
                                g_i = g0 + jt
                                if jt >= 4 * I:
                                    v.wait_ge(lAe, g_i + 1)
                                    mk = smask[:, jt - 4 * I, :].rearrange("p (h c) -> p h c", h=2)
                                    ins = v.tensor_mul(spt[:, 2 * (g_i % 4):2 * (g_i % 4) + 2, :],
                                                       sew[:, g_i % 2, :].rearrange("p (h c) -> p h c", h=2), mk)
                                    ins.then_inc(lVmk, 2)
                                if jt == 1 and pendv:
                                    emit_savn(v, *pendv.pop())
                            pg = 8 * I + p
                            v.wait_ge(lTav, g0 + njt)
                            v.tensor_copy(savp[0:64, :], pav[0][0:64, :])
                            v.tensor_copy(savp[64:128, :], pav[1][0:64, :])
                            v.tensor_copy(srr0[:], pav[0][64:65, :])
                            ins = v.tensor_copy(srr1[:], pav[1][64:65, :])
                            ins.then_inc(lVr0, 1)
                            _tnop(v)
                            v.reciprocal(srr0[:], srr0[:])
                            ins = v.reciprocal(srr1[:], srr1[:])
                            ins.then_inc(lVrr, 1)
                            pendv.append((p, pg))
                        while pendv:
                            emit_savn(v, *pendv.pop())
                        v.wait_ge(lLf4, 16 * (I + 1))
                        for sub in range(4):
                            scnt = 4 * I + sub
                            v.wait_ge(lTcp, scnt + 1)
                            v.tensor_add(sf4t[I % 2][:, sub, 0:512], pcp[:],
                                         sf4t[I % 2][:, sub, 0:512])
                            ins = v.tensor_add(sf4t[I % 2][:, sub, 512:1024], pbr[:],
                                               sf4t[I % 2][:, sub, 512:1024])
                            ins.then_inc(lVep, 1)
        _phase_l1_ffn_tail(env, x_in, x_out, f4, lnbcm, sroute4, seps4,
                           [(lSa, 16 * NI)],
                           dict(nLa=nLa, nLx3=nLx3, nAsq=nAsq, nVst=nVst, nAstd=nAstd,
                                nVln=nVln, nSlb=nSlb, nTtr=nTtr, nVtr=nVtr, nTlg=nTlg,
                                nAex=nAex, nVrt=nVrt, nVxa=nVxa, nSxa=nSxa, nLw=nLw, nLwg=nLwg,
                                mLw=mLw, mTh=mTh, mAh=mAh, mTy=mTy, mVy=mVy,
                                mLf2=mLf2, mLxa=mLxa, mSxa=mSxa))
    return [(mSxa, 16 * (T // 256) * E)]


def _phase_l1_ffn_tail(env, x_in, x_out, f4, lnbcm, sroute4, seps4, fence, sm):
    nc = env.nc
    d = env.d
    NT = T // 512
    E = 8
    NIT = T // 256
    nLa = sm["nLa"]; nLx3 = sm["nLx3"]; nAsq = sm["nAsq"]; nVst = sm["nVst"]
    nAstd = sm["nAstd"]; nVln = sm["nVln"]; nSlb = sm["nSlb"]; nTtr = sm["nTtr"]
    nVtr = sm["nVtr"]; nTlg = sm["nTlg"]; nAex = sm["nAex"]; nVrt = sm["nVrt"]
    nVxa = sm["nVxa"]; nSxa = sm["nSxa"]; nLw = sm["nLw"]; nLwg = sm["nLwg"]
    mLw = sm["mLw"]; mTh = sm["mTh"]; mAh = sm["mAh"]; mTy = sm["mTy"]
    mVy = sm["mVy"]; mLf2 = sm["mLf2"]; mLxa = sm["mLxa"]; mSxa = sm["mSxa"]

    with ExitStack() as h2:
        asb = lambda n, s, dt=F32: h2.enter_context(nc.sbuf_tensor(n, s, dt))
        aps = lambda n, s, dt=F32: h2.enter_context(nc.psum_tensor(n, s, dt))
        sa = [asb(f"l1f_h2a{i}", [128, 4, C]) for i in range(2)]
        sx3t = [asb(f"l1f_h2x{i}", [128, 4, C]) for i in range(2)]
        slnb = [asb(f"l1f_h2l{i}", [128, 4, C], BF16) for i in range(2)]
        sxai = [asb(f"l1f_h2xa{i}", [128, 4, C]) for i in range(2)]
        sln2b = asb("l1f_sln2b", [128, C])
        srw16 = asb("l1f_srw16", [128, 8, E], BF16)
        sid16b = asb("l1f_id16b", [128, 128], BF16)
        strx = [asb(f"l1f_h2trx{i}", [128, 128], BF16) for i in range(2)]
        sqact = asb("l1f_h2sq", [128, C])
        sqd = asb("l1f_h2sqd", [128, C])
        sstat = asb("l1f_h2st", [128, 28])
        sstd = asb("l1f_h2sd", [128, 4])
        slg = asb("l1f_h2lg", [128, E])
        srr = asb("l1f_h2rr", [128, 8])
        swm = asb("l1f_h2wm", [128, 2, E])
        ptrb = [aps(f"l1f_h2ptr{i}", [128, 128], BF16) for i in range(2)]
        plg = aps("l1f_h2plg", [128, E])
        ssum = sstat[:, 0:4]; ssq = sstat[:, 4:8]; smc = sstat[:, 8:12]
        se2 = sstat[:, 12:16]; svar = sstat[:, 16:20]; srstd = sstat[:, 20:24]

        blk = h2.enter_context(nc.Block("l1f_H2", no_gpsimd_drain=True))

        def fw2(eng):
            for sem_, val_ in fence:
                eng.wait_ge(sem_, val_)

        @blk.sync
        def _(s):
            fw2(s)
            s.dma_start(out=sln2b[:], in_=d["l1f_ln2b"]).then_inc(nLw, 16)
            for it in range(NT):
                t0 = it * 512
                if it >= 2:
                    s.wait_ge(nVln, it - 1)
                    s.wait_ge(nAsq, it - 1)
                s.dma_start(out=sa[it % 2][:],
                            in_=f4[t0:t0 + 512, :].rearrange("(s p) c -> p s c", p=128)
                            ).then_inc(nLa, 16)
                s.wait_ge(nVln, it + 1)
                for kk in range(8):
                    s.dma_start(out=lnbcm[kk * T + t0: kk * T + t0 + 512, :].rearrange("(s2 p) c -> p s2 c", p=128),
                                in_=slnb[it % 2][:, :, kk * 128:(kk + 1) * 128]
                                ).then_inc(nSlb, 16)

        @blk.gpsimd
        def _(g):
            fw2(g)
            g.dma_start(out=srw16[:], in_=d["l1f_router"].rearrange("(k p) e -> p k e", p=128),
                        allow_slow_non_contiguous=True).then_inc(nLwg, 16)
            g.dma_start(out=sid16b[:], in_=d["idf32"]).then_inc(nLwg, 16)
            for it in range(NT):
                t0 = it * 512
                if it >= 2:
                    g.wait_ge(nVxa, it - 1)
                g.dma_start(out=sx3t[it % 2][:],
                            in_=x_in[t0:t0 + 512, :].rearrange("(s p) c -> p s c", p=128)
                            ).then_inc(nLx3, 16)
                g.wait_ge(nVxa, it + 1)
                g.dma_start(out=x_out[t0:t0 + 512, :].rearrange("(s p) c -> p s c", p=128),
                            in_=sxai[it % 2][:]).then_inc(nSxa, 16)

        @blk.scalar
        def _(s):
            fw2(s)
            for it in range(NT):
                s.wait_ge(nLa, 16 * (it + 1))
                if it >= 1:
                    s.wait_ge(nVst, it)
                for ss in range(4):
                    ins = s.activation(sqact[:], sa[it % 2][:, ss, :], AF.Square,
                                       accum_out=ssq[:, ss:ss + 1])
                ins.then_inc(nAsq, 1)
                s.wait_ge(nVst, it + 1)
                s.activation(sstd[:], svar[:], AF.Sqrt, bias=seps4[:], scale=-1.0).then_inc(nAstd, 1)
                for su in range(4):
                    gidx = 4 * it + su
                    s.wait_ge(nVrt, 3 * gidx + 1)
                    s.activation(srr[:, 3:4], srr[:, 1:2], AF.Exp,
                                 bias=srr[:, 2:3]).then_inc(nAex, 1)

        @blk.vector
        def _(v):
            fw2(v)
            for it in range(NT):
                ab = sa[it % 2]
                v.wait_ge(nLa, 16 * (it + 1))
                if it >= 2:
                    v.wait_ge(nSlb, 128 * (it - 1))
                    v.wait_ge(nSxa, 16 * (it - 1))
                    v.wait_ge(nTlg, 4 * (it - 1))
                for ss in range(4):
                    v.reduce_sum(ssum[:, ss:ss + 1], ab[:, ss, :], axis=AX)
                v.wait_ge(nAsq, it + 1)
                v.tensor_scalar_mul(smc[:], ssum[:], 1.0 / C)
                v.tensor_scalar_mul(se2[:], ssq[:], 1.0 / C)
                _tnop(v)
                for ss in range(4):
                    ins = v.scalar_tensor_tensor(svar[:, ss:ss + 1], smc[:, ss:ss + 1],
                                                 smc[:, ss:ss + 1], se2[:, ss:ss + 1],
                                                 ALU.mult, ALU.subtract)
                ins.then_inc(nVst, 1)
                v.wait_ge(nAstd, it + 1)
                v.reciprocal(srstd[:], sstd[:])
                _tnop(v)
                v.wait_ge(nLw, 16)
                for ss in range(4):
                    v.tensor_scalar(sqd[:], ab[:, ss, :],
                                    smc[:, ss:ss + 1], srstd[:, ss:ss + 1],
                                    ALU.subtract, ALU.mult)
                    ins = v.tensor_mul(slnb[it % 2][:, ss, :], sqd[:], sln2b[:])
                ins.then_inc(nVln, 1)
                v.wait_ge(nLx3, 16 * (it + 1))
                for ss in range(4):
                    ins = v.tensor_add(sxai[it % 2][:, ss, :], sx3t[it % 2][:, ss, :],
                                       ab[:, ss, :])
                ins.then_inc(nVxa, 1)
                for su in range(4):
                    gidx = 4 * it + su
                    for kk in range(8):
                        tcnt = 8 * gidx + kk
                        v.wait_ge(nTtr, tcnt + 1)
                        v.tensor_copy(strx[tcnt % 2][:], ptrb[tcnt % 2][:]).then_inc(nVtr, 1)
                    v.wait_ge(nTlg, gidx + 1)
                    v.tensor_copy(slg[:], plg[:])
                    _tnop(v)
                    v.reduce_max(srr[:, 0:1], slg[:], axis=AX)
                    _tnop(v)
                    v.tensor_scalar(swm[:, 0, :], slg[:], srr[:, 0:1], None, ALU.is_equal)
                    v.tensor_scalar(srr[:, 2:3], srr[:, 0:1], -1.0, None, ALU.mult)
                    _tnop(v)
                    v.scalar_tensor_tensor(sqd[:, 0:E], swm[:, 0, :], -1e30, slg[:],
                                           ALU.mult, ALU.add)
                    _tnop(v)
                    ins = v.reduce_max(srr[:, 1:2], sqd[:, 0:E], axis=AX)
                    ins.then_inc(nVrt, 1)
                    _tnop(v)
                    v.tensor_scalar(swm[:, 1, :], sqd[:, 0:E], srr[:, 1:2], None, ALU.is_equal)
                    v.wait_ge(nAex, gidx + 1)
                    v.tensor_scalar(srr[:, 4:5], srr[:, 3:4], 1.0, None, ALU.add)
                    _tnop(v)
                    v.reciprocal(srr[:, 5:6], srr[:, 4:5])
                    _tnop(v)
                    v.tensor_mul(srr[:, 6:7], srr[:, 3:4], srr[:, 5:6])
                    _tnop(v)
                    v.tensor_scalar(sqd[:, 0:E], swm[:, 0, :], srr[:, 5:6], None, ALU.mult)
                    _tnop(v)
                    ins = v.scalar_tensor_tensor(sroute4[:, gidx, :], swm[:, 1, :],
                                                 srr[:, 6:7], sqd[:, 0:E],
                                                 ALU.mult, ALU.add)
                    ins.then_inc(nVrt, 2)

        @blk.tensor
        def _(t):
            fw2(t)
            t.wait_ge(nLwg, 32)
            for it in range(NT):
                t.wait_ge(nVln, it + 1)
                for su in range(4):
                    gidx = 4 * it + su
                    for kk in range(8):
                        tcnt = 8 * gidx + kk
                        if tcnt >= 2:
                            t.wait_ge(nVtr, tcnt - 1)
                        t.transpose(ptrb[tcnt % 2][:], slnb[it % 2][:, su, kk * 128:(kk + 1) * 128],
                                    sid16b[:]).then_inc(nTtr, 1)
                        t.wait_ge(nVtr, tcnt + 1)
                        ins = t.matmul(plg[:], strx[tcnt % 2][:], srw16[:, kk, :],
                                       start=(kk == 0), stop=(kk == 7))
                    ins.then_inc(nTlg, 1)


    with ExitStack() as h3:
        asb = lambda n, s, dt=F32: h3.enter_context(nc.sbuf_tensor(n, s, dt))
        aps = lambda n, s, dt=F32: h3.enter_context(nc.psum_tensor(n, s, dt))
        sW1m = [asb(f"l1f_w1m{i}", [128, 8, 2048], BF16) for i in range(2)]
        sW2m = [asb(f"l1f_w2m{i}", [128, 16, C], BF16) for i in range(2)]
        sh = asb("l1f_h3h", [128, 16, 256], BF16)
        sf2t = [asb(f"l1f_h3f{i}", [128, 8, 256], BF16) for i in range(2)]
        sxa = [asb(f"l1f_h3xa{i}", [128, 2, C]) for i in range(2)]
        ph_ = [aps(f"l1f_h3ph{i}", [128, 256]) for i in range(2)]
        po = [aps(f"l1f_h3po{i}", [128, 512]) for i in range(4)]

        blk = h3.enter_context(nc.Block("l1f_H3", no_gpsimd_drain=True))
        H2F = [(nSlb, 128 * NT), (nSxa, 16 * NT), (nVrt, 12 * NT)]

        def h2w(eng):
            for sem_, val_ in H2F:
                eng.wait_ge(sem_, val_)

        @blk.gpsimd
        def _(g):
            h2w(g)
            for e in range(E):
                if e >= 2:
                    g.wait_ge(mTy, NIT * (e - 1))
                g.dma_start(out=sW1m[e % 2][:],
                            in_=d["l1f_w1"][:, e * 2048:(e + 1) * 2048].rearrange("(k p) f -> p k f", p=128)
                            ).then_inc(mLw, 16)
                g.dma_start(out=sW2m[e % 2][:],
                            in_=d["l1f_w2"][e * 2048:(e + 1) * 2048, :].rearrange("(k p) dd -> p k dd", p=128)
                            ).then_inc(mLw, 16)

        @blk.sync
        def _(s):
            h2w(s)
            NTOT = NIT * E

            def f2load(n):
                ee, ii = divmod(n, NIT)
                t0 = ii * 256
                if n >= 2:
                    s.wait_ge(mTh, 16 * (n - 1))
                for kk in range(8):
                    s.dma_start_transpose(sf2t[ii % 2][:, kk, :],
                                          lnbcm[kk * T + t0: kk * T + t0 + 256, :]
                                          ).then_inc(mLf2, 16)

            for e in range(E):
                for it in range(NIT):
                    n = NIT * e + it
                    t0 = it * 256
                    if n == 0:
                        f2load(0)
                        f2load(1)
                    if n >= 1:
                        s.wait_ge(mSxa, 16 * n)
                    s.dma_start(out=sxa[it % 2][:],
                                in_=x_out[t0:t0 + 256, :].rearrange("(s p) c -> p s c", p=128)
                                ).then_inc(mLxa, 16)
                    if n + 2 < NTOT:
                        f2load(n + 2)
                    s.wait_ge(mVy, n + 1)
                    s.dma_start(out=x_out[t0:t0 + 256, :].rearrange("(s p) c -> p s c", p=128),
                                in_=sxa[it % 2][:]).then_inc(mSxa, 16)
            s.wait_ge(mSxa, 16 * NIT * E)

        @blk.tensor
        def _(t):
            h2w(t)
            for e in range(E):
                t.wait_ge(mLw, 32 * (e + 1))
                for it in range(NIT):
                    n = NIT * e + it
                    t.wait_ge(mLf2, 128 * (n + 1))
                    for fc in range(16):
                        if fc % 2 == 0 and 16 * n + fc >= 1:
                            t.wait_ge(mAh, 16 * n + fc)
                        for kk in range(8):
                            ins = t.matmul(ph_[fc % 2][:], sW1m[e % 2][:, kk, fc * 128:(fc + 1) * 128],
                                           sf2t[it % 2][:, kk, :],
                                           start=(kk == 0), stop=(kk == 7))
                        ins.then_inc(mTh, 1)
                    if n >= 1:
                        t.wait_ge(mVy, n)
                    for fc in range(16):
                        if fc % 2 == 0:
                            t.wait_ge(mAh, min(16 * n + fc + 2, 16 * n + 16))
                        for sub in range(2):
                            for half in range(2):
                                ins = t.matmul(po[2 * sub + half][:],
                                               sh[:, fc, sub * 128:(sub + 1) * 128],
                                               sW2m[e % 2][:, fc, half * 512:(half + 1) * 512],
                                               start=(fc == 0), stop=(fc == 15))
                    ins.then_inc(mTy, 1)

        @blk.scalar
        def _(s):
            h2w(s)
            for e in range(E):
                for it in range(NIT):
                    n = NIT * e + it
                    for fc in range(16):
                        s.wait_ge(mTh, 16 * n + fc + 1)
                        s.activation(sh[:, fc, :], ph_[fc % 2][:], AF.Gelu).then_inc(mAh, 1)

        @blk.vector
        def _(v):
            h2w(v)
            for e in range(E):
                for it in range(NIT):
                    n = NIT * e + it
                    v.wait_ge(mTy, n + 1)
                    v.wait_ge(mLxa, 16 * (n + 1))
                    for sub in range(2):
                        wcol = sroute4[:, 2 * it + sub, e:e + 1]
                        for half in range(2):
                            ins = v.scalar_tensor_tensor(
                                sxa[it % 2][:, sub, half * 512:(half + 1) * 512],
                                po[2 * sub + half][:], wcol,
                                sxa[it % 2][:, sub, half * 512:(half + 1) * 512],
                                ALU.mult, ALU.add)
                    ins.then_inc(mVy, 1)
    return


def _l1f_masks():
    il = np.arange(512)[None, :]
    jl = np.arange(128)[:, None]
    m = np.zeros((128, 4, 2, 512), np.float32)
    for s_ in range(4):
        m[:, s_, 0, :] = (il >= 128 * s_ + jl).astype(np.float32)
        m[:, s_, 1, :] = m[:, s_, 0, :]
    return np.ascontiguousarray(m.reshape(128, 4096))


def _host_consts():
    perm = np.concatenate([np.arange(0, C, 2), np.arange(1, C, 2)])
    tpos = np.arange(T, dtype=np.float64)
    jidx = np.arange(C // 2, dtype=np.float64)
    ang = np.outer(T - 1 - tpos, jidx)
    cosfm = np.cos(ang).T.astype(np.float32).copy()
    sinfm = np.sin(ang).T.astype(np.float32).copy()
    il = np.arange(256)[None, :]
    jl = np.arange(128)[:, None]
    maskA = (il >= jl).astype(np.float32)
    maskB = (il >= 128 + jl).astype(np.float32)
    return perm, cosfm, sinfm, maskA, maskB


def build_program(n_phases=4, detect_races=True):
    nc = bass.Bass("TRN2", target_bir_lowering=False, debug=False, detect_race_conditions=detect_races)
    top = ExitStack()
    env = Env(nc, top)
    d = env.d

    def din(name, shape, dt=F32):
        d[name] = nc.dram_tensor(name, shape, dt, kind="ExternalInput").ap()

    din("x0", [T, C])
    din("l0_wq", [C, C]); din("l0_wk", [C, C])
    din("l0_bq", [1, C]); din("l0_bk", [1, C])
    din("l0_wvT", [C, C]); din("l0_wp", [C, C])
    din("l0_bv128", [128, 8]); din("l0_bp", [1, C])
    din("cosfm", [C // 2, T]); din("sinfm", [C // 2, T])
    din("l0_maskA", [128, 256]); din("l0_maskB", [128, 256])
    din("idf32", [128, 128])
    din("l0f_centT", [C, 8])
    din("l0f_w1T", [8 * C, 4096]); din("l0f_w2T", [8 * 4096, C])
    din("l0f_gam128", [8 * 128, 8]); din("l0f_beta128", [8 * 128, 8])
    din("l0f_b1c128", [8 * 128, 32]); din("l0f_b2", [8, C])
    din("l1a_qkvwT", [C, 3072]); din("l1a_qkvb128", [128, 24])
    din("l1a_sel2a", [128, 2]); din("l1a_sel2b", [2, 128])
    din("l1a_kgb", [128, 4]); din("l1a_temp128", [128, 8])
    din("l1a_atnb", [128, C])
    din("l1a_p1wT", [C, C]); din("l1a_p1b128", [128, 8])
    din("l1a_p2wT", [C, C]); din("l1a_p2b", [1, C])
    din("l1f_ln1b", [128, C]); din("l1f_ln2b", [128, C])
    din("l1f_cattn", [C, 3072]); din("l1f_cproj", [C, C])
    din("l1f_router", [C, 8]); din("l1f_masks", [128, 4096])
    din("l1f_w1", [C, 16384]); din("l1f_w2", [16384, C])

    out = nc.dram_tensor("out", [T, C], F32, kind="ExternalOutput").ap()

    streams = [d["x0"]]
    names = ["x1", "x2", "x3"]
    for i in range(n_phases - 1):
        streams.append(nc.dram_tensor(names[i], [T, C], F32, kind="Internal").ap())
    streams.append(out)
    assert len(streams) == n_phases + 1

    fence = []
    fence = phase_l0_attn(env, streams[0], streams[1], fence)
    if n_phases >= 2:
        fence = env.phase_boundary(fence)
        fence = phase_l0_ffn(env, streams[1], streams[2], fence)
    if n_phases >= 3:
        fence = env.phase_boundary(fence)
        fence = phase_l1_attn(env, streams[2], streams[3], fence)
    if n_phases >= 4:
        fence = env.phase_boundary(fence)
        fence = phase_l1_ffn(env, streams[3], streams[4], fence)

    top.close()
    return nc


def prep_in_maps(inputs):
    perm, cosfm, sinfm, maskA, maskB = _host_consts()
    wq = np.ascontiguousarray(inputs["L0_attn_Wq"][:, perm])
    wk = np.ascontiguousarray(inputs["L0_attn_Wk"][:, perm])
    bq = np.ascontiguousarray(inputs["L0_attn_bq"][perm])[None, :]
    bk = np.ascontiguousarray(inputs["L0_attn_bk"][perm])[None, :]
    wvT = np.ascontiguousarray(inputs["L0_attn_Wv"].T)
    wp = np.ascontiguousarray(inputs["L0_attn_Wp"])
    bv128 = np.ascontiguousarray(inputs["L0_attn_bv"].reshape(8, 128).T)
    bp = np.ascontiguousarray(inputs["L0_attn_bp"])[None, :]
    x = inputs["L0_attn_x"]

    idf32 = np.eye(128, dtype=np.float32)
    centT = np.ascontiguousarray(inputs["L0_ffn_centroids"].T)
    w1T = np.ascontiguousarray(inputs["L0_ffn_W1"].transpose(0, 2, 1)).reshape(8 * C, 4096)
    w2T = np.ascontiguousarray(inputs["L0_ffn_W2"].transpose(0, 2, 1)).reshape(8 * 4096, C)
    gam128 = np.ascontiguousarray(inputs["L0_ffn_gamma"].reshape(8, 8, 128).transpose(0, 2, 1)).reshape(8 * 128, 8)
    bet128 = np.ascontiguousarray(inputs["L0_ffn_beta"].reshape(8, 8, 128).transpose(0, 2, 1)).reshape(8 * 128, 8)
    b1c128 = np.ascontiguousarray(inputs["L0_ffn_b1"].reshape(8, 32, 128).transpose(0, 2, 1)).reshape(8 * 128, 32)
    b2 = np.ascontiguousarray(inputs["L0_ffn_b2"])
    qkvwT = np.ascontiguousarray(inputs["L1_attn_qkv_w"].T)
    qkvb128 = np.ascontiguousarray(inputs["L1_attn_qkv_b"].reshape(24, 128).T)
    sel2a = np.zeros((128, 2), np.float32); sel2a[:64, 0] = 1.0; sel2a[64:, 1] = 1.0
    sel2b = np.ascontiguousarray(sel2a.T)
    kgb = np.stack([np.tile(inputs["L1_attn_kn_g"], 2), np.tile(inputs["L1_attn_kn_b"], 2),
                    np.tile(inputs["L1_attn_vn_g"], 2), np.tile(inputs["L1_attn_vn_b"], 2)],
                   axis=1).astype(np.float32)
    tarr = inputs["L1_attn_temperature"].reshape(16)
    temp128 = np.stack([np.repeat(tarr[2 * p:2 * p + 2], 64) for p in range(8)],
                       axis=1).astype(np.float32)
    atnb = np.zeros((128, C), np.float32)
    ab = inputs["L1_attn_attn_bias"]
    for p in range(8):
        atnb[0:64, p * 128:p * 128 + 64] = ab[2 * p]
        atnb[64:128, p * 128 + 64:p * 128 + 128] = ab[2 * p + 1]
    p1wT = np.ascontiguousarray(inputs["L1_attn_p1_w"].T)
    p1b128 = np.ascontiguousarray(inputs["L1_attn_p1_b"].reshape(8, 128).T)
    p2wT = np.ascontiguousarray(inputs["L1_attn_p2_w"].T)
    p2b = np.ascontiguousarray(inputs["L1_attn_p2_b"])[None, :]
    shared = dict(l0_wq=wq, l0_wk=wk, l0_bq=bq, l0_bk=bk, l0_wvT=wvT, l0_wp=wp,
                  l0_bv128=bv128, l0_bp=bp, cosfm=cosfm, sinfm=sinfm,
                  l0_maskA=maskA, l0_maskB=maskB,
                  idf32=idf32, l0f_centT=centT, l0f_w1T=w1T, l0f_w2T=w2T,
                  l0f_gam128=gam128, l0f_beta128=bet128, l0f_b1c128=b1c128, l0f_b2=b2,
                  l1a_qkvwT=qkvwT, l1a_qkvb128=qkvb128, l1a_sel2a=sel2a, l1a_sel2b=sel2b,
                  l1a_kgb=kgb, l1a_temp128=temp128, l1a_atnb=atnb,
                  l1a_p1wT=p1wT, l1a_p1b128=p1b128, l1a_p2wT=p2wT, l1a_p2b=p2b,
                  l1f_ln1b=np.ascontiguousarray(np.broadcast_to(inputs["L1_ffn_ln1_w"], (128, C))),
                  l1f_ln2b=np.ascontiguousarray(np.broadcast_to(inputs["L1_ffn_ln2_w"], (128, C))),
                  l1f_cattn=np.ascontiguousarray(inputs["L1_ffn_c_attn_w"]),
                  l1f_cproj=np.ascontiguousarray(inputs["L1_ffn_c_proj_w"]),
                  l1f_router=np.ascontiguousarray(inputs["L1_ffn_router_w"]),
                  l1f_masks=_l1f_masks(),
                  l1f_w1=np.ascontiguousarray(inputs["L1_ffn_w1"]),
                  l1f_w2=np.ascontiguousarray(inputs["L1_ffn_w2"]))
    in_maps = []
    for b in range(x.shape[0]):
        m = dict(shared)
        m["x0"] = np.ascontiguousarray(x[b][:T])
        in_maps.append(m)
    return in_maps


def kernel(**inputs):
    in_maps = prep_in_maps(inputs)
    nc = build_program(N_PHASES)
    res = run_bass_kernel_spmd(nc, in_maps, core_ids=list(range(NCORES)))
    return np.stack([r["out"] for r in res.results], axis=0)


# version 16
# speedup vs baseline: 1.1055x; 1.1055x over previous
import numpy as np
import ml_dtypes
from contextlib import ExitStack

import concourse.bass as bass
import concourse.mybir as mybir
from concourse.bass_utils import run_bass_kernel_spmd

F32 = mybir.dt.float32
BF16 = mybir.dt.bfloat16
U32 = mybir.dt.uint32
AF = mybir.ActivationFunctionType
ALU = mybir.AluOpType
AX = mybir.AxisListType.X

T = 4096
C = 1024
NCORES = 8

N_PHASES = 4
DEV_DUMP = False
EMIT_NOPS = True


class Env:
    def __init__(self, nc, top):
        self.nc = nc
        self.top = top
        self.d = {}
        self.pool = []
        self.cursor = 0

    def sem(self, name):
        if self.cursor < len(self.pool):
            h = self.pool[self.cursor]
        else:
            h = self.top.enter_context(self.nc.semaphore(f"pool{len(self.pool)}"))
            self.pool.append(h)
        self.cursor += 1
        return h

    def persistent_sem(self, name):
        return self.top.enter_context(self.nc.semaphore(name))

    def phase_boundary(self, fence):
        n_used = self.cursor
        self.cursor = 0
        self._bnd_n = getattr(self, "_bnd_n", 0) + 1
        bnd = self.persistent_sem(f"bnd{self._bnd_n}")
        with self.nc.Block(f"boundary{self._bnd_n}", no_gpsimd_drain=True) as blk:
            @blk.gpsimd
            def _(g):
                for sem_, val_ in fence:
                    g.wait_ge(sem_, val_)
                g.dma_reset()
                for h in self.pool[:n_used]:
                    g.sem_clear(h)
                g.sem_inc(bnd, 1)
        return [(bnd, 1)]


def _tnop(eng):
    if EMIT_NOPS:
        eng.nop(cycle_cnt=512, nofuse=True)


def _ln_stats(v, a, sx, ssum, ssq, mcols, e2, msq, var, sqj, std, rstd, nsub, sem_sq, sq_target, sem_std, std_target, inv_n):
    raise NotImplementedError


def phase_l0_attn(env, x_in, x_out, fence):
    nc = env.nc
    d = env.d

    def fw(eng):
        for sem_, val_ in fence:
            eng.wait_ge(sem_, val_)

    lnx16 = nc.dram_tensor("l0a_lnx16", [8 * T, 128], BF16, kind="Internal").ap()
    qfm = nc.dram_tensor("l0a_qfm", [C, T], BF16, kind="Internal").ap()
    kfm = nc.dram_tensor("l0a_kfm", [C, T], BF16, kind="Internal").ap()
    vtm = nc.dram_tensor("l0a_vtm", [T, C], BF16, kind="Internal").ap()

    S = env.sem
    pL = S("l0a_pL"); pLs = S("l0a_pLs"); pV = S("l0a_pV")
    pTwc = S("l0a_pTwc"); pVwc = S("l0a_pVwc"); pTbc = S("l0a_pTbc")
    pAbc = S("l0a_pAbc"); pTbp = S("l0a_pTbp"); pVbp = S("l0a_pVbp")
    aLx = S("l0a_aLx"); aLt = S("l0a_aLt"); aAsq = S("l0a_aAsq")
    aVst1 = S("l0a_aVst1"); aAstd = S("l0a_aAstd"); aVln = S("l0a_aVln")
    aLst = S("l0a_aLst"); aLtr = S("l0a_aLtr")
    aTq = S("l0a_aTq"); aTk = S("l0a_aTk"); aTv = S("l0a_aTv")
    aVq = S("l0a_aVq"); aAk = S("l0a_aAk"); aPk = S("l0a_aPk"); aVv = S("l0a_aVv")
    aSt = S("l0a_aSt")
    bLqv = S("l0a_bLqv"); bLk = S("l0a_bLk"); bLx0 = S("l0a_bLx0")
    bTw = S("l0a_bTw"); bAsp = S("l0a_bAsp"); bVmk = S("l0a_bVmk")
    bToj = S("l0a_bToj"); bVep = S("l0a_bVep"); bSx1 = S("l0a_bSx1")

    NT = T // 512
    NI = T // 256
    jts = [2 * it + 2 for it in range(NI)]
    cum0 = [sum(jts[:it]) for it in range(NI)]
    CUMJ = sum(jts)
    dg0 = [2 * it for it in range(NI)]

    with ExitStack() as ph:
        sb = lambda n, s, dt=F32: ph.enter_context(nc.sbuf_tensor(n, s, dt))
        smaskA = sb("l0a_maskA", [128, 256])
        smaskB = sb("l0a_maskB", [128, 256])
        sbpb = sb("l0a_bpb", [128, C])
        seps = sb("l0a_eps", [128, 1])
        sonec = sb("l0a_onec", [128, 1])

        saw = ExitStack()
        sbw = lambda n, s, dt=F32: saw.enter_context(nc.sbuf_tensor(n, s, dt))
        sones16 = sbw("l0a_ones16", [1, 128], BF16)
        sones512 = sbw("l0a_ones512", [1, 512], BF16)
        swcomb = sbw("l0a_wcomb", [128, 8, C], BF16)
        sbcomb = sbw("l0a_bcomb", [1, C], BF16)
        sbq = sbw("l0a_bq", [1, C], BF16)
        sbk = sbw("l0a_bk", [1, C], BF16)
        swq = sbw("l0a_wq", [128, 8, C], BF16)
        swk = sbw("l0a_wk", [128, 8, C], BF16)

        with ExitStack() as pp:
            psb = lambda n, s, dt=F32: pp.enter_context(nc.sbuf_tensor(n, s, dt))
            pps = lambda n, s, dt=F32: pp.enter_context(nc.psum_tensor(n, s, dt))
            swvT = psb("l0a_wvT", [128, 8, C], BF16)
            swp = psb("l0a_wp", [128, 8, C], BF16)
            sbv = psb("l0a_bv", [128, 8], BF16)
            sbpr = psb("l0a_bpr", [1, C])
            sonesf = psb("l0a_onesf", [1, 128])
            ppsA = pps("l0a_ppsA", [128, 512])
            ppsB = pps("l0a_ppsB", [128, 512])
            ppsR = pps("l0a_ppsR", [1, 512])

            blk = pp.enter_context(nc.Block("l0a_prep", no_gpsimd_drain=True))

            @blk.gpsimd
            def _(g):
                fw(g)
                g.dma_start(out=swq[:], in_=d["l0_wq"].rearrange("(k p) c -> p k c", p=128)).then_inc(pL, 16)
                g.dma_start(out=swk[:], in_=d["l0_wk"].rearrange("(k p) c -> p k c", p=128)).then_inc(pL, 16)
                g.dma_start(out=swvT[:], in_=d["l0_wvT"].rearrange("(k p) c -> p k c", p=128)).then_inc(pL, 16)
                g.dma_start(out=swp[:], in_=d["l0_wp"].rearrange("(k p) c -> p k c", p=128)).then_inc(pL, 16)
                g.dma_start(out=sbq[:], in_=d["l0_bq"]).then_inc(pL, 16)
                g.dma_start(out=sbk[:], in_=d["l0_bk"]).then_inc(pL, 16)
                g.dma_start(out=sbv[:], in_=d["l0_bv128"]).then_inc(pL, 16)

            @blk.sync
            def _(s):
                fw(s)
                s.dma_start(out=sbpr[:], in_=d["l0_bp"]).then_inc(pLs, 16)
                s.dma_start(out=smaskA[:], in_=d["l0_maskA"]).then_inc(pLs, 16)
                s.dma_start(out=smaskB[:], in_=d["l0_maskB"]).then_inc(pLs, 16)

            @blk.vector
            def _(v):
                fw(v)
                v.memset(sones16[:], 1.0)
                v.memset(sones512[:], 1.0)
                v.memset(seps[:], 1e-5)
                v.memset(sonec[:], 1.0)
                v.memset(sonesf[:], 1.0).then_inc(pV, 1)
                for g_i in range(16):
                    v.wait_ge(pTwc, g_i + 1)
                    kk, hh = divmod(g_i, 2)
                    src = ppsA if g_i % 2 == 0 else ppsB
                    v.tensor_copy(swcomb[:, kk, hh * 512:(hh + 1) * 512], src[:]).then_inc(pVwc, 1)
                for hh in range(2):
                    v.wait_ge(pTbp, hh + 1)
                    src = ppsA if hh == 0 else ppsB
                    v.tensor_copy(sbpb[:, hh * 512:(hh + 1) * 512], src[:]).then_inc(pVbp, 1)

            @blk.tensor
            def _(t):
                t.wait_ge(pL, 16 * 7)
                t.wait_ge(pV, 1)
                for g_i in range(16):
                    kk, hh = divmod(g_i, 2)
                    dst = ppsA if g_i % 2 == 0 else ppsB
                    if g_i >= 2:
                        t.wait_ge(pVwc, g_i - 1)
                    for m in range(8):
                        ins = t.matmul(dst[:], swvT[:, m, kk * 128:(kk + 1) * 128],
                                       swp[:, m, hh * 512:(hh + 1) * 512],
                                       start=(m == 0), stop=(m == 7))
                    ins.then_inc(pTwc, 1)
                for hh in range(2):
                    if hh >= 1:
                        t.wait_ge(pAbc, hh)
                    for m in range(8):
                        ins = t.matmul(ppsR[:], sbv[:, m:m + 1],
                                       swp[:, m, hh * 512:(hh + 1) * 512],
                                       start=(m == 0), stop=(m == 7))
                    ins.then_inc(pTbc, 1)
                t.wait_ge(pLs, 48)
                t.wait_ge(pVwc, 16)
                for hh in range(2):
                    dst = ppsA if hh == 0 else ppsB
                    t.matmul(dst[:], sonesf[:], sbpr[:, hh * 512:(hh + 1) * 512],
                             start=True, stop=True).then_inc(pTbp, 1)

            @blk.scalar
            def _(s):
                fw(s)
                for hh in range(2):
                    s.wait_ge(pTbc, hh + 1)
                    s.copy(sbcomb[:, hh * 512:(hh + 1) * 512], ppsR[:]).then_inc(pAbc, 1)

        with ExitStack() as sa:
            asb = lambda n, s, dt=F32: sa.enter_context(nc.sbuf_tensor(n, s, dt))
            aps = lambda n, s, dt=F32: sa.enter_context(nc.psum_tensor(n, s, dt))
            sx = [asb(f"l0a_sx{i}", [128, 4, C]) for i in range(2)]
            scos = [asb(f"l0a_cos{i}", [128, 4, 512], BF16) for i in range(2)]
            ssin = [asb(f"l0a_sin{i}", [128, 4, 512], BF16) for i in range(2)]
            slnx = [asb(f"l0a_slnx{i}", [128, 4, C], BF16) for i in range(2)]
            slnf = [asb(f"l0a_slnf{i}", [128, 8, 512], BF16) for i in range(2)]
            sqst = [asb(f"l0a_sqst{i}", [128, 8, 512], BF16) for i in range(2)]
            skst = [asb(f"l0a_skst{i}", [128, 8, 512], BF16) for i in range(2)]
            svst = [asb(f"l0a_svst{i}", [128, 4, C], BF16) for i in range(2)]
            skf32 = [asb(f"l0a_skf32{i}", [128, 2, 512]) for i in range(2)]
            sqact = asb("l0a_sqact", [128, C])
            sqdve = asb("l0a_sqdve", [128, C])
            spool = asb("l0a_spool", [128, 512])
            sstat = asb("l0a_sstat", [128, 28])
            sstd = asb("l0a_sstd", [128, 4])
            pq = [aps(f"l0a_pq{i}", [128, 512]) for i in range(4)]
            pk = [aps(f"l0a_pk{i}", [128, 512]) for i in range(2)]
            pv = [aps(f"l0a_pv{i}", [128, 512]) for i in range(2)]

            ssum = sstat[:, 0:4]; ssq = sstat[:, 4:8]; smc = sstat[:, 8:12]
            se2 = sstat[:, 12:16]; svar = sstat[:, 16:20]; srstd = sstat[:, 20:24]
            smsq = sstat[:, 24:28]

            blk = sa.enter_context(nc.Block("l0a_stageA", no_gpsimd_drain=True))

            @blk.sync
            def _(s):
                for it in range(NT):
                    t0 = it * 512
                    if it >= 2:
                        s.wait_ge(aVln, it - 1)
                        s.wait_ge(aAsq, it - 1)
                    s.dma_start(out=sx[it % 2][:],
                                in_=x_in[t0:t0 + 512, :].rearrange("(s p) c -> p s c", p=128)
                                ).then_inc(aLx, 16)
                    s.wait_ge(aVln, it + 1)
                    for kk in range(8):
                        s.dma_start(out=lnx16[kk * T + t0: kk * T + t0 + 512, :].rearrange("(s2 p) c -> p s2 c", p=128),
                                    in_=slnx[it % 2][:, :, kk * 128:(kk + 1) * 128]
                                    ).then_inc(aLst, 16)

            @blk.scalar
            def _(s):
                for it in range(NT):
                    t0 = it * 512
                    s.wait_ge(aLx, 16 * (it + 1))
                    if it >= 1:
                        s.wait_ge(aVst1, it)
                    for ss in range(4):
                        ins = s.activation(sqact[:], sx[it % 2][:, ss, :], AF.Square,
                                           accum_out=ssq[:, ss:ss + 1])
                    ins.then_inc(aAsq, 1)
                    s.wait_ge(aVst1, it + 1)
                    s.activation(sstd[:], svar[:], AF.Sqrt, bias=seps[:], scale=-1.0).then_inc(aAstd, 1)
                    s.wait_ge(aLst, 128 * (it + 1))
                    if it >= 2:
                        s.wait_ge(aTv, 4 * (it - 1))
                    for kk in range(8):
                        s.dma_start_transpose(slnf[it % 2][:, kk, :],
                                              lnx16[kk * T + t0: kk * T + t0 + 512, :]
                                              ).then_inc(aLtr, 16)
                    for p_i in range(4):
                        g = 4 * it + p_i
                        s.wait_ge(aTk, g + 1)
                        if g >= 2:
                            s.wait_ge(aPk, g - 1)
                        s.copy(skf32[g % 2][:, 0, :], pk[0][:])
                        s.copy(skf32[g % 2][:, 1, :], pk[1][:]).then_inc(aAk, 1)

            @blk.vector
            def _(v):
                for it in range(NT):
                    v.wait_ge(aLx, 16 * (it + 1))
                    if it >= 2:
                        v.wait_ge(aLst, 128 * (it - 1))
                    for ss in range(4):
                        v.reduce_sum(ssum[:, ss:ss + 1], sx[it % 2][:, ss, :], axis=AX)
                    v.wait_ge(aAsq, it + 1)
                    v.tensor_scalar_mul(smc[:], ssum[:], 1.0 / C)
                    v.tensor_scalar_mul(se2[:], ssq[:], 1.0 / C)
                    _tnop(v)
                    for ss in range(3):
                        v.scalar_tensor_tensor(svar[:, ss:ss + 1], smc[:, ss:ss + 1],
                                               smc[:, ss:ss + 1], se2[:, ss:ss + 1],
                                               ALU.mult, ALU.subtract)
                    ins = v.scalar_tensor_tensor(svar[:, 3:4], smc[:, 3:4],
                                                 smc[:, 3:4], se2[:, 3:4],
                                                 ALU.mult, ALU.subtract)
                    ins.then_inc(aVst1, 1)
                    v.wait_ge(aAstd, it + 1)
                    v.reciprocal(srstd[:], sstd[:])
                    _tnop(v)
                    for ss in range(4):
                        ins = v.tensor_scalar(slnx[it % 2][:, ss, :], sx[it % 2][:, ss, :],
                                              smc[:, ss:ss + 1], srstd[:, ss:ss + 1],
                                              ALU.subtract, ALU.mult)
                    ins.then_inc(aVln, 1)
                    for p_i in range(4):
                        g = 4 * it + p_i
                        v.wait_ge(aTq, g + 1)
                        v.wait_ge(aLt, 32 * (it + 1))
                        if it >= 2:
                            v.wait_ge(aSt, 48 * (it - 1))
                        cosp = scos[it % 2][:, p_i, :]
                        sinp = ssin[it % 2][:, p_i, :]
                        qs = sqst[it % 2]
                        z0 = pq[2 * (g % 2)]
                        z1 = pq[2 * (g % 2) + 1]
                        v.tensor_mul(sqdve[:, 0:512], cosp, z0[:])
                        v.tensor_mul(sqdve[:, 512:1024], sinp, z1[:])
                        v.tensor_add(qs[:, p_i, :], sqdve[:, 0:512], sqdve[:, 512:1024])
                        v.tensor_mul(sqdve[:, 0:512], cosp, z1[:])
                        v.tensor_mul(sqdve[:, 512:1024], sinp, z0[:])
                        v.tensor_sub(qs[:, p_i + 4, :], sqdve[:, 0:512], sqdve[:, 512:1024]).then_inc(aVq, 1)
                    for p_i in range(4):
                        gv = 4 * it + p_i
                        if it >= 2:
                            v.wait_ge(aSt, 48 * (it - 1))
                        v.wait_ge(aTv, gv + 1)
                        v.tensor_copy(svst[it % 2][:, p_i, 0:512], pv[0][:])
                        v.tensor_copy(svst[it % 2][:, p_i, 512:1024], pv[1][:]).then_inc(aVv, 1)

            @blk.gpsimd
            def _(g):
                for it in range(NT):
                    t0 = it * 512
                    if it >= 2:
                        g.wait_ge(aVq, 4 * (it - 1))
                        g.wait_ge(aPk, 4 * (it - 1))
                    g.dma_start(out=scos[it % 2][:],
                                in_=d["cosfm"][:, t0:t0 + 512].rearrange("(j p) t -> p j t", p=128)
                                ).then_inc(aLt, 16)
                    g.dma_start(out=ssin[it % 2][:],
                                in_=d["sinfm"][:, t0:t0 + 512].rearrange("(j p) t -> p j t", p=128)
                                ).then_inc(aLt, 16)
                    for p_i in range(4):
                        g_i = 4 * it + p_i
                        g.wait_ge(aAk, g_i + 1)
                        if it >= 2:
                            g.wait_ge(aSt, 48 * (it - 1))
                        cosp = scos[it % 2][:, p_i, :]
                        sinp = ssin[it % 2][:, p_i, :]
                        ks = skst[it % 2]
                        z0 = skf32[g_i % 2][:, 0, :]
                        z1 = skf32[g_i % 2][:, 1, :]
                        g.tensor_mul(ks[:, p_i, :], cosp, z0)
                        g.tensor_mul(spool[:], sinp, z1)
                        g.tensor_add(ks[:, p_i, :], ks[:, p_i, :], spool[:])
                        g.tensor_mul(ks[:, p_i + 4, :], cosp, z1)
                        g.tensor_mul(spool[:], sinp, z0)
                        g.tensor_sub(ks[:, p_i + 4, :], ks[:, p_i + 4, :], spool[:]).then_inc(aPk, 1)
                    g.wait_ge(aVq, 4 * (it + 1))
                    g.dma_start(out=qfm.rearrange("(k p) t -> p k t", p=128)[:, :, t0:t0 + 512],
                                in_=sqst[it % 2][:]).then_inc(aSt, 16)
                    g.wait_ge(aPk, 4 * (it + 1))
                    g.dma_start(out=kfm.rearrange("(k p) t -> p k t", p=128)[:, :, t0:t0 + 512],
                                in_=skst[it % 2][:]).then_inc(aSt, 16)
                    g.wait_ge(aVv, 4 * (it + 1))
                    g.dma_start(out=vtm[t0:t0 + 512, :].rearrange("(s p) c -> p s c", p=128),
                                in_=svst[it % 2][:]).then_inc(aSt, 16)

            @blk.tensor
            def _(t):
                for it in range(NT):
                    t.wait_ge(aLtr, 128 * (it + 1))
                    lnf = slnf[it % 2]
                    for p_i in range(4):
                        g_i = 4 * it + p_i
                        if g_i >= 2:
                            t.wait_ge(aVq, g_i - 1)
                        for half, co in ((0, p_i), (1, p_i + 4)):
                            dst = pq[2 * (g_i % 2) + half]
                            t.matmul(dst[:], sbq[:, co * 128:(co + 1) * 128], sones512[:],
                                     start=True, stop=False)
                            for kk in range(8):
                                ins = t.matmul(dst[:], swq[:, kk, co * 128:(co + 1) * 128],
                                               lnf[:, kk, :], start=False, stop=(kk == 7))
                        ins.then_inc(aTq, 1)
                    for p_i in range(4):
                        g_i = 4 * it + p_i
                        if g_i >= 1:
                            t.wait_ge(aAk, g_i)
                        for half, co in ((0, p_i), (1, p_i + 4)):
                            dst = pk[half]
                            t.matmul(dst[:], sbk[:, co * 128:(co + 1) * 128], sones512[:],
                                     start=True, stop=False)
                            for kk in range(8):
                                ins = t.matmul(dst[:], swk[:, kk, co * 128:(co + 1) * 128],
                                               lnf[:, kk, :], start=False, stop=(kk == 7))
                        ins.then_inc(aTk, 1)
                    for sub in range(4):
                        gv = 4 * it + sub
                        if gv >= 1:
                            t.wait_ge(aVv, gv)
                        for half in range(2):
                            dst = pv[half]
                            t.matmul(dst[:], sones16[:], sbcomb[:, half * 512:(half + 1) * 512],
                                     start=True, stop=False)
                            for kk in range(8):
                                ins = t.matmul(dst[:], lnf[:, kk, sub * 128:(sub + 1) * 128],
                                               swcomb[:, kk, half * 512:(half + 1) * 512],
                                               start=False, stop=(kk == 7))
                        ins.then_inc(aTv, 1)
        saw.close()

        with ExitStack() as sbk_:
            asb = lambda n, s, dt=F32: sbk_.enter_context(nc.sbuf_tensor(n, s, dt))
            aps = lambda n, s, dt=F32: sbk_.enter_context(nc.psum_tensor(n, s, dt))
            q_sb = asb("l0a_qsb", [128, 8, T], BF16)
            v_sb = asb("l0a_vsb", [128, T // 128, C], BF16)
            sk = [asb(f"l0a_sk{i}", [128, 8, 256], BF16) for i in range(2)]
            sew = [asb(f"l0a_sew{i}", [128, 256]) for i in range(2)]
            swei = [asb(f"l0a_swei{i}", [128, 256], BF16) for i in range(2)]
            snl = [asb(f"l0a_snl{i}", [128, 256]) for i in range(2)]
            sx0 = [asb(f"l0a_sx0_{i}", [128, 2, C]) for i in range(2)]
            sx1 = [asb(f"l0a_sx1_{i}", [128, 2, C]) for i in range(2)]
            pw = [aps(f"l0a_pw{i}", [128, 256]) for i in range(2)]
            po = [aps(f"l0a_po{i}", [128, 512]) for i in range(4)]

            blk = sbk_.enter_context(nc.Block("l0a_stageB", no_gpsimd_drain=True))

            @blk.sync
            def _(s):
                s.wait_ge(aSt, 48 * NT)
                s.dma_start(out=q_sb[:], in_=qfm.rearrange("(k p) t -> p k t", p=128)).then_inc(bLqv, 16)
                s.dma_start(out=v_sb[:], in_=vtm.rearrange("(j p) c -> p j c", p=128)).then_inc(bLqv, 16)
                for it in range(NI):
                    i0 = it * 256
                    if it >= 2:
                        s.wait_ge(bTw, cum0[it - 1])
                    s.dma_start(out=sk[it % 2][:],
                                in_=kfm.rearrange("(k p) t -> p k t", p=128)[:, :, i0:i0 + 256]
                                ).then_inc(bLk, 16)

            @blk.gpsimd
            def _(g):
                for it in range(NI):
                    i0 = it * 256
                    if it >= 2:
                        g.wait_ge(bVep, it - 1)
                    g.dma_start(out=sx0[it % 2][:],
                                in_=x_in[i0:i0 + 256, :].rearrange("(s p) c -> p s c", p=128)
                                ).then_inc(bLx0, 16)
                    g.wait_ge(bVep, it + 1)
                    g.dma_start(out=x_out[i0:i0 + 256, :].rearrange("(s p) c -> p s c", p=128),
                                in_=sx1[it % 2][:]).then_inc(bSx1, 16)
                g.wait_ge(bSx1, 16 * NI)

            @blk.tensor
            def _(t):
                t.wait_ge(bLqv, 32)
                for it in range(NI):
                    njt = jts[it]
                    t.wait_ge(bLk, 16 * (it + 1))
                    for jt in range(njt + 1):
                        if jt < njt:
                            cj = cum0[it] + jt
                            if cj >= 2:
                                t.wait_ge(bAsp, cj - 1)
                            for kk in range(8):
                                ins = t.matmul(pw[cj % 2][:], q_sb[:, kk, jt * 128:(jt + 1) * 128],
                                               sk[it % 2][:, kk, :], start=(kk == 0), stop=(kk == 7))
                            ins.then_inc(bTw, 1)
                        if jt >= 1:
                            jo = jt - 1
                            cj = cum0[it] + jo
                            t.wait_ge(bAsp, cj + 1)
                            if jo in (2 * it, 2 * it + 1):
                                t.wait_ge(bVmk, dg0[it] + (jo - 2 * it) + 1)
                            if jo == 0 and it >= 1:
                                t.wait_ge(bVep, it)
                            for sub in range(2):
                                for half in range(2):
                                    ins = t.matmul(po[2 * sub + half][:],
                                                   swei[cj % 2][:, sub * 128:(sub + 1) * 128],
                                                   v_sb[:, jo, half * 512:(half + 1) * 512],
                                                   start=(jo == 0), stop=(jo == njt - 1))
                            ins.then_inc(bToj, 1)

            @blk.scalar
            def _(s):
                for it in range(NI):
                    for jt in range(jts[it]):
                        cj = cum0[it] + jt
                        s.wait_ge(bTw, cj + 1)
                        if cj >= 2:
                            s.wait_ge(bToj, cj - 1)
                        s.activation(sew[cj % 2][:], pw[cj % 2][:], AF.Exp, scale=1.0 / 32.0)
                        if jt in (2 * it, 2 * it + 1):
                            dcnt = dg0[it] + (jt - 2 * it)
                            if dcnt >= 2:
                                s.wait_ge(bVmk, dcnt - 1)
                            s.activation(snl[dcnt % 2][:], sew[cj % 2][:], AF.Ln, bias=sonec[:]).then_inc(bAsp, 1)
                        else:
                            s.activation(swei[cj % 2][:], sew[cj % 2][:], AF.Ln, bias=sonec[:]).then_inc(bAsp, 1)

            @blk.vector
            def _(v):
                for it in range(NI):
                    for dd in range(2):
                        jt = 2 * it + dd
                        cj = cum0[it] + jt
                        dcnt = dg0[it] + dd
                        v.wait_ge(bAsp, cj + 1)
                        v.tensor_mul(swei[cj % 2][:], snl[dcnt % 2][:],
                                     smaskA[:] if dd == 0 else smaskB[:]).then_inc(bVmk, 1)
                    v.wait_ge(bToj, cum0[it] + jts[it])
                    v.wait_ge(bLx0, 16 * (it + 1))
                    if it >= 2:
                        v.wait_ge(bSx1, 16 * (it - 1))
                    for sub in range(2):
                        for half in range(2):
                            dst = sx1[it % 2][:, sub, half * 512:(half + 1) * 512]
                            v.tensor_add(dst, po[2 * sub + half][:], sbpb[:, half * 512:(half + 1) * 512])
                            ins = v.tensor_add(dst, dst, sx0[it % 2][:, sub, half * 512:(half + 1) * 512])
                    ins.then_inc(bVep, 1)
    return [(bSx1, 16 * NI)]


def phase_l0_ffn(env, x_in, x_out, fence):
    nc = env.nc
    d = env.d
    E = 8
    NT = T // 512

    f2cm = nc.dram_tensor("l0f_f2cm", [8 * T, 128], BF16, kind="Internal").ap()
    if DEV_DUMP:
        dbg_sc = nc.dram_tensor("dbg_sc", [128, T // 128, 8], F32, kind="ExternalOutput").ap()
        dbg_rt = nc.dram_tensor("dbg_rt", [128, T // 128, 8], F32, kind="ExternalOutput").ap()
        dbg_tr = nc.dram_tensor("dbg_tr", [128, T // 128, 8], F32, kind="ExternalOutput").ap()
        dbg_f2 = nc.dram_tensor("dbg_f2", [8 * T, 128], BF16, kind="ExternalOutput").ap()
        dbg_xa = nc.dram_tensor("dbg_xa", [T, C], F32, kind="ExternalOutput").ap()
        dbg_st = nc.dram_tensor("dbg_st", [128, T // 512, 32], F32, kind="ExternalOutput").ap()

    S = env.sem
    c0Lx = S("l0f_c0Lx"); c0A1 = S("l0f_c0A1"); c0V1 = S("l0f_c0V1")
    c0A1s = S("l0f_c0A1s"); c0Vf = S("l0f_c0Vf"); c0A2 = S("l0f_c0A2")
    c0V2 = S("l0f_c0V2"); c0A2s = S("l0f_c0A2s"); c0Vln = S("l0f_c0Vln")
    c0Sf2 = S("l0f_c0Sf2"); c0Sxa = S("l0f_c0Sxa")
    c0Ttr = S("l0f_c0Ttr"); c0Vtr = S("l0f_c0Vtr"); c0Tsc = S("l0f_c0Tsc")
    c0Vam = S("l0f_c0Vam"); c0Aal = S("l0f_c0Aal"); c0Vrt = S("l0f_c0Vrt")
    c0Lc = S("l0f_c0Lc")
    fLw = S("l0f_fLw"); fLw2 = S("l0f_fLw2"); fVw = S("l0f_fVw"); fTbc = S("l0f_fTbc")
    fAb = S("l0f_fAb"); fTtr2 = S("l0f_fTtr2")
    fVb = S("l0f_fVb"); fTh = S("l0f_fTh"); fAh = S("l0f_fAh")
    fTy = S("l0f_fTy"); fVy = S("l0f_fVy")
    fLf2 = S("l0f_fLf2"); fLxa = S("l0f_fLxa"); fSxa = S("l0f_fSxa")
    fLb = S("l0f_fLb"); fLbg = S("l0f_fLbg")

    def fw(eng):
        for sem_, val_ in fence:
            eng.wait_ge(sem_, val_)

    with ExitStack() as ph:
        sb = lambda n, s, dt=F32: ph.enter_context(nc.sbuf_tensor(n, s, dt))
        sroute = sb("l0f_route", [128, T // 128, E])
        seps2 = sb("l0f_eps", [128, 1])

        with ExitStack() as f0:
            asb = lambda n, s, dt=F32: f0.enter_context(nc.sbuf_tensor(n, s, dt))
            aps = lambda n, s, dt=F32: f0.enter_context(nc.psum_tensor(n, s, dt))
            sx1f = [asb(f"l0f_sx1f{i}", [128, 4, C]) for i in range(2)]
            sfeat = [asb(f"l0f_sfeat{i}", [128, 4, C]) for i in range(2)]
            sf2 = [asb(f"l0f_sf2_{i}", [128, 4, C], BF16) for i in range(2)]
            sxai = [asb(f"l0f_sxai{i}", [128, 4, C]) for i in range(2)]
            strx = [asb(f"l0f_strx{i}", [128, 128]) for i in range(2)]
            scent = asb("l0f_cent", [128, 8, E])
            sidf = asb("l0f_idf", [128, 128])
            ssc = asb("l0f_ssc", [128, E])
            samax = asb("l0f_amax", [128, 1])
            salpha = asb("l0f_alpha", [128, 1])
            swm = asb("l0f_wm", [128, E])
            sqact = asb("l0f_sqact", [128, C])
            sqd = asb("l0f_sqd", [128, C])
            sstat = asb("l0f_sstat", [128, 28])
            sstd = asb("l0f_sstd", [128, 4])
            if DEV_DUMP:
                sdbg_sc = asb("l0f_dbgsc", [128, T // 128, 8])
                sdbg_tr = asb("l0f_dbgtr", [128, T // 128, 8])
                sdbg_st = asb("l0f_dbgst", [128, T // 512, 32])
            ptr = [aps(f"l0f_ptr{i}", [128, 128]) for i in range(2)]
            psc = aps("l0f_psc", [128, E])

            ssum = sstat[:, 0:4]; ssq = sstat[:, 4:8]; smc = sstat[:, 8:12]
            se2 = sstat[:, 12:16]; svar = sstat[:, 16:20]; srstd = sstat[:, 20:24]
            smsq = sstat[:, 24:28]

            blk = f0.enter_context(nc.Block("l0f_F0", no_gpsimd_drain=True))

            @blk.sync
            def _(s):
                fw(s)
                s.dma_start(out=scent[:], in_=d["l0f_centT"].rearrange("(k p) e -> p k e", p=128),
                            allow_slow_non_contiguous=True).then_inc(c0Lc, 16)
                s.dma_start(out=sidf[:], in_=d["idf32"]).then_inc(c0Lc, 16)
                for it in range(NT):
                    t0 = it * 512
                    if it >= 2:
                        s.wait_ge(c0Vln, it - 1)
                        s.wait_ge(c0A1, it - 1)
                    s.dma_start(out=sx1f[it % 2][:],
                                in_=x_in[t0:t0 + 512, :].rearrange("(s p) c -> p s c", p=128)
                                ).then_inc(c0Lx, 16)
                    s.wait_ge(c0Vln, it + 1)
                    for kk in range(8):
                        s.dma_start(out=f2cm[kk * T + t0: kk * T + t0 + 512, :].rearrange("(s2 p) c -> p s2 c", p=128),
                                    in_=sf2[it % 2][:, :, kk * 128:(kk + 1) * 128]
                                    ).then_inc(c0Sf2, 16)

            @blk.gpsimd
            def _(g):
                fw(g)
                for it in range(NT):
                    t0 = it * 512
                    g.wait_ge(c0Vln, it + 1)
                    g.dma_start(out=x_out[t0:t0 + 512, :].rearrange("(s p) c -> p s c", p=128),
                                in_=sxai[it % 2][:]).then_inc(c0Sxa, 16)
                if DEV_DUMP:
                    g.wait_ge(c0Vrt, 4 * NT)
                    g.wait_ge(c0Sf2, 128 * NT)
                    g.wait_ge(c0Sxa, 16 * NT)
                    g.dma_start(out=dbg_sc, in_=sdbg_sc[:]).then_inc(c0Sxa, 16)
                    g.dma_start(out=dbg_rt, in_=sroute[:]).then_inc(c0Sxa, 16)
                    g.dma_start(out=dbg_tr, in_=sdbg_tr[:]).then_inc(c0Sxa, 16)
                    g.dma_start(out=dbg_f2, in_=f2cm).then_inc(c0Sxa, 16)
                    g.dma_start(out=dbg_xa, in_=x_out).then_inc(c0Sxa, 16)
                    g.dma_start(out=dbg_st, in_=sdbg_st[:]).then_inc(c0Sxa, 16)
                    g.wait_ge(c0Sxa, 16 * NT + 96)

            @blk.scalar
            def _(s):
                fw(s)
                for it in range(NT):
                    s.wait_ge(c0Lx, 16 * (it + 1))
                    if it >= 1:
                        s.wait_ge(c0V2, it)
                    for ss in range(4):
                        ins = s.activation(sqact[:], sx1f[it % 2][:, ss, :], AF.Square,
                                           accum_out=ssq[:, ss:ss + 1])
                    ins.then_inc(c0A1, 1)
                    s.wait_ge(c0V1, it + 1)
                    s.activation(sstd[:], svar[:], AF.Sqrt, bias=seps2[:], scale=-1.0).then_inc(c0A1s, 1)
                    s.wait_ge(c0V2, it + 1)
                    s.activation(sstd[:], svar[:], AF.Sqrt, bias=seps2[:], scale=-1.0).then_inc(c0A2s, 1)
                    for su in range(4):
                        s.wait_ge(c0Vam, 4 * it + su + 1)
                        s.activation(salpha[:], samax[:], AF.Sigmoid).then_inc(c0Aal, 1)

            @blk.vector
            def _(v):
                fw(v)
                v.memset(seps2[:], 1e-5)
                for it in range(NT):
                    xb = sx1f[it % 2]
                    fb = sfeat[it % 2]
                    v.wait_ge(c0Lx, 16 * (it + 1))
                    if it >= 2:
                        v.wait_ge(c0Tsc, 4 * (it - 1))
                        v.wait_ge(c0Sf2, 128 * (it - 1))
                        v.wait_ge(c0Sxa, 16 * (it - 1))
                    for ss in range(4):
                        v.reduce_sum(ssum[:, ss:ss + 1], xb[:, ss, :], axis=AX)
                    v.wait_ge(c0A1, it + 1)
                    v.tensor_scalar_mul(smc[:], ssum[:], 1.0 / C)
                    v.tensor_scalar_mul(se2[:], ssq[:], 1.0 / C)
                    _tnop(v)
                    for ss in range(3):
                        v.scalar_tensor_tensor(svar[:, ss:ss + 1], smc[:, ss:ss + 1],
                                               smc[:, ss:ss + 1], se2[:, ss:ss + 1],
                                               ALU.mult, ALU.subtract)
                    ins = v.scalar_tensor_tensor(svar[:, 3:4], smc[:, 3:4],
                                                 smc[:, 3:4], se2[:, 3:4],
                                                 ALU.mult, ALU.subtract)
                    ins.then_inc(c0V1, 1)
                    v.wait_ge(c0A1s, it + 1)
                    v.reciprocal(srstd[:], sstd[:])
                    _tnop(v)
                    for ss in range(4):
                        ins = v.tensor_scalar(fb[:, ss, :], xb[:, ss, :],
                                              smc[:, ss:ss + 1], srstd[:, ss:ss + 1],
                                              ALU.subtract, ALU.mult)
                    ins.then_inc(c0Vf, 1)
                    for ss in range(4):
                        v.tensor_add(sxai[it % 2][:, ss, :], xb[:, ss, :], fb[:, ss, :])
                    for ss in range(4):
                        v.reduce_sum(ssum[:, ss:ss + 1], fb[:, ss, :], axis=AX)
                        v.tensor_mul(sqd[:], fb[:, ss, :], fb[:, ss, :])
                        v.reduce_sum(ssq[:, ss:ss + 1], sqd[:], axis=AX)
                    v.tensor_scalar_mul(smc[:], ssum[:], 1.0 / C)
                    v.tensor_scalar_mul(se2[:], ssq[:], 1.0 / C)
                    _tnop(v)
                    for ss in range(3):
                        v.scalar_tensor_tensor(svar[:, ss:ss + 1], smc[:, ss:ss + 1],
                                               smc[:, ss:ss + 1], se2[:, ss:ss + 1],
                                               ALU.mult, ALU.subtract)
                    ins = v.scalar_tensor_tensor(svar[:, 3:4], smc[:, 3:4],
                                                 smc[:, 3:4], se2[:, 3:4],
                                                 ALU.mult, ALU.subtract)
                    ins.then_inc(c0V2, 1)
                    v.wait_ge(c0A2s, it + 1)
                    v.reciprocal(srstd[:], sstd[:])
                    _tnop(v)
                    if DEV_DUMP:
                        v.tensor_copy(sdbg_st[:, it, 0:28], sstat[:])
                        v.tensor_copy(sdbg_st[:, it, 28:32], sstd[:])
                    for ss in range(4):
                        ins = v.tensor_scalar(sf2[it % 2][:, ss, :], fb[:, ss, :],
                                              smc[:, ss:ss + 1], srstd[:, ss:ss + 1],
                                              ALU.subtract, ALU.mult)
                    ins.then_inc(c0Vln, 1)
                    for su in range(4):
                        gidx = 4 * it + su
                        for kk in range(8):
                            v.wait_ge(c0Ttr, 8 * gidx + kk + 1)
                            v.tensor_copy(strx[kk % 2][:], ptr[kk % 2][:]).then_inc(c0Vtr, 1)
                        v.wait_ge(c0Tsc, gidx + 1)
                        v.tensor_copy(ssc[:], psc[:])
                        _tnop(v)
                        if DEV_DUMP:
                            v.tensor_copy(sdbg_sc[:, gidx, :], ssc[:])
                            v.tensor_copy(sdbg_tr[:, gidx, :], strx[1][:, 0:8])
                        v.reduce_max(samax[:], ssc[:], axis=AX).then_inc(c0Vam, 1)
                        v.wait_ge(c0Aal, gidx + 1)
                        v.tensor_scalar(swm[:], ssc[:], samax[:], None, ALU.is_equal)
                        _tnop(v)
                        v.tensor_scalar(sroute[:, gidx, :], swm[:], salpha[:], None,
                                        ALU.mult).then_inc(c0Vrt, 1)

            @blk.tensor
            def _(t):
                fw(t)
                t.wait_ge(c0Lc, 32)
                for it in range(NT):
                    t.wait_ge(c0Vf, it + 1)
                    for su in range(4):
                        gidx = 4 * it + su
                        if gidx >= 1:
                            t.wait_ge(c0Vrt, gidx)
                        for kk in range(8):
                            if 8 * gidx + kk >= 2:
                                t.wait_ge(c0Vtr, 8 * gidx + kk - 1)
                            t.transpose(ptr[kk % 2][:], sfeat[it % 2][:, su, kk * 128:(kk + 1) * 128],
                                        sidf[:]).then_inc(c0Ttr, 1)
                            t.wait_ge(c0Vtr, 8 * gidx + kk + 1)
                            ins = t.matmul(psc[:], strx[kk % 2][:], scent[:, kk, :],
                                           start=(kk == 0), stop=(kk == 7))
                        ins.then_inc(c0Tsc, 1)

        NIT = T // 256
        with ExitStack() as f1:
            asb = lambda n, s, dt=F32: f1.enter_context(nc.sbuf_tensor(n, s, dt))
            aps = lambda n, s, dt=F32: f1.enter_context(nc.psum_tensor(n, s, dt))
            sW1 = asb("l0f_w1", [128, 8, 4096], BF16)
            sW2 = asb("l0f_w2", [128, 32, C], BF16)
            sh = asb("l0f_h", [128, 32, 256], BF16)
            sf2t = [asb(f"l0f_f2t{i}", [128, 8, 256], BF16) for i in range(2)]
            sxa = [asb(f"l0f_sxa{i}", [128, 2, C]) for i in range(2)]
            sgam = asb("l0f_gam", [128, E])
            sbet = asb("l0f_bet", [128, E], BF16)
            sb1c = asb("l0f_b1c", [128, 32])
            sb2r = asb("l0f_b2r", [1, C], BF16)
            sbias = asb("l0f_bias", [128, 32])
            sbwrow = asb("l0f_bwrow", [1, 4096], BF16)
            sidf1 = asb("l0f_id16", [128, 128], BF16)
            sones16 = asb("l0f_ones16", [1, 128], BF16)
            ph_ = [aps(f"l0f_ph{i}", [128, 256])[:] for i in range(2)]
            po = [aps(f"l0f_po{i}", [128, 512]) for i in range(4)]
            prow_t = aps("l0f_prow", [1, 512])
            prow = [prow_t, prow_t]
            pcol = aps("l0f_pcol", [128, 64], BF16)

            blk = f1.enter_context(nc.Block("l0f_F1", no_gpsimd_drain=True))
            F0FENCE = [(c0Sf2, 128 * NT), (c0Sxa, 16 * NT), (c0Vrt, 4 * NT)]

            def f0w(eng):
                for sem_, val_ in F0FENCE:
                    eng.wait_ge(sem_, val_)

            @blk.vector
            def _(v):
                f0w(v)
                v.memset(sones16[:], 1.0)
                for e in range(E):
                    v.wait_ge(fLb, 32 * (e + 1))
                    v.wait_ge(fTtr2, e + 1)
                    v.tensor_add(sbias[:], pcol[:, 0:64:2], sb1c[:]).then_inc(fVb, 1)
                    for q in range(4):
                        v.wait_ge(fTbc, 8 * e + 2 * (q + 1))
                        v.wait_ge(fLw, 64 * e + 16 * (q + 1))
                        for kk in range(8):
                            ins = v.tensor_scalar_mul(sW1[:, kk, q * 1024:(q + 1) * 1024],
                                                      sW1[:, kk, q * 1024:(q + 1) * 1024],
                                                      sgam[:, kk:kk + 1])
                        ins.then_inc(fVw, 1)
                    for it in range(NIT):
                        n = NIT * e + it
                        v.wait_ge(fTy, 4 * n + 4)
                        v.wait_ge(fLxa, 16 * (n + 1))
                        for sub in range(2):
                            mcol = sroute[:, 2 * it + sub, e:e + 1]
                            for half in range(2):
                                ins = v.scalar_tensor_tensor(
                                    sxa[it % 2][:, sub, half * 512:(half + 1) * 512],
                                    po[2 * sub + half][:], mcol,
                                    sxa[it % 2][:, sub, half * 512:(half + 1) * 512],
                                    ALU.mult, ALU.add)
                        ins.then_inc(fVy, 1)

            @blk.gpsimd
            def _(g):
                f0w(g)
                for e in range(E):
                    if e >= 1:
                        g.wait_ge(fTbc, 8 * e)
                        g.wait_ge(fTy, 4 * (NIT * e - 1) + 1)
                    g.dma_start(out=sbet[:], in_=d["l0f_beta128"][e * 128:(e + 1) * 128, :]).then_inc(fLbg, 16)
                    g.dma_start(out=sb2r[:], in_=d["l0f_b2"][e:e + 1, :]).then_inc(fLbg, 16)
                    if e == 0:
                        g.dma_start(out=sidf1[:], in_=d["idf32"]).then_inc(fLbg, 16)
                    for q in range(4):
                        if e >= 1:
                            g.wait_ge(fTh, 32 * (NIT * e - 1) + 8 * (q + 1))
                        g.dma_start(out=sW1[:, :, q * 1024:(q + 1) * 1024],
                                    in_=d["l0f_w1T"][e * C:(e + 1) * C, :].rearrange("(k p) f -> p k f", p=128)[:, :, q * 1024:(q + 1) * 1024]
                                    ).then_inc(fLw, 16)
                    for q in range(4):
                        if e >= 1:
                            g.wait_ge(fTy, 4 * (NIT * e - 1) + q + 1)
                        g.dma_start(out=sW2[:, 8 * q:8 * (q + 1), :],
                                    in_=d["l0f_w2T"][e * 4096 + 1024 * q: e * 4096 + 1024 * (q + 1), :].rearrange("(k p) dd -> p k dd", p=128)
                                    ).then_inc(fLw2, 16)

            @blk.sync
            def _(s):
                f0w(s)
                NTOT = NIT * E

                def f2load(n):
                    ee, ii = divmod(n, NIT)
                    t0 = ii * 256
                    if n >= 2:
                        s.wait_ge(fTh, 32 * (n - 1))
                    for kk in range(8):
                        s.dma_start_transpose(sf2t[ii % 2][:, kk, :],
                                              f2cm[kk * T + t0: kk * T + t0 + 256, :]
                                              ).then_inc(fLf2, 16)

                for e in range(E):
                    if e >= 1:
                        s.wait_ge(fVw, 4 * e)
                        s.wait_ge(fVb, e)
                    s.dma_start(out=sgam[:], in_=d["l0f_gam128"][e * 128:(e + 1) * 128, :]).then_inc(fLb, 16)
                    s.dma_start(out=sb1c[:], in_=d["l0f_b1c128"][e * 128:(e + 1) * 128, :]).then_inc(fLb, 16)
                    for it in range(NIT):
                        n = NIT * e + it
                        t0 = it * 256
                        if e == 0 and it == 0:
                            f2load(0)
                            f2load(1)
                        if n >= 1:
                            s.wait_ge(fSxa, 16 * n)
                        s.dma_start(out=sxa[it % 2][:],
                                    in_=x_out[t0:t0 + 256, :].rearrange("(s p) c -> p s c", p=128)
                                    ).then_inc(fLxa, 16)
                        if n + 2 < NTOT:
                            f2load(n + 2)
                        s.wait_ge(fVy, n + 1)
                        s.dma_start(out=x_out[t0:t0 + 256, :].rearrange("(s p) c -> p s c", p=128),
                                    in_=sxa[it % 2][:]).then_inc(fSxa, 16)
                s.wait_ge(fSxa, 16 * NIT * E)

            @blk.tensor
            def _(t):
                f0w(t)
                for e in range(E):
                    t.wait_ge(fLb, 32 * (e + 1))
                    t.wait_ge(fLbg, 32 * e + 48)
                    for fg in range(8):
                        q = fg // 2
                        t.wait_ge(fLw, 64 * e + 16 * (q + 1))
                        if 8 * e + fg >= 1:
                            t.wait_ge(fAb, 8 * e + fg)
                        for kk in range(8):
                            ins = t.matmul(prow[fg % 2][:], sbet[:, kk:kk + 1],
                                           sW1[:, kk, fg * 512:(fg + 1) * 512],
                                           start=(kk == 0), stop=(kk == 7))
                        ins.then_inc(fTbc, 1)
                    t.wait_ge(fAb, 8 * (e + 1))
                    if e >= 1:
                        t.wait_ge(fVb, e)
                    for fc in range(32):
                        ins = t.transpose(pcol[:, 2 * fc:2 * fc + 1], sbwrow[0:1, fc * 128:(fc + 1) * 128],
                                          sidf1[0:1, 0:1])
                    ins.then_inc(fTtr2, 1)
                    for it in range(NIT):
                        n = NIT * e + it
                        t.wait_ge(fLf2, 128 * (n + 1))
                        for fc in range(32):
                            q = fc // 8
                            if fc % 8 == 0:
                                t.wait_ge(fVw, 4 * e + q + 1)
                            if 32 * n + fc >= 2:
                                t.wait_ge(fAh, 32 * n + fc - 1)
                            for kk in range(8):
                                ins = t.matmul(ph_[fc % 2], sW1[:, kk, fc * 128:(fc + 1) * 128],
                                               sf2t[it % 2][:, kk, :],
                                               start=(kk == 0), stop=(kk == 7))
                            ins.then_inc(fTh, 1)
                        if n >= 1:
                            t.wait_ge(fVy, n)
                        for sub in range(2):
                            for half in range(2):
                                t.matmul(po[2 * sub + half][:], sones16[:],
                                         sb2r[:, half * 512:(half + 1) * 512],
                                         start=True, stop=False)
                        for fc in range(32):
                            t.wait_ge(fAh, 32 * n + fc + 1)
                            if fc % 8 == 0:
                                t.wait_ge(fLw2, 64 * e + 16 * (fc // 8 + 1))
                            for sub in range(2):
                                for half in range(2):
                                    ins = t.matmul(po[2 * sub + half][:],
                                                   sh[:, fc, sub * 128:(sub + 1) * 128],
                                                   sW2[:, fc, half * 512:(half + 1) * 512],
                                                   start=False, stop=(fc == 31))
                            if fc % 8 == 7:
                                ins.then_inc(fTy, 1)

            @blk.scalar
            def _(s):
                f0w(s)
                for e in range(E):
                    for fg in range(8):
                        s.wait_ge(fTbc, 8 * e + fg + 1)
                        s.copy(sbwrow[0:1, fg * 512:(fg + 1) * 512], prow[fg % 2][:]).then_inc(fAb, 1)
                    for it in range(NIT):
                        n = NIT * e + it
                        s.wait_ge(fVb, e + 1)
                        for fc in range(32):
                            s.wait_ge(fTh, 32 * n + fc + 1)
                            s.activation(sh[:, fc, :], ph_[fc % 2], AF.Relu,
                                         bias=sbias[:, fc:fc + 1]).then_inc(fAh, 1)
    return [(fSxa, 16 * (T // 256) * E)]


def phase_l1_attn(env, x_in, x_out, fence):
    nc = env.nc
    d = env.d
    NT = T // 512
    NP = 8

    ln3cm = nc.dram_tensor("l1a_ln3cm", [8 * T, 128], BF16, kind="Internal").ap()
    qfm3 = nc.dram_tensor("l1a_qfm", [C, T], BF16, kind="Internal").ap()
    xb3 = nc.dram_tensor("l1a_xb", [T, C], F32, kind="Internal").ap()

    S = env.sem
    gLx = S("l1a_gLx"); gAsq = S("l1a_gAsq"); gVst = S("l1a_gVst")
    gAstd = S("l1a_gAstd"); gVln = S("l1a_gVln"); gVxb = S("l1a_gVxb")
    gSln = S("l1a_gSln"); gSxb = S("l1a_gSxb")
    hLtr = S("l1a_hLtr"); hLw = S("l1a_hLw"); hLwg = S("l1a_hLwg")
    hTq = S("l1a_hTq"); hAe = S("l1a_hAe"); hTst = S("l1a_hTst")
    hVr = S("l1a_hVr"); hAsd = S("l1a_hAsd"); hTb = S("l1a_hTb")
    hVn = S("l1a_hVn"); hTtr = S("l1a_hTtr"); hVtr = S("l1a_hVtr")
    hVrc = S("l1a_hVrc")
    hSq = S("l1a_hSq"); hTkv = S("l1a_hTkv"); hVkv = S("l1a_hVkv")
    iLq = S("l1a_iLq"); iLqs = S("l1a_iLqs"); iLtr = S("l1a_iLtr"); iLxb = S("l1a_iLxb")
    iTat = S("l1a_iTat"); iVat = S("l1a_iVat"); iTp1 = S("l1a_iTp1")
    iAh = S("l1a_iAh"); iTp2 = S("l1a_iTp2"); iVep = S("l1a_iVep")
    iSx3 = S("l1a_iSx3")

    def fw(eng):
        for sem_, val_ in fence:
            eng.wait_ge(sem_, val_)

    with ExitStack() as ph:
        sb = lambda n, s, dt=F32: ph.enter_context(nc.sbuf_tensor(n, s, dt))
        seps3 = sb("l1a_eps", [128, 1])
        skv16 = sb("l1a_skv16", [128, 8, 128], BF16)

        with ExitStack() as g0:
            asb = lambda n, s, dt=F32: g0.enter_context(nc.sbuf_tensor(n, s, dt))
            sx = [asb(f"l1a_sx{i}", [128, 4, C]) for i in range(2)]
            slnf = [asb(f"l1a_slnf{i}", [128, 4, C]) for i in range(2)]
            sln16 = [asb(f"l1a_sln16{i}", [128, 4, C], BF16) for i in range(2)]
            sxb = [asb(f"l1a_sxb{i}", [128, 4, C]) for i in range(2)]
            sqact = asb("l1a_sqact", [128, C])
            sstat = asb("l1a_sstat", [128, 28])
            sstd = asb("l1a_sstd", [128, 4])
            ssum = sstat[:, 0:4]; ssq = sstat[:, 4:8]; smc = sstat[:, 8:12]
            se2 = sstat[:, 12:16]; svar = sstat[:, 16:20]; srstd = sstat[:, 20:24]

            blk = g0.enter_context(nc.Block("l1a_G0", no_gpsimd_drain=True))

            @blk.sync
            def _(s):
                fw(s)
                for it in range(NT):
                    t0 = it * 512
                    if it >= 2:
                        s.wait_ge(gVln, it - 1)
                        s.wait_ge(gAsq, it - 1)
                    s.dma_start(out=sx[it % 2][:],
                                in_=x_in[t0:t0 + 512, :].rearrange("(s p) c -> p s c", p=128)
                                ).then_inc(gLx, 16)
                    s.wait_ge(gVln, it + 1)
                    for kk in range(8):
                        s.dma_start(out=ln3cm[kk * T + t0: kk * T + t0 + 512, :].rearrange("(s2 p) c -> p s2 c", p=128),
                                    in_=sln16[it % 2][:, :, kk * 128:(kk + 1) * 128]
                                    ).then_inc(gSln, 16)

            @blk.gpsimd
            def _(g):
                fw(g)
                for it in range(NT):
                    t0 = it * 512
                    g.wait_ge(gVxb, it + 1)
                    g.dma_start(out=xb3[t0:t0 + 512, :].rearrange("(s p) c -> p s c", p=128),
                                in_=sxb[it % 2][:]).then_inc(gSxb, 16)

            @blk.scalar
            def _(s):
                fw(s)
                for it in range(NT):
                    s.wait_ge(gLx, 16 * (it + 1))
                    if it >= 1:
                        s.wait_ge(gVst, it)
                    for ss in range(4):
                        ins = s.activation(sqact[:], sx[it % 2][:, ss, :], AF.Square,
                                           accum_out=ssq[:, ss:ss + 1])
                    ins.then_inc(gAsq, 1)
                    s.wait_ge(gVst, it + 1)
                    s.activation(sstd[:], svar[:], AF.Sqrt, bias=seps3[:], scale=-1.0).then_inc(gAstd, 1)

            @blk.vector
            def _(v):
                fw(v)
                v.memset(seps3[:], 1e-5)
                for it in range(NT):
                    xb = sx[it % 2]
                    v.wait_ge(gLx, 16 * (it + 1))
                    if it >= 2:
                        v.wait_ge(gSln, 128 * (it - 1))
                        v.wait_ge(gSxb, 16 * (it - 1))
                    for ss in range(4):
                        v.reduce_sum(ssum[:, ss:ss + 1], xb[:, ss, :], axis=AX)
                    v.wait_ge(gAsq, it + 1)
                    v.tensor_scalar_mul(smc[:], ssum[:], 1.0 / C)
                    v.tensor_scalar_mul(se2[:], ssq[:], 1.0 / C)
                    _tnop(v)
                    for ss in range(4):
                        ins = v.scalar_tensor_tensor(svar[:, ss:ss + 1], smc[:, ss:ss + 1],
                                                     smc[:, ss:ss + 1], se2[:, ss:ss + 1],
                                                     ALU.mult, ALU.subtract)
                    ins.then_inc(gVst, 1)
                    v.wait_ge(gAstd, it + 1)
                    v.reciprocal(srstd[:], sstd[:])
                    _tnop(v)
                    for ss in range(4):
                        v.tensor_scalar(slnf[it % 2][:, ss, :], xb[:, ss, :],
                                        smc[:, ss:ss + 1], srstd[:, ss:ss + 1],
                                        ALU.subtract, ALU.mult)
                    for ss in range(4):
                        ins = v.tensor_copy(sln16[it % 2][:, ss, :], slnf[it % 2][:, ss, :])
                    ins.then_inc(gVln, 1)
                    for ss in range(4):
                        ins = v.tensor_add(sxb[it % 2][:, ss, :], xb[:, ss, :], slnf[it % 2][:, ss, :])
                    ins.then_inc(gVxb, 1)

        with ExitStack() as g1:
            asb = lambda n, s, dt=F32: g1.enter_context(nc.sbuf_tensor(n, s, dt))
            aps = lambda n, s, dt=F32: g1.enter_context(nc.psum_tensor(n, s, dt))
            sktm = asb("l1a_sktm", [128, T // 128, C], BF16)
            svtm = asb("l1a_svtm", [128, T // 128, C], BF16)
            sqkvw = asb("l1a_qkvw", [128, 8, 3072], BF16)
            sqkvb = asb("l1a_sqkvb", [128, 24])
            slnxf = [asb("l1a_lnxf", [128, 8, 512], BF16)] * 2
            skf = [asb("l1a_skf", [128, 512])] * 2
            sksq = asb("l1a_sksq", [128, 512])
            srows = asb("l1a_srows", [2, 4, 512])
            sk16 = [asb(f"l1a_sk16{i}", [128, 512], BF16) for i in range(2)]
            sq16h = asb("l1a_sq16", [128, 4, 512], BF16)
            ssel2a = asb("l1a_ssel2a", [128, 2])
            ssel2b = asb("l1a_ssel2b", [2, 128])
            skgb = asb("l1a_skgb", [128, 4])
            sid16 = asb("l1a_id16", [128, 128], BF16)
            stemp = asb("l1a_stemp", [128, 8])
            satnb = asb("l1a_satnb", [128, 8, 128], BF16)
            seps_r = asb("l1a_epsr", [2, 1])
            pqkv = [aps(f"l1a_pqkv{i}", [128, 512]) for i in range(2)]
            pst = [aps(f"l1a_pst{i}", [2, 512]) for i in range(2)]
            pmb = aps("l1a_pmb", [128, 512])
            prb = aps("l1a_prb", [128, 512])
            ptr = [aps(f"l1a_ptr{i}", [128, 128], BF16) for i in range(2)]

            blk = g1.enter_context(nc.Block("l1a_G1", no_gpsimd_drain=True))
            G0F = [(gSln, 128 * NT), (gSxb, 16 * NT)]

            def g0w(eng):
                for sem_, val_ in G0F:
                    eng.wait_ge(sem_, val_)

            @blk.gpsimd
            def _(g):
                g0w(g)
                g.dma_start(out=sqkvw[:], in_=d["l1a_qkvwT"].rearrange("(k p) r -> p k r", p=128)).then_inc(hLwg, 16)
                g.dma_start(out=sid16[:], in_=d["idf32"]).then_inc(hLwg, 16)
                g.dma_start(out=satnb[:], in_=d["l1a_atnb"].rearrange("p (q e) -> p q e", q=8)).then_inc(hLwg, 16)
                for it in range(NT):
                    t0 = it * 512
                    for hf in range(2):
                        g.wait_ge(hAe, 24 * it + 4 * (hf + 1))
                        g.dma_start(out=qfm3.rearrange("(k p) t -> p k t", p=128)[:, 4 * hf:4 * (hf + 1), t0:t0 + 512],
                                    in_=sq16h[:]).then_inc(hSq, 16)

            @blk.sync
            def _(s):
                g0w(s)
                s.dma_start(out=sqkvb[:], in_=d["l1a_qkvb128"]).then_inc(hLw, 16)
                s.dma_start(out=ssel2a[:], in_=d["l1a_sel2a"]).then_inc(hLw, 16)
                s.dma_start(out=ssel2b[:], in_=d["l1a_sel2b"]).then_inc(hLw, 16)
                s.dma_start(out=skgb[:], in_=d["l1a_kgb"]).then_inc(hLw, 16)
                s.dma_start(out=stemp[:], in_=d["l1a_temp128"]).then_inc(hLw, 16)
                for it in range(NT):
                    t0 = it * 512
                    if it >= 1:
                        s.wait_ge(hTq, 24 * it)
                    for kk in range(8):
                        s.dma_start_transpose(slnxf[it % 2][:, kk, :],
                                              ln3cm[kk * T + t0: kk * T + t0 + 512, :]
                                              ).then_inc(hLtr, 16)

            @blk.tensor
            def _(t):
                g0w(t)
                t.wait_ge(hLwg, 48)
                t.wait_ge(hLw, 80)
                for it in range(NT):
                    t.wait_ge(hLtr, 128 * (it + 1))
                    lnf = slnxf[it % 2]
                    for rc in range(24):
                        cnt = 24 * it + rc
                        if cnt >= 2:
                            t.wait_ge(hAe, cnt - 1)
                        for kk in range(8):
                            ins = t.matmul(pqkv[cnt % 2][:], sqkvw[:, kk, rc * 128:(rc + 1) * 128],
                                           lnf[:, kk, :], start=(kk == 0), stop=(kk == 7))
                        ins.then_inc(hTq, 1)
                        if rc >= 8:
                            pcnt = it * 16 + (rc - 8)
                            t.wait_ge(hAe, cnt + 1)
                            if pcnt >= 1:
                                t.wait_ge(hVr, pcnt)
                            t.matmul(pst[0][:], ssel2a[:], skf[pcnt % 2][:], start=True, stop=True)
                            t.matmul(pst[1][:], ssel2a[:], sksq[:], start=True, stop=True).then_inc(hTst, 1)
                            t.wait_ge(hVr, pcnt + 1)
                            if pcnt >= 1:
                                t.wait_ge(hVn, pcnt)
                            t.matmul(pmb[:], ssel2b[:], srows[:, 0, :], start=True, stop=True)
                            t.wait_ge(hVrc, pcnt + 1)
                            t.matmul(prb[:], ssel2b[:], srows[:, 1, :], start=True, stop=True).then_inc(hTb, 1)
                            t.wait_ge(hVn, pcnt + 1)
                            for sub in range(4):
                                tcnt = 4 * pcnt + sub
                                if tcnt >= 2:
                                    t.wait_ge(hVtr, tcnt - 1)
                                t.transpose(ptr[tcnt % 2][:], sk16[pcnt % 2][:, sub * 128:(sub + 1) * 128],
                                            sid16[:]).then_inc(hTtr, 1)
                t.wait_ge(hAe, 24 * NT)
                t.wait_ge(hVtr, 64 * NT)
                for p_i in range(NP):
                    if p_i >= 2:
                        t.wait_ge(hVkv, p_i - 1)
                    dst = pqkv[p_i % 2]
                    for tc in range(T // 128):
                        ins = t.matmul(dst[0:128, 0:128], sktm[:, tc, p_i * 128:(p_i + 1) * 128],
                                       svtm[:, tc, p_i * 128:(p_i + 1) * 128],
                                       start=(tc == 0), stop=(tc == T // 128 - 1))
                    ins.then_inc(hTkv, 1)

            @blk.scalar
            def _(s):
                g0w(s)
                for it in range(NT):
                    for rc in range(24):
                        cnt = 24 * it + rc
                        s.wait_ge(hTq, cnt + 1)
                        if rc < 8:
                            if rc % 4 == 0 and (it + rc) > 0:
                                s.wait_ge(hSq, 32 * it + (16 if rc == 4 else 0))
                            s.activation(sq16h[:, rc % 4, :], pqkv[cnt % 2][:], AF.Identity,
                                         bias=sqkvb[:, rc:rc + 1]).then_inc(hAe, 1)
                        else:
                            pcnt = it * 16 + (rc - 8)
                            if pcnt >= 1:
                                s.wait_ge(hTst, pcnt)
                                s.wait_ge(hVn, pcnt)
                            s.activation(skf[pcnt % 2][:], pqkv[cnt % 2][:], AF.Identity,
                                         bias=sqkvb[:, rc:rc + 1])
                            s.activation(sksq[:], skf[pcnt % 2][:], AF.Square).then_inc(hAe, 1)
                            s.wait_ge(hVr, pcnt + 1)
                            s.activation(srows[:, 3, :], srows[:, 2, :], AF.Sqrt, bias=seps_r[:],
                                         scale=-1.0).then_inc(hAsd, 1)

            @blk.vector
            def _(v):
                g0w(v)
                v.memset(seps_r[:], 1e-5)
                v.wait_ge(hLwg, 48)
                v.wait_ge(hLw, 80)
                for it in range(NT):
                    for rc in range(8, 24):
                        cnt = 24 * it + rc
                        pcnt = it * 16 + (rc - 8)
                        is_v = rc >= 16
                        gcol = skgb[:, 2:3] if is_v else skgb[:, 0:1]
                        bcol = skgb[:, 3:4] if is_v else skgb[:, 1:2]
                        v.wait_ge(hTst, pcnt + 1)
                        v.tensor_scalar_mul(srows[:, 0, :], pst[0][:], 1.0 / 64.0)
                        v.tensor_scalar_mul(srows[:, 1, :], pst[1][:], 1.0 / 64.0)
                        v.tensor_mul(srows[:, 2, :], srows[:, 0, :], srows[:, 0, :])
                        ins = v.tensor_sub(srows[:, 2, :], srows[:, 2, :], srows[:, 1, :])
                        ins.then_inc(hVr, 1)
                        v.wait_ge(hAsd, pcnt + 1)
                        v.reciprocal(srows[:, 1, :], srows[:, 3, :]).then_inc(hVrc, 1)
                        v.wait_ge(hTb, pcnt + 1)
                        if pcnt >= 2:
                            v.wait_ge(hTtr, 4 * (pcnt - 1))
                        v.tensor_sub(skf[pcnt % 2][:], skf[pcnt % 2][:], pmb[:])
                        v.tensor_mul(skf[pcnt % 2][:], skf[pcnt % 2][:], prb[:])
                        ins = v.tensor_scalar(sk16[pcnt % 2][:], skf[pcnt % 2][:], gcol, bcol,
                                              ALU.mult, ALU.add)
                        ins.then_inc(hVn, 1)
                        for sub in range(4):
                            tcnt = 4 * pcnt + sub
                            v.wait_ge(hTtr, tcnt + 1)
                            dst = sktm if not is_v else svtm
                            col0 = (rc - 8 if not is_v else rc - 16) * 128
                            ins = v.tensor_copy(dst[:, 4 * it + sub, col0:col0 + 128],
                                                ptr[tcnt % 2][:])
                            ins.then_inc(hVtr, 1)
                for p_i in range(NP):
                    v.wait_ge(hTkv, p_i + 1)
                    v.tensor_scalar(sksq[:, 0:128], pqkv[p_i % 2][0:128, 0:128],
                                    stemp[:, p_i:p_i + 1], 1.0 / T, ALU.mult, ALU.mult)
                    _tnop(v)
                    ins = v.tensor_add(skv16[:, p_i, :], sksq[:, 0:128], satnb[:, p_i, :])
                    ins.then_inc(hVkv, 1)

        with ExitStack() as g2:
            asb = lambda n, s, dt=F32: g2.enter_context(nc.sbuf_tensor(n, s, dt))
            aps = lambda n, s, dt=F32: g2.enter_context(nc.psum_tensor(n, s, dt))
            sp1w = asb("l1a_p1w", [128, 8, C], BF16)
            sp2w = asb("l1a_p2w", [128, 8, C], BF16)
            sp1b = asb("l1a_sp1b", [128, 8])
            sp2b = asb("l1a_p2br", [1, C], BF16)
            sones16 = asb("l1a_ones16", [1, 128], BF16)
            sq16g = [asb(f"l1a_q16g{i}", [128, 8, 512], BF16) for i in range(2)]
            slnxg = [asb(f"l1a_lnxg{i}", [128, 8, 512], BF16) for i in range(2)]
            satt2 = asb("l1a_satt2", [128, 8, 512], BF16)
            sh1 = asb("l1a_sh1", [128, 8, 512], BF16)
            sxb2 = [asb(f"l1a_sxb2{i}", [128, 4, C]) for i in range(2)]
            sx3 = [asb(f"l1a_sx3{i}", [128, 4, C]) for i in range(2)]
            pat = [aps(f"l1a_pat{i}", [128, 512]) for i in range(2)]
            pp1 = [aps(f"l1a_pp1{i}", [128, 512]) for i in range(2)]
            po = [aps(f"l1a_po{i}", [128, 512]) for i in range(2)]

            blk = g2.enter_context(nc.Block("l1a_G2", no_gpsimd_drain=True))
            G1F = [(hVkv, NP), (hSq, 32 * NT)]

            def g1w(eng):
                for sem_, val_ in G1F:
                    eng.wait_ge(sem_, val_)

            @blk.gpsimd
            def _(g):
                g1w(g)
                g.dma_start(out=sp1w[:], in_=d["l1a_p1wT"].rearrange("(k p) r -> p k r", p=128)).then_inc(iLq, 16)
                g.dma_start(out=sp2w[:], in_=d["l1a_p2wT"].rearrange("(k p) r -> p k r", p=128)).then_inc(iLq, 16)
                g.dma_start(out=sp2b[:], in_=d["l1a_p2b"]).then_inc(iLq, 16)
                for it in range(NT):
                    t0 = it * 512
                    if it >= 2:
                        g.wait_ge(iVep, 4 * (it - 1))
                    g.dma_start(out=sxb2[it % 2][:],
                                in_=xb3[t0:t0 + 512, :].rearrange("(s p) c -> p s c", p=128)
                                ).then_inc(iLxb, 16)
                    g.wait_ge(iVep, 4 * (it + 1))
                    g.dma_start(out=x_out[t0:t0 + 512, :].rearrange("(s p) c -> p s c", p=128),
                                in_=sx3[it % 2][:]).then_inc(iSx3, 16)
                g.wait_ge(iSx3, 16 * NT)

            @blk.sync
            def _(s):
                g1w(s)
                s.dma_start(out=sp1b[:], in_=d["l1a_p1b128"]).then_inc(iLqs, 16)
                for it in range(NT):
                    t0 = it * 512
                    if it >= 2:
                        s.wait_ge(iTat, 8 * (it - 1))
                        s.wait_ge(iVat, 8 * (it - 1))
                    s.dma_start(out=sq16g[it % 2][:],
                                in_=qfm3.rearrange("(k p) t -> p k t", p=128)[:, :, t0:t0 + 512]
                                ).then_inc(iLqs, 16)
                    for kk in range(8):
                        s.dma_start_transpose(slnxg[it % 2][:, kk, :],
                                              ln3cm[kk * T + t0: kk * T + t0 + 512, :]
                                              ).then_inc(iLtr, 16)

            @blk.vector
            def _(v):
                g1w(v)
                v.memset(sones16[:], 1.0)
                for it in range(NT):
                    for p_i in range(NP):
                        acnt = 8 * it + p_i
                        v.wait_ge(iTat, acnt + 1)
                        if it >= 1:
                            v.wait_ge(iTp1, 8 * it)
                        v.wait_ge(iLtr, 128 * (it + 1))
                        ins = v.tensor_add(satt2[:, p_i, :], pat[acnt % 2][:],
                                           slnxg[it % 2][:, p_i, :])
                        ins.then_inc(iVat, 1)
                    v.wait_ge(iLxb, 16 * (it + 1))
                    if it >= 2:
                        v.wait_ge(iSx3, 16 * (it - 1))
                    for sub in range(4):
                        ocnt = 4 * it + sub
                        v.wait_ge(iTp2, ocnt + 1)
                        for half in range(2):
                            ins = v.tensor_add(sx3[it % 2][:, sub, half * 512:(half + 1) * 512],
                                               po[half][:],
                                               sxb2[it % 2][:, sub, half * 512:(half + 1) * 512])
                        ins.then_inc(iVep, 1)

            @blk.tensor
            def _(t):
                g1w(t)
                t.wait_ge(iLq, 48)
                t.wait_ge(iLqs, 16)
                for it in range(NT):
                    t.wait_ge(iLqs, 16 * (it + 2))
                    for p_i in range(NP):
                        acnt = 8 * it + p_i
                        if acnt >= 2:
                            t.wait_ge(iVat, acnt - 1)
                        dst = pat[acnt % 2]
                        t.matmul(dst[0:64, :], skv16[0:64, p_i, 0:64],
                                 sq16g[it % 2][0:64, p_i, :], start=True, stop=True)
                        t.matmul(dst[64:128, :], skv16[64:128, p_i, 64:128],
                                 sq16g[it % 2][64:128, p_i, :], start=True, stop=True,
                                 tile_position=(64, 64)).then_inc(iTat, 1)
                    for rc in range(8):
                        pcnt = 8 * it + rc
                        t.wait_ge(iVat, 8 * (it + 1))
                        if pcnt >= 2:
                            t.wait_ge(iAh, pcnt - 1)
                        for kk in range(8):
                            ins = t.matmul(pp1[pcnt % 2][:], sp1w[:, kk, rc * 128:(rc + 1) * 128],
                                           satt2[:, kk, :], start=(kk == 0), stop=(kk == 7))
                        ins.then_inc(iTp1, 1)
                    for sub in range(4):
                        ocnt = 4 * it + sub
                        t.wait_ge(iAh, 8 * (it + 1))
                        if ocnt >= 1:
                            t.wait_ge(iVep, ocnt)
                        for half in range(2):
                            t.matmul(po[half][:], sones16[:], sp2b[:, half * 512:(half + 1) * 512],
                                     start=True, stop=False)
                            for kk in range(8):
                                ins = t.matmul(po[half][:], sh1[:, kk, sub * 128:(sub + 1) * 128],
                                               sp2w[:, kk, half * 512:(half + 1) * 512],
                                               start=False, stop=(kk == 7))
                        ins.then_inc(iTp2, 1)

            @blk.scalar
            def _(s):
                g1w(s)
                for it in range(NT):
                    for rc in range(8):
                        pcnt = 8 * it + rc
                        s.wait_ge(iTp1, pcnt + 1)
                        s.activation(sh1[:, rc, :], pp1[pcnt % 2][:], AF.Gelu,
                                     bias=sp1b[:, rc:rc + 1]).then_inc(iAh, 1)
    return [(iSx3, 16 * NT)]


def phase_l1_ffn(env, x_in, x_out, fence):
    nc = env.nc
    d = env.d
    NT = T // 512
    NP = 8
    E = 8
    NI = T // 512

    lnacm = nc.dram_tensor("l1f_lnacm", [8 * T, 128], BF16, kind="Internal").ap()
    lnbcm = nc.dram_tensor("l1f_lnbcm", [8 * T, 128], BF16, kind="Internal").ap()
    q4fm = nc.dram_tensor("l1f_q4fm", [C, T], BF16, kind="Internal").ap()
    f4 = nc.dram_tensor("l1f_f4", [T, C], F32, kind="Internal").ap()

    S = env.sem
    jLx = S("l1f_jLx"); jAsq = S("l1f_jAsq"); jVst = S("l1f_jVst")
    jAstd = S("l1f_jAstd"); jV2 = S("l1f_jV2"); jA2s = S("l1f_jA2s")
    jVf = S("l1f_jVf"); jVln = S("l1f_jVln"); jSln = S("l1f_jSln"); jSf4 = S("l1f_jSf4")
    kLtr = S("l1f_kLtr"); kLw = S("l1f_kLw"); kTq = S("l1f_kTq")
    kAe = S("l1f_kAe"); kSq = S("l1f_kSq"); kTtr = S("l1f_kTtr"); kVtr = S("l1f_kVtr")
    lSq = S("l1f_lSq"); lTs = S("l1f_lTs"); lAe = S("l1f_lAe")
    lVmk = S("l1f_lVmk"); lTav = S("l1f_lTav"); lVr0 = S("l1f_lVr0")
    lVrr = S("l1f_lVrr")
    lTbr = S("l1f_lTbr"); lVav = S("l1f_lVav"); lTcp = S("l1f_lTcp")
    lVep = S("l1f_lVep"); lLf4 = S("l1f_lLf4"); lSa = S("l1f_lSa")
    lLw = S("l1f_lLw"); lLwg = S("l1f_lLwg")
    nLa = S("l1f_nLa"); nLx3 = S("l1f_nLx3"); nAsq = S("l1f_nAsq")
    nVst = S("l1f_nVst"); nAstd = S("l1f_nAstd"); nVln = S("l1f_nVln")
    nSlb = S("l1f_nSlb"); nTtr = S("l1f_nTtr"); nVtr = S("l1f_nVtr")
    nTlg = S("l1f_nTlg"); nAex = S("l1f_nAex"); nVrt = S("l1f_nVrt")
    nVxa = S("l1f_nVxa"); nSxa = S("l1f_nSxa"); nLw = S("l1f_nLw"); nLwg = S("l1f_nLwg")
    mLw = S("l1f_mLw"); mTh = S("l1f_mTh"); mAh = S("l1f_mAh")
    mTy = S("l1f_mTy"); mVy = S("l1f_mVy"); mLf2 = S("l1f_mLf2")
    mLxa = S("l1f_mLxa"); mSxa = S("l1f_mSxa")

    def fw(eng):
        for sem_, val_ in fence:
            eng.wait_ge(sem_, val_)

    with ExitStack() as ph:
        psb = lambda n, s, dt=F32: ph.enter_context(nc.sbuf_tensor(n, s, dt))
        seps4 = psb("l1f_eps", [128, 1])
        sroute4 = psb("l1f_route", [128, T // 128, E])

        with ExitStack() as h0:
            asb = lambda n, s, dt=F32: h0.enter_context(nc.sbuf_tensor(n, s, dt))
            sx = [asb(f"l1f_sx{i}", [128, 4, C]) for i in range(2)]
            sfeat = [asb(f"l1f_sft{i}", [128, 4, C]) for i in range(2)]
            sln16 = [asb(f"l1f_sln16{i}", [128, 4, C], BF16) for i in range(2)]
            sln1b = asb("l1f_sln1b", [128, C])
            sqact = asb("l1f_sqact", [128, C])
            sqd = asb("l1f_sqd", [128, C])
            sstat = asb("l1f_sstat", [128, 28])
            sstd = asb("l1f_sstd", [128, 4])
            ssum = sstat[:, 0:4]; ssq = sstat[:, 4:8]; smc = sstat[:, 8:12]
            se2 = sstat[:, 12:16]; svar = sstat[:, 16:20]; srstd = sstat[:, 20:24]

            blk = h0.enter_context(nc.Block("l1f_H0", no_gpsimd_drain=True))

            @blk.sync
            def _(s):
                fw(s)
                s.dma_start(out=sln1b[:], in_=d["l1f_ln1b"]).then_inc(jSln, 16)
                for it in range(NT):
                    t0 = it * 512
                    if it >= 2:
                        s.wait_ge(jVln, it - 1)
                        s.wait_ge(jAsq, it - 1)
                    s.dma_start(out=sx[it % 2][:],
                                in_=x_in[t0:t0 + 512, :].rearrange("(s p) c -> p s c", p=128)
                                ).then_inc(jLx, 16)
                    s.wait_ge(jVln, it + 1)
                    for kk in range(8):
                        s.dma_start(out=lnacm[kk * T + t0: kk * T + t0 + 512, :].rearrange("(s2 p) c -> p s2 c", p=128),
                                    in_=sln16[it % 2][:, :, kk * 128:(kk + 1) * 128]
                                    ).then_inc(jSln, 16)

            @blk.gpsimd
            def _(g):
                fw(g)
                for it in range(NT):
                    t0 = it * 512
                    g.wait_ge(jVf, it + 1)
                    g.dma_start(out=f4[t0:t0 + 512, :].rearrange("(s p) c -> p s c", p=128),
                                in_=sfeat[it % 2][:]).then_inc(jSf4, 16)

            @blk.scalar
            def _(s):
                fw(s)
                for it in range(NT):
                    s.wait_ge(jLx, 16 * (it + 1))
                    if it >= 1:
                        s.wait_ge(jVst, it)
                    for ss in range(4):
                        ins = s.activation(sqact[:], sx[it % 2][:, ss, :], AF.Square,
                                           accum_out=ssq[:, ss:ss + 1])
                    ins.then_inc(jAsq, 1)
                    s.wait_ge(jVst, it + 1)
                    s.activation(sstd[:], svar[:], AF.Sqrt, bias=seps4[:], scale=-1.0).then_inc(jAstd, 1)
                    s.wait_ge(jV2, it + 1)
                    s.activation(sstd[:], svar[:], AF.Sqrt, bias=seps4[:], scale=-1.0).then_inc(jA2s, 1)

            @blk.vector
            def _(v):
                fw(v)
                v.memset(seps4[:], 1e-5)
                for it in range(NT):
                    xb = sx[it % 2]
                    fb = sfeat[it % 2]
                    v.wait_ge(jLx, 16 * (it + 1))
                    if it >= 2:
                        v.wait_ge(jSf4, 16 * (it - 1))
                        v.wait_ge(jSln, 16 + 128 * (it - 1))
                    for ss in range(4):
                        v.reduce_sum(ssum[:, ss:ss + 1], xb[:, ss, :], axis=AX)
                    v.wait_ge(jAsq, it + 1)
                    v.tensor_scalar_mul(smc[:], ssum[:], 1.0 / C)
                    v.tensor_scalar_mul(se2[:], ssq[:], 1.0 / C)
                    _tnop(v)
                    for ss in range(4):
                        ins = v.scalar_tensor_tensor(svar[:, ss:ss + 1], smc[:, ss:ss + 1],
                                                     smc[:, ss:ss + 1], se2[:, ss:ss + 1],
                                                     ALU.mult, ALU.subtract)
                    ins.then_inc(jVst, 1)
                    v.wait_ge(jAstd, it + 1)
                    v.reciprocal(srstd[:], sstd[:])
                    _tnop(v)
                    for ss in range(4):
                        ins = v.tensor_scalar(fb[:, ss, :], xb[:, ss, :],
                                              smc[:, ss:ss + 1], srstd[:, ss:ss + 1],
                                              ALU.subtract, ALU.mult)
                    ins.then_inc(jVf, 1)
                    for ss in range(4):
                        v.reduce_sum(ssum[:, ss:ss + 1], fb[:, ss, :], axis=AX)
                        v.tensor_mul(sqd[:], fb[:, ss, :], fb[:, ss, :])
                        v.reduce_sum(ssq[:, ss:ss + 1], sqd[:], axis=AX)
                    v.tensor_scalar_mul(smc[:], ssum[:], 1.0 / C)
                    v.tensor_scalar_mul(se2[:], ssq[:], 1.0 / C)
                    _tnop(v)
                    for ss in range(4):
                        ins = v.scalar_tensor_tensor(svar[:, ss:ss + 1], smc[:, ss:ss + 1],
                                                     smc[:, ss:ss + 1], se2[:, ss:ss + 1],
                                                     ALU.mult, ALU.subtract)
                    ins.then_inc(jV2, 1)
                    v.wait_ge(jA2s, it + 1)
                    v.reciprocal(srstd[:], sstd[:])
                    _tnop(v)
                    v.wait_ge(jSln, 16)
                    for ss in range(4):
                        v.tensor_scalar(sqd[:], fb[:, ss, :],
                                        smc[:, ss:ss + 1], srstd[:, ss:ss + 1],
                                        ALU.subtract, ALU.mult)
                        ins = v.tensor_mul(sln16[it % 2][:, ss, :], sqd[:], sln1b[:])
                    ins.then_inc(jVln, 1)

        with ExitStack() as h1o:
            osb = lambda n, s, dt=F32: h1o.enter_context(nc.sbuf_tensor(n, s, dt))
            sk_fm = osb("l1f_skfm", [128, 8, T], BF16)
            sv_ext = osb("l1f_svext", [128, T // 128, 16, 65], BF16)
            ssel2b4 = osb("l1f_sel2b", [2, 128])
            sid16 = osb("l1f_id16", [128, 128], BF16)

            with ExitStack() as h1a:
                asb = lambda n, s, dt=F32: h1a.enter_context(nc.sbuf_tensor(n, s, dt))
                aps = lambda n, s, dt=F32: h1a.enter_context(nc.psum_tensor(n, s, dt))
                scattn = asb("l1f_scattn", [128, 8, 3072], BF16)
                slnxf = [asb("l1f_lnxf", [128, 8, 512], BF16)] * 2
                sq16h = asb("l1f_sq16", [128, 4, 512], BF16)
                sv16 = [asb(f"l1f_sv16{i}", [128, 512], BF16) for i in range(2)]
                pqkv = [aps(f"l1f_pqkv{i}", [128, 512]) for i in range(2)]
                ptr = [aps(f"l1f_ptr{i}", [128, 128], BF16) for i in range(2)]

                blk = h1a.enter_context(nc.Block("l1f_H1a", no_gpsimd_drain=True))
                H0F = [(jSln, 16 + 128 * NT), (jSf4, 16 * NT)]

                def h0w(eng):
                    for sem_, val_ in H0F:
                        eng.wait_ge(sem_, val_)

                @blk.gpsimd
                def _(g):
                    h0w(g)
                    g.dma_start(out=scattn[:], in_=d["l1f_cattn"].rearrange("(k p) r -> p k r", p=128)).then_inc(kLw, 16)
                    g.dma_start(out=sid16[:], in_=d["idf32"]).then_inc(kLw, 16)
                    g.dma_start(out=ssel2b4[:], in_=d["l1a_sel2b"]).then_inc(kLw, 16)
                    for it in range(NT):
                        t0 = it * 512
                        for hf in range(2):
                            g.wait_ge(kAe, 24 * it + 4 * (hf + 1))
                            g.dma_start(out=q4fm.rearrange("(k p) t -> p k t", p=128)[:, 4 * hf:4 * (hf + 1), t0:t0 + 512],
                                        in_=sq16h[:]).then_inc(kSq, 16)

                @blk.sync
                def _(s):
                    h0w(s)
                    for it in range(NT):
                        t0 = it * 512
                        if it >= 1:
                            s.wait_ge(kTq, 24 * it)
                        for kk in range(8):
                            s.dma_start_transpose(slnxf[it % 2][:, kk, :],
                                                  lnacm[kk * T + t0: kk * T + t0 + 512, :]
                                                  ).then_inc(kLtr, 16)

                @blk.tensor
                def _(t):
                    h0w(t)
                    t.wait_ge(kLw, 48)
                    for it in range(NT):
                        t.wait_ge(kLtr, 128 * (it + 1))
                        lnf = slnxf[it % 2]
                        for rc in range(24):
                            cnt = 24 * it + rc
                            if cnt >= 2:
                                t.wait_ge(kAe, cnt - 1)
                            for kk in range(8):
                                ins = t.matmul(pqkv[cnt % 2][:], scattn[:, kk, rc * 128:(rc + 1) * 128],
                                               lnf[:, kk, :], start=(kk == 0), stop=(kk == 7))
                            ins.then_inc(kTq, 1)
                            if rc >= 16:
                                pcnt = it * 8 + (rc - 16)
                                t.wait_ge(kAe, cnt + 1)
                                for sub in range(4):
                                    tcnt = 4 * pcnt + sub
                                    if tcnt >= 2:
                                        t.wait_ge(kVtr, tcnt - 1)
                                    t.transpose(ptr[tcnt % 2][:], sv16[pcnt % 2][:, sub * 128:(sub + 1) * 128],
                                                sid16[:]).then_inc(kTtr, 1)

                @blk.scalar
                def _(s):
                    h0w(s)
                    for it in range(NT):
                        for rc in range(24):
                            cnt = 24 * it + rc
                            s.wait_ge(kTq, cnt + 1)
                            if rc < 8:
                                if rc % 4 == 0 and (it + rc) > 0:
                                    s.wait_ge(kSq, 32 * it + (16 if rc == 4 else 0))
                                s.copy(sq16h[:, rc % 4, :], pqkv[cnt % 2][:]).then_inc(kAe, 1)
                            elif rc < 16:
                                s.copy(sk_fm[:, rc - 8, it * 512:(it + 1) * 512],
                                       pqkv[cnt % 2][:]).then_inc(kAe, 1)
                            else:
                                pcnt = it * 8 + (rc - 16)
                                if pcnt >= 2:
                                    s.wait_ge(kTtr, 4 * (pcnt - 1))
                                s.copy(sv16[pcnt % 2][:], pqkv[cnt % 2][:]).then_inc(kAe, 1)

                @blk.vector
                def _(v):
                    h0w(v)
                    v.memset(sv_ext[:], 1.0)
                    for it in range(NT):
                        for pp in range(8):
                            pcnt = it * 8 + pp
                            for sub in range(4):
                                tcnt = 4 * pcnt + sub
                                v.wait_ge(kTtr, tcnt + 1)
                                ins = v.tensor_copy(
                                    sv_ext[:, 4 * it + sub, 2 * pp:2 * pp + 2, 0:64],
                                    ptr[tcnt % 2][:].rearrange("p (h e) -> p h e", h=2))
                                ins.then_inc(kVtr, 1)

            with ExitStack() as h1b:
                asb = lambda n, s, dt=F32: h1b.enter_context(nc.sbuf_tensor(n, s, dt))
                aps = lambda n, s, dt=F32: h1b.enter_context(nc.psum_tensor(n, s, dt))
                scproj = asb("l1f_scproj", [128, 8, C], BF16)
                smask = asb("l1f_smasks", [128, 4, 1024], BF16)
                sq4 = [asb("l1f_sq4", [128, 8, 512], BF16)] * 2
                spt = asb("l1f_spt", [128, 8, 512], BF16)
                sew = asb("l1f_sew", [128, 2, 1024], BF16)
                savp = asb("l1f_savp", [128, 512])
                srr0 = asb("l1f_srr0", [1, 512])
                srr1 = asb("l1f_srr1", [1, 512])
                sselB = asb("l1f_selB", [1, 128])
                savn = asb("l1f_savn", [128, 8, 512], BF16)
                sf4t = [asb("l1f_sf4t", [128, 4, C])] * 2
                psS = [aps(f"l1f_psS{i}", [128, 1024]) for i in range(2)]
                pav = [aps(f"l1f_pav{i}", [128, 512]) for i in range(2)]
                pbr = aps("l1f_pbr", [128, 512])
                pcp = aps("l1f_pcp", [128, 512])

                blk = h1b.enter_context(nc.Block("l1f_H1b", no_gpsimd_drain=True))
                H1AF = [(kSq, 32 * NT), (kVtr, 32 * NT), (kAe, 24 * NT)]

                def h1aw(eng):
                    for sem_, val_ in H1AF:
                        eng.wait_ge(sem_, val_)

                njts = [4 * (I + 1) for I in range(NI)]
                gcum = []
                gtot = 0
                for I in range(NI):
                    row = []
                    for p in range(NP):
                        row.append(gtot)
                        gtot += njts[I]
                    gcum.append(row)
                PGT = NI * NP
                SUBT = 4 * NI

                @blk.sync
                def _(s):
                    h1aw(s)
                    s.dma_start(out=sselB[:], in_=d["l1a_sel2b"][1:2, :]).then_inc(lLw, 16)
                    for I in range(NI):
                        if I >= 1:
                            s.wait_ge(lTs, gcum[I - 1][0] + njts[I - 1] * NP)
                        s.dma_start(out=sq4[I % 2][:],
                                    in_=q4fm.rearrange("(k p) t -> p k t", p=128)[:, :, I * 512:(I + 1) * 512]
                                    ).then_inc(lSq, 16)

                @blk.gpsimd
                def _(g):
                    h1aw(g)
                    g.dma_start(out=scproj[:], in_=d["l1f_cproj"].rearrange("(k p) r -> p k r", p=128)).then_inc(lLwg, 16)
                    g.dma_start(out=smask[:], in_=d["l1f_masks"].rearrange("p (s c) -> p s c", s=4)).then_inc(lLwg, 16)
                    for I in range(NI):
                        t0 = I * 512
                        if I >= 1:
                            g.wait_ge(lSa, 16 * I)
                        g.dma_start(out=sf4t[I % 2][:],
                                    in_=f4[t0:t0 + 512, :].rearrange("(s p) c -> p s c", p=128)
                                    ).then_inc(lLf4, 16)
                        g.wait_ge(lVep, 4 * (I + 1))
                        g.dma_start(out=f4[t0:t0 + 512, :].rearrange("(s p) c -> p s c", p=128),
                                    in_=sf4t[I % 2][:]).then_inc(lSa, 16)
                    g.wait_ge(lSa, 16 * NI)

                @blk.tensor
                def _(t):
                    h1aw(t)
                    t.wait_ge(lLw, 16)
                    t.wait_ge(lLwg, 32)
                    pend = []

                    def emit_pbr(t, pgx):
                        t.wait_ge(lVrr, pgx + 1)
                        t.matmul(pbr[:], ssel2b4[0:1, :], srr0[:], start=True, stop=False)
                        t.matmul(pbr[:], sselB[:], srr1[:], start=False, stop=True).then_inc(lTbr, 1)

                    for I in range(NI):
                        t.wait_ge(lSq, 16 * (I + 1))
                        njt = njts[I]
                        for p in range(NP):
                            pg = 8 * I + p
                            g0 = gcum[I][p]

                            def do_av(jt):
                                g_i = g0 + jt
                                t.wait_ge(lAe, g_i + 1)
                                if jt >= 4 * I:
                                    t.wait_ge(lVmk, 8 * pg + 2 * (jt - 4 * I + 1))
                                if jt == 0 and pg >= 1:
                                    t.wait_ge(lVr0, pg)
                                t.matmul(pav[0][0:65, :], sv_ext[:, jt, 2 * p, :],
                                         spt[:, 2 * (g_i % 4), :], start=(jt == 0), stop=(jt == njt - 1))
                                t.matmul(pav[1][0:65, :], sv_ext[:, jt, 2 * p + 1, :],
                                         spt[:, 2 * (g_i % 4) + 1, :],
                                         start=(jt == 0), stop=(jt == njt - 1)).then_inc(lTav, 1)

                            for jt in range(njt):
                                g_i = g0 + jt
                                if g_i >= 2:
                                    t.wait_ge(lAe, g_i - 1)
                                t.matmul(psS[g_i % 2][:, 0:512], sk_fm[0:64, p, jt * 128:(jt + 1) * 128],
                                         sq4[I % 2][0:64, p, :], start=True, stop=True)
                                t.matmul(psS[g_i % 2][:, 512:1024], sk_fm[64:128, p, jt * 128:(jt + 1) * 128],
                                         sq4[I % 2][64:128, p, :], start=True, stop=True,
                                         tile_position=(64, 0)).then_inc(lTs, 1)
                                if jt >= 1:
                                    do_av(jt - 1)
                                if jt == 1 and pend:
                                    emit_pbr(t, pend.pop())
                            do_av(njt - 1)
                            pend.append(pg)
                        while pend:
                            emit_pbr(t, pend.pop())
                        t.wait_ge(lVav, 8 * (I + 1))
                        for sub in range(4):
                            scnt = 4 * I + sub
                            if scnt >= 1:
                                t.wait_ge(lVep, scnt)
                            for half in range(2):
                                for p in range(NP):
                                    ins = t.matmul(pcp[:] if half == 0 else pbr[:],
                                                   savn[:, p, sub * 128:(sub + 1) * 128],
                                                   scproj[:, p, half * 512:(half + 1) * 512],
                                                   start=(p == 0), stop=(p == 7))
                            ins.then_inc(lTcp, 1)

                @blk.scalar
                def _(s):
                    h1aw(s)
                    for I in range(NI):
                        njt = njts[I]
                        for p in range(NP):
                            pg = 8 * I + p
                            g0 = gcum[I][p]
                            for jt in range(njt):
                                g_i = g0 + jt
                                s.wait_ge(lTs, g_i + 1)
                                if g_i >= 4:
                                    s.wait_ge(lTav, g_i - 3)
                                diag = jt >= 4 * I
                                if diag:
                                    if jt - 4 * I >= 2:
                                        s.wait_ge(lVmk, 8 * (8 * I + p) + 2 * (jt - 4 * I - 1))
                                    ins = s.activation(sew[:, g_i % 2, :], psS[g_i % 2][:], AF.Exp, scale=0.125)
                                else:
                                    ins = s.activation(spt[:, 2 * (g_i % 4):2 * (g_i % 4) + 2, :],
                                                       psS[g_i % 2][:], AF.Exp, scale=0.125)
                                ins.then_inc(lAe, 1)

                @blk.vector
                def _(v):
                    h1aw(v)
                    v.wait_ge(lLwg, 32)
                    pendv = []

                    def emit_savn(v, pp_, ppg_):
                        v.wait_ge(lTbr, ppg_ + 1)
                        ins = v.tensor_mul(savn[:, pp_, :], savp[:], pbr[:])
                        ins.then_inc(lVav, 1)

                    for I in range(NI):
                        njt = njts[I]
                        for p in range(NP):
                            g0 = gcum[I][p]
                            for jt in range(njt):
                                g_i = g0 + jt
                                if jt >= 4 * I:
                                    v.wait_ge(lAe, g_i + 1)
                                    mk = smask[:, jt - 4 * I, :].rearrange("p (h c) -> p h c", h=2)
                                    ins = v.tensor_mul(spt[:, 2 * (g_i % 4):2 * (g_i % 4) + 2, :],
                                                       sew[:, g_i % 2, :].rearrange("p (h c) -> p h c", h=2), mk)
                                    ins.then_inc(lVmk, 2)
                                if jt == 1 and pendv:
                                    emit_savn(v, *pendv.pop())
                            pg = 8 * I + p
                            v.wait_ge(lTav, g0 + njt)
                            v.tensor_copy(savp[0:64, :], pav[0][0:64, :])
                            v.tensor_copy(savp[64:128, :], pav[1][0:64, :])
                            v.tensor_copy(srr0[:], pav[0][64:65, :])
                            ins = v.tensor_copy(srr1[:], pav[1][64:65, :])
                            ins.then_inc(lVr0, 1)
                            _tnop(v)
                            v.reciprocal(srr0[:], srr0[:])
                            ins = v.reciprocal(srr1[:], srr1[:])
                            ins.then_inc(lVrr, 1)
                            pendv.append((p, pg))
                        while pendv:
                            emit_savn(v, *pendv.pop())
                        v.wait_ge(lLf4, 16 * (I + 1))
                        for sub in range(4):
                            scnt = 4 * I + sub
                            v.wait_ge(lTcp, scnt + 1)
                            v.tensor_add(sf4t[I % 2][:, sub, 0:512], pcp[:],
                                         sf4t[I % 2][:, sub, 0:512])
                            ins = v.tensor_add(sf4t[I % 2][:, sub, 512:1024], pbr[:],
                                               sf4t[I % 2][:, sub, 512:1024])
                            ins.then_inc(lVep, 1)
        _phase_l1_ffn_tail(env, x_in, x_out, f4, lnbcm, sroute4, seps4,
                           [(lSa, 16 * NI)],
                           dict(nLa=nLa, nLx3=nLx3, nAsq=nAsq, nVst=nVst, nAstd=nAstd,
                                nVln=nVln, nSlb=nSlb, nTtr=nTtr, nVtr=nVtr, nTlg=nTlg,
                                nAex=nAex, nVrt=nVrt, nVxa=nVxa, nSxa=nSxa, nLw=nLw, nLwg=nLwg,
                                mLw=mLw, mTh=mTh, mAh=mAh, mTy=mTy, mVy=mVy,
                                mLf2=mLf2, mLxa=mLxa, mSxa=mSxa))
    return [(mSxa, 16 * (T // 256) * E)]


def _phase_l1_ffn_tail(env, x_in, x_out, f4, lnbcm, sroute4, seps4, fence, sm):
    nc = env.nc
    d = env.d
    NT = T // 512
    E = 8
    NIT = T // 256
    nLa = sm["nLa"]; nLx3 = sm["nLx3"]; nAsq = sm["nAsq"]; nVst = sm["nVst"]
    nAstd = sm["nAstd"]; nVln = sm["nVln"]; nSlb = sm["nSlb"]; nTtr = sm["nTtr"]
    nVtr = sm["nVtr"]; nTlg = sm["nTlg"]; nAex = sm["nAex"]; nVrt = sm["nVrt"]
    nVxa = sm["nVxa"]; nSxa = sm["nSxa"]; nLw = sm["nLw"]; nLwg = sm["nLwg"]
    mLw = sm["mLw"]; mTh = sm["mTh"]; mAh = sm["mAh"]; mTy = sm["mTy"]
    mVy = sm["mVy"]; mLf2 = sm["mLf2"]; mLxa = sm["mLxa"]; mSxa = sm["mSxa"]

    with ExitStack() as h2:
        asb = lambda n, s, dt=F32: h2.enter_context(nc.sbuf_tensor(n, s, dt))
        aps = lambda n, s, dt=F32: h2.enter_context(nc.psum_tensor(n, s, dt))
        sa = [asb(f"l1f_h2a{i}", [128, 4, C]) for i in range(2)]
        sx3t = [asb(f"l1f_h2x{i}", [128, 4, C]) for i in range(2)]
        slnb = [asb(f"l1f_h2l{i}", [128, 4, C], BF16) for i in range(2)]
        sxai = [asb(f"l1f_h2xa{i}", [128, 4, C]) for i in range(2)]
        sln2b = asb("l1f_sln2b", [128, C])
        srw16 = asb("l1f_srw16", [128, 8, E], BF16)
        sid16b = asb("l1f_id16b", [128, 128], BF16)
        strx = [asb(f"l1f_h2trx{i}", [128, 128], BF16) for i in range(2)]
        sqact = asb("l1f_h2sq", [128, C])
        sqd = asb("l1f_h2sqd", [128, C])
        sstat = asb("l1f_h2st", [128, 28])
        sstd = asb("l1f_h2sd", [128, 4])
        slg = asb("l1f_h2lg", [128, E])
        srr = asb("l1f_h2rr", [128, 8])
        swm = asb("l1f_h2wm", [128, 2, E])
        ptrb = [aps(f"l1f_h2ptr{i}", [128, 128], BF16) for i in range(2)]
        plg = aps("l1f_h2plg", [128, E])
        ssum = sstat[:, 0:4]; ssq = sstat[:, 4:8]; smc = sstat[:, 8:12]
        se2 = sstat[:, 12:16]; svar = sstat[:, 16:20]; srstd = sstat[:, 20:24]

        blk = h2.enter_context(nc.Block("l1f_H2", no_gpsimd_drain=True))

        def fw2(eng):
            for sem_, val_ in fence:
                eng.wait_ge(sem_, val_)

        @blk.sync
        def _(s):
            fw2(s)
            s.dma_start(out=sln2b[:], in_=d["l1f_ln2b"]).then_inc(nLw, 16)
            for it in range(NT):
                t0 = it * 512
                if it >= 2:
                    s.wait_ge(nVln, it - 1)
                    s.wait_ge(nAsq, it - 1)
                s.dma_start(out=sa[it % 2][:],
                            in_=f4[t0:t0 + 512, :].rearrange("(s p) c -> p s c", p=128)
                            ).then_inc(nLa, 16)
                s.wait_ge(nVln, it + 1)
                for kk in range(8):
                    s.dma_start(out=lnbcm[kk * T + t0: kk * T + t0 + 512, :].rearrange("(s2 p) c -> p s2 c", p=128),
                                in_=slnb[it % 2][:, :, kk * 128:(kk + 1) * 128]
                                ).then_inc(nSlb, 16)

        @blk.gpsimd
        def _(g):
            fw2(g)
            g.dma_start(out=srw16[:], in_=d["l1f_router"].rearrange("(k p) e -> p k e", p=128),
                        allow_slow_non_contiguous=True).then_inc(nLwg, 16)
            g.dma_start(out=sid16b[:], in_=d["idf32"]).then_inc(nLwg, 16)
            for it in range(NT):
                t0 = it * 512
                if it >= 2:
                    g.wait_ge(nVxa, it - 1)
                g.dma_start(out=sx3t[it % 2][:],
                            in_=x_in[t0:t0 + 512, :].rearrange("(s p) c -> p s c", p=128)
                            ).then_inc(nLx3, 16)
                g.wait_ge(nVxa, it + 1)
                g.dma_start(out=x_out[t0:t0 + 512, :].rearrange("(s p) c -> p s c", p=128),
                            in_=sxai[it % 2][:]).then_inc(nSxa, 16)

        @blk.scalar
        def _(s):
            fw2(s)
            for it in range(NT):
                s.wait_ge(nLa, 16 * (it + 1))
                if it >= 1:
                    s.wait_ge(nVst, it)
                for ss in range(4):
                    ins = s.activation(sqact[:], sa[it % 2][:, ss, :], AF.Square,
                                       accum_out=ssq[:, ss:ss + 1])
                ins.then_inc(nAsq, 1)
                s.wait_ge(nVst, it + 1)
                s.activation(sstd[:], svar[:], AF.Sqrt, bias=seps4[:], scale=-1.0).then_inc(nAstd, 1)
                for su in range(4):
                    gidx = 4 * it + su
                    s.wait_ge(nVrt, 3 * gidx + 1)
                    s.activation(srr[:, 3:4], srr[:, 1:2], AF.Exp,
                                 bias=srr[:, 2:3]).then_inc(nAex, 1)

        @blk.vector
        def _(v):
            fw2(v)
            for it in range(NT):
                ab = sa[it % 2]
                v.wait_ge(nLa, 16 * (it + 1))
                if it >= 2:
                    v.wait_ge(nSlb, 128 * (it - 1))
                    v.wait_ge(nSxa, 16 * (it - 1))
                    v.wait_ge(nTlg, 4 * (it - 1))
                for ss in range(4):
                    v.reduce_sum(ssum[:, ss:ss + 1], ab[:, ss, :], axis=AX)
                v.wait_ge(nAsq, it + 1)
                v.tensor_scalar_mul(smc[:], ssum[:], 1.0 / C)
                v.tensor_scalar_mul(se2[:], ssq[:], 1.0 / C)
                _tnop(v)
                for ss in range(4):
                    ins = v.scalar_tensor_tensor(svar[:, ss:ss + 1], smc[:, ss:ss + 1],
                                                 smc[:, ss:ss + 1], se2[:, ss:ss + 1],
                                                 ALU.mult, ALU.subtract)
                ins.then_inc(nVst, 1)
                v.wait_ge(nAstd, it + 1)
                v.reciprocal(srstd[:], sstd[:])
                _tnop(v)
                v.wait_ge(nLw, 16)
                for ss in range(4):
                    v.tensor_scalar(sqd[:], ab[:, ss, :],
                                    smc[:, ss:ss + 1], srstd[:, ss:ss + 1],
                                    ALU.subtract, ALU.mult)
                    ins = v.tensor_mul(slnb[it % 2][:, ss, :], sqd[:], sln2b[:])
                ins.then_inc(nVln, 1)
                v.wait_ge(nLx3, 16 * (it + 1))
                for ss in range(4):
                    ins = v.tensor_add(sxai[it % 2][:, ss, :], sx3t[it % 2][:, ss, :],
                                       ab[:, ss, :])
                ins.then_inc(nVxa, 1)
                for su in range(4):
                    gidx = 4 * it + su
                    for kk in range(8):
                        tcnt = 8 * gidx + kk
                        v.wait_ge(nTtr, tcnt + 1)
                        v.tensor_copy(strx[tcnt % 2][:], ptrb[tcnt % 2][:]).then_inc(nVtr, 1)
                    v.wait_ge(nTlg, gidx + 1)
                    v.tensor_copy(slg[:], plg[:])
                    _tnop(v)
                    v.reduce_max(srr[:, 0:1], slg[:], axis=AX)
                    _tnop(v)
                    v.tensor_scalar(swm[:, 0, :], slg[:], srr[:, 0:1], None, ALU.is_equal)
                    v.tensor_scalar(srr[:, 2:3], srr[:, 0:1], -1.0, None, ALU.mult)
                    _tnop(v)
                    v.scalar_tensor_tensor(sqd[:, 0:E], swm[:, 0, :], -1e30, slg[:],
                                           ALU.mult, ALU.add)
                    _tnop(v)
                    ins = v.reduce_max(srr[:, 1:2], sqd[:, 0:E], axis=AX)
                    ins.then_inc(nVrt, 1)
                    _tnop(v)
                    v.tensor_scalar(swm[:, 1, :], sqd[:, 0:E], srr[:, 1:2], None, ALU.is_equal)
                    v.wait_ge(nAex, gidx + 1)
                    v.tensor_scalar(srr[:, 4:5], srr[:, 3:4], 1.0, None, ALU.add)
                    _tnop(v)
                    v.reciprocal(srr[:, 5:6], srr[:, 4:5])
                    _tnop(v)
                    v.tensor_mul(srr[:, 6:7], srr[:, 3:4], srr[:, 5:6])
                    _tnop(v)
                    v.tensor_scalar(sqd[:, 0:E], swm[:, 0, :], srr[:, 5:6], None, ALU.mult)
                    _tnop(v)
                    ins = v.scalar_tensor_tensor(sroute4[:, gidx, :], swm[:, 1, :],
                                                 srr[:, 6:7], sqd[:, 0:E],
                                                 ALU.mult, ALU.add)
                    ins.then_inc(nVrt, 2)

        @blk.tensor
        def _(t):
            fw2(t)
            t.wait_ge(nLwg, 32)
            for it in range(NT):
                t.wait_ge(nVln, it + 1)
                for su in range(4):
                    gidx = 4 * it + su
                    for kk in range(8):
                        tcnt = 8 * gidx + kk
                        if tcnt >= 2:
                            t.wait_ge(nVtr, tcnt - 1)
                        t.transpose(ptrb[tcnt % 2][:], slnb[it % 2][:, su, kk * 128:(kk + 1) * 128],
                                    sid16b[:]).then_inc(nTtr, 1)
                        t.wait_ge(nVtr, tcnt + 1)
                        ins = t.matmul(plg[:], strx[tcnt % 2][:], srw16[:, kk, :],
                                       start=(kk == 0), stop=(kk == 7))
                    ins.then_inc(nTlg, 1)


    with ExitStack() as h3:
        asb = lambda n, s, dt=F32: h3.enter_context(nc.sbuf_tensor(n, s, dt))
        aps = lambda n, s, dt=F32: h3.enter_context(nc.psum_tensor(n, s, dt))
        sW1m = [asb(f"l1f_w1m{i}", [128, 8, 2048], BF16) for i in range(2)]
        sW2m = [asb(f"l1f_w2m{i}", [128, 16, C], BF16) for i in range(2)]
        sh = asb("l1f_h3h", [128, 16, 256], BF16)
        sf2t = [asb(f"l1f_h3f{i}", [128, 8, 256], BF16) for i in range(2)]
        sxa = [asb(f"l1f_h3xa{i}", [128, 2, C]) for i in range(2)]
        ph_ = [aps(f"l1f_h3ph{i}", [128, 256]) for i in range(2)]
        po = [aps(f"l1f_h3po{i}", [128, 512]) for i in range(4)]

        blk = h3.enter_context(nc.Block("l1f_H3", no_gpsimd_drain=True))
        H2F = [(nSlb, 128 * NT), (nSxa, 16 * NT), (nVrt, 12 * NT)]

        def h2w(eng):
            for sem_, val_ in H2F:
                eng.wait_ge(sem_, val_)

        @blk.gpsimd
        def _(g):
            h2w(g)
            for e in range(E):
                if e >= 2:
                    g.wait_ge(mTy, NIT * (e - 1))
                g.dma_start(out=sW1m[e % 2][:],
                            in_=d["l1f_w1"][:, e * 2048:(e + 1) * 2048].rearrange("(k p) f -> p k f", p=128)
                            ).then_inc(mLw, 16)
                g.dma_start(out=sW2m[e % 2][:],
                            in_=d["l1f_w2"][e * 2048:(e + 1) * 2048, :].rearrange("(k p) dd -> p k dd", p=128)
                            ).then_inc(mLw, 16)

        @blk.sync
        def _(s):
            h2w(s)
            NTOT = NIT * E

            def f2load(n):
                ee, ii = divmod(n, NIT)
                t0 = ii * 256
                if n >= 2:
                    s.wait_ge(mTh, 16 * (n - 1))
                for kk in range(8):
                    s.dma_start_transpose(sf2t[ii % 2][:, kk, :],
                                          lnbcm[kk * T + t0: kk * T + t0 + 256, :]
                                          ).then_inc(mLf2, 16)

            for e in range(E):
                for it in range(NIT):
                    n = NIT * e + it
                    t0 = it * 256
                    if n == 0:
                        f2load(0)
                        f2load(1)
                    if n >= 1:
                        s.wait_ge(mSxa, 16 * n)
                    s.dma_start(out=sxa[it % 2][:],
                                in_=x_out[t0:t0 + 256, :].rearrange("(s p) c -> p s c", p=128)
                                ).then_inc(mLxa, 16)
                    if n + 2 < NTOT:
                        f2load(n + 2)
                    s.wait_ge(mVy, n + 1)
                    s.dma_start(out=x_out[t0:t0 + 256, :].rearrange("(s p) c -> p s c", p=128),
                                in_=sxa[it % 2][:]).then_inc(mSxa, 16)
            s.wait_ge(mSxa, 16 * NIT * E)

        @blk.tensor
        def _(t):
            h2w(t)
            for e in range(E):
                t.wait_ge(mLw, 32 * (e + 1))
                for it in range(NIT):
                    n = NIT * e + it
                    t.wait_ge(mLf2, 128 * (n + 1))
                    for fc in range(16):
                        if 16 * n + fc >= 2:
                            t.wait_ge(mAh, 16 * n + fc - 1)
                        for kk in range(8):
                            ins = t.matmul(ph_[fc % 2][:], sW1m[e % 2][:, kk, fc * 128:(fc + 1) * 128],
                                           sf2t[it % 2][:, kk, :],
                                           start=(kk == 0), stop=(kk == 7))
                        ins.then_inc(mTh, 1)
                    if n >= 1:
                        t.wait_ge(mVy, n)
                    for fc in range(16):
                        t.wait_ge(mAh, 16 * n + fc + 1)
                        for sub in range(2):
                            for half in range(2):
                                ins = t.matmul(po[2 * sub + half][:],
                                               sh[:, fc, sub * 128:(sub + 1) * 128],
                                               sW2m[e % 2][:, fc, half * 512:(half + 1) * 512],
                                               start=(fc == 0), stop=(fc == 15))
                    ins.then_inc(mTy, 1)

        @blk.scalar
        def _(s):
            h2w(s)
            for e in range(E):
                for it in range(NIT):
                    n = NIT * e + it
                    for fc in range(16):
                        s.wait_ge(mTh, 16 * n + fc + 1)
                        s.activation(sh[:, fc, :], ph_[fc % 2][:], AF.Gelu).then_inc(mAh, 1)

        @blk.vector
        def _(v):
            h2w(v)
            for e in range(E):
                for it in range(NIT):
                    n = NIT * e + it
                    v.wait_ge(mTy, n + 1)
                    v.wait_ge(mLxa, 16 * (n + 1))
                    for sub in range(2):
                        wcol = sroute4[:, 2 * it + sub, e:e + 1]
                        for half in range(2):
                            ins = v.scalar_tensor_tensor(
                                sxa[it % 2][:, sub, half * 512:(half + 1) * 512],
                                po[2 * sub + half][:], wcol,
                                sxa[it % 2][:, sub, half * 512:(half + 1) * 512],
                                ALU.mult, ALU.add)
                    ins.then_inc(mVy, 1)
    return


def _l1f_masks():
    il = np.arange(512)[None, :]
    jl = np.arange(128)[:, None]
    m = np.zeros((128, 4, 2, 512), np.float32)
    for s_ in range(4):
        m[:, s_, 0, :] = (il >= 128 * s_ + jl).astype(np.float32)
        m[:, s_, 1, :] = m[:, s_, 0, :]
    return np.ascontiguousarray(m.reshape(128, 4096))


def _host_consts():
    perm = np.concatenate([np.arange(0, C, 2), np.arange(1, C, 2)])
    tpos = np.arange(T, dtype=np.float64)
    jidx = np.arange(C // 2, dtype=np.float64)
    ang = np.outer(T - 1 - tpos, jidx)
    cosfm = np.cos(ang).T.astype(np.float32).copy()
    sinfm = np.sin(ang).T.astype(np.float32).copy()
    il = np.arange(256)[None, :]
    jl = np.arange(128)[:, None]
    maskA = (il >= jl).astype(np.float32)
    maskB = (il >= 128 + jl).astype(np.float32)
    return perm, cosfm, sinfm, maskA, maskB


def build_program(n_phases=4, detect_races=True):
    nc = bass.Bass("TRN2", target_bir_lowering=False, debug=False, detect_race_conditions=detect_races)
    top = ExitStack()
    env = Env(nc, top)
    d = env.d

    def din(name, shape, dt=F32):
        d[name] = nc.dram_tensor(name, shape, dt, kind="ExternalInput").ap()

    din("x0", [T, C])
    din("l0_wq", [C, C]); din("l0_wk", [C, C])
    din("l0_bq", [1, C]); din("l0_bk", [1, C])
    din("l0_wvT", [C, C]); din("l0_wp", [C, C])
    din("l0_bv128", [128, 8]); din("l0_bp", [1, C])
    din("cosfm", [C // 2, T]); din("sinfm", [C // 2, T])
    din("l0_maskA", [128, 256]); din("l0_maskB", [128, 256])
    din("idf32", [128, 128])
    din("l0f_centT", [C, 8])
    din("l0f_w1T", [8 * C, 4096]); din("l0f_w2T", [8 * 4096, C])
    din("l0f_gam128", [8 * 128, 8]); din("l0f_beta128", [8 * 128, 8])
    din("l0f_b1c128", [8 * 128, 32]); din("l0f_b2", [8, C])
    din("l1a_qkvwT", [C, 3072]); din("l1a_qkvb128", [128, 24])
    din("l1a_sel2a", [128, 2]); din("l1a_sel2b", [2, 128])
    din("l1a_kgb", [128, 4]); din("l1a_temp128", [128, 8])
    din("l1a_atnb", [128, C])
    din("l1a_p1wT", [C, C]); din("l1a_p1b128", [128, 8])
    din("l1a_p2wT", [C, C]); din("l1a_p2b", [1, C])
    din("l1f_ln1b", [128, C]); din("l1f_ln2b", [128, C])
    din("l1f_cattn", [C, 3072]); din("l1f_cproj", [C, C])
    din("l1f_router", [C, 8]); din("l1f_masks", [128, 4096])
    din("l1f_w1", [C, 16384]); din("l1f_w2", [16384, C])

    out = nc.dram_tensor("out", [T, C], F32, kind="ExternalOutput").ap()

    streams = [d["x0"]]
    names = ["x1", "x2", "x3"]
    for i in range(n_phases - 1):
        streams.append(nc.dram_tensor(names[i], [T, C], F32, kind="Internal").ap())
    streams.append(out)
    assert len(streams) == n_phases + 1

    fence = []
    fence = phase_l0_attn(env, streams[0], streams[1], fence)
    if n_phases >= 2:
        fence = env.phase_boundary(fence)
        fence = phase_l0_ffn(env, streams[1], streams[2], fence)
    if n_phases >= 3:
        fence = env.phase_boundary(fence)
        fence = phase_l1_attn(env, streams[2], streams[3], fence)
    if n_phases >= 4:
        fence = env.phase_boundary(fence)
        fence = phase_l1_ffn(env, streams[3], streams[4], fence)

    top.close()
    return nc


def prep_in_maps(inputs):
    perm, cosfm, sinfm, maskA, maskB = _host_consts()
    wq = np.ascontiguousarray(inputs["L0_attn_Wq"][:, perm])
    wk = np.ascontiguousarray(inputs["L0_attn_Wk"][:, perm])
    bq = np.ascontiguousarray(inputs["L0_attn_bq"][perm])[None, :]
    bk = np.ascontiguousarray(inputs["L0_attn_bk"][perm])[None, :]
    wvT = np.ascontiguousarray(inputs["L0_attn_Wv"].T)
    wp = np.ascontiguousarray(inputs["L0_attn_Wp"])
    bv128 = np.ascontiguousarray(inputs["L0_attn_bv"].reshape(8, 128).T)
    bp = np.ascontiguousarray(inputs["L0_attn_bp"])[None, :]
    x = inputs["L0_attn_x"]

    idf32 = np.eye(128, dtype=np.float32)
    centT = np.ascontiguousarray(inputs["L0_ffn_centroids"].T)
    w1T = np.ascontiguousarray(inputs["L0_ffn_W1"].transpose(0, 2, 1)).reshape(8 * C, 4096)
    w2T = np.ascontiguousarray(inputs["L0_ffn_W2"].transpose(0, 2, 1)).reshape(8 * 4096, C)
    gam128 = np.ascontiguousarray(inputs["L0_ffn_gamma"].reshape(8, 8, 128).transpose(0, 2, 1)).reshape(8 * 128, 8)
    bet128 = np.ascontiguousarray(inputs["L0_ffn_beta"].reshape(8, 8, 128).transpose(0, 2, 1)).reshape(8 * 128, 8)
    b1c128 = np.ascontiguousarray(inputs["L0_ffn_b1"].reshape(8, 32, 128).transpose(0, 2, 1)).reshape(8 * 128, 32)
    b2 = np.ascontiguousarray(inputs["L0_ffn_b2"])
    qkvwT = np.ascontiguousarray(inputs["L1_attn_qkv_w"].T)
    qkvb128 = np.ascontiguousarray(inputs["L1_attn_qkv_b"].reshape(24, 128).T)
    sel2a = np.zeros((128, 2), np.float32); sel2a[:64, 0] = 1.0; sel2a[64:, 1] = 1.0
    sel2b = np.ascontiguousarray(sel2a.T)
    kgb = np.stack([np.tile(inputs["L1_attn_kn_g"], 2), np.tile(inputs["L1_attn_kn_b"], 2),
                    np.tile(inputs["L1_attn_vn_g"], 2), np.tile(inputs["L1_attn_vn_b"], 2)],
                   axis=1).astype(np.float32)
    tarr = inputs["L1_attn_temperature"].reshape(16)
    temp128 = np.stack([np.repeat(tarr[2 * p:2 * p + 2], 64) for p in range(8)],
                       axis=1).astype(np.float32)
    atnb = np.zeros((128, C), np.float32)
    ab = inputs["L1_attn_attn_bias"]
    for p in range(8):
        atnb[0:64, p * 128:p * 128 + 64] = ab[2 * p]
        atnb[64:128, p * 128 + 64:p * 128 + 128] = ab[2 * p + 1]
    p1wT = np.ascontiguousarray(inputs["L1_attn_p1_w"].T)
    p1b128 = np.ascontiguousarray(inputs["L1_attn_p1_b"].reshape(8, 128).T)
    p2wT = np.ascontiguousarray(inputs["L1_attn_p2_w"].T)
    p2b = np.ascontiguousarray(inputs["L1_attn_p2_b"])[None, :]
    shared = dict(l0_wq=wq, l0_wk=wk, l0_bq=bq, l0_bk=bk, l0_wvT=wvT, l0_wp=wp,
                  l0_bv128=bv128, l0_bp=bp, cosfm=cosfm, sinfm=sinfm,
                  l0_maskA=maskA, l0_maskB=maskB,
                  idf32=idf32, l0f_centT=centT, l0f_w1T=w1T, l0f_w2T=w2T,
                  l0f_gam128=gam128, l0f_beta128=bet128, l0f_b1c128=b1c128, l0f_b2=b2,
                  l1a_qkvwT=qkvwT, l1a_qkvb128=qkvb128, l1a_sel2a=sel2a, l1a_sel2b=sel2b,
                  l1a_kgb=kgb, l1a_temp128=temp128, l1a_atnb=atnb,
                  l1a_p1wT=p1wT, l1a_p1b128=p1b128, l1a_p2wT=p2wT, l1a_p2b=p2b,
                  l1f_ln1b=np.ascontiguousarray(np.broadcast_to(inputs["L1_ffn_ln1_w"], (128, C))),
                  l1f_ln2b=np.ascontiguousarray(np.broadcast_to(inputs["L1_ffn_ln2_w"], (128, C))),
                  l1f_cattn=np.ascontiguousarray(inputs["L1_ffn_c_attn_w"]),
                  l1f_cproj=np.ascontiguousarray(inputs["L1_ffn_c_proj_w"]),
                  l1f_router=np.ascontiguousarray(inputs["L1_ffn_router_w"]),
                  l1f_masks=_l1f_masks(),
                  l1f_w1=np.ascontiguousarray(inputs["L1_ffn_w1"]),
                  l1f_w2=np.ascontiguousarray(inputs["L1_ffn_w2"]))
    in_maps = []
    for b in range(x.shape[0]):
        m = dict(shared)
        m["x0"] = np.ascontiguousarray(x[b][:T])
        in_maps.append(m)
    return in_maps


def kernel(**inputs):
    in_maps = prep_in_maps(inputs)
    nc = build_program(N_PHASES)
    res = run_bass_kernel_spmd(nc, in_maps, core_ids=list(range(NCORES)))
    return np.stack([r["out"] for r in res.results], axis=0)


# version 19 (final)
# speedup vs baseline: 1.1587x; 1.0482x over previous
import numpy as np
import ml_dtypes
from contextlib import ExitStack

import concourse.bass as bass
import concourse.mybir as mybir
from concourse.bass_utils import run_bass_kernel_spmd

F32 = mybir.dt.float32
BF16 = mybir.dt.bfloat16
U32 = mybir.dt.uint32
AF = mybir.ActivationFunctionType
ALU = mybir.AluOpType
AX = mybir.AxisListType.X

T = 4096
C = 1024
NCORES = 8

N_PHASES = 4
DEV_DUMP = False
EMIT_NOPS = True


class Env:
    def __init__(self, nc, top):
        self.nc = nc
        self.top = top
        self.d = {}
        self.pool = []
        self.cursor = 0

    def sem(self, name):
        if self.cursor < len(self.pool):
            h = self.pool[self.cursor]
        else:
            h = self.top.enter_context(self.nc.semaphore(f"pool{len(self.pool)}"))
            self.pool.append(h)
        self.cursor += 1
        return h

    def persistent_sem(self, name):
        return self.top.enter_context(self.nc.semaphore(name))

    def phase_boundary(self, fence):
        n_used = self.cursor
        self.cursor = 0
        self._bnd_n = getattr(self, "_bnd_n", 0) + 1
        bnd = self.persistent_sem(f"bnd{self._bnd_n}")
        with self.nc.Block(f"boundary{self._bnd_n}", no_gpsimd_drain=True) as blk:
            @blk.gpsimd
            def _(g):
                for sem_, val_ in fence:
                    g.wait_ge(sem_, val_)
                g.dma_reset()
                for h in self.pool[:n_used]:
                    g.sem_clear(h)
                g.sem_inc(bnd, 1)
        return [(bnd, 1)]


def _tnop(eng):
    if EMIT_NOPS:
        eng.nop(cycle_cnt=512, nofuse=True)


def _ln_stats(v, a, sx, ssum, ssq, mcols, e2, msq, var, sqj, std, rstd, nsub, sem_sq, sq_target, sem_std, std_target, inv_n):
    raise NotImplementedError


def phase_l0_attn(env, x_in, x_out, fence):
    nc = env.nc
    d = env.d

    def fw(eng):
        for sem_, val_ in fence:
            eng.wait_ge(sem_, val_)

    lnx16 = nc.dram_tensor("l0a_lnx16", [8 * T, 128], BF16, kind="Internal").ap()
    qfm = nc.dram_tensor("l0a_qfm", [C, T], BF16, kind="Internal").ap()
    kfm = nc.dram_tensor("l0a_kfm", [C, T], BF16, kind="Internal").ap()
    vtm = nc.dram_tensor("l0a_vtm", [T, C], BF16, kind="Internal").ap()

    S = env.sem
    pL = S("l0a_pL"); pLs = S("l0a_pLs"); pV = S("l0a_pV")
    pTwc = S("l0a_pTwc"); pVwc = S("l0a_pVwc"); pTbc = S("l0a_pTbc")
    pAbc = S("l0a_pAbc"); pTbp = S("l0a_pTbp"); pVbp = S("l0a_pVbp")
    aLx = S("l0a_aLx"); aLt = S("l0a_aLt"); aAsq = S("l0a_aAsq")
    aVst1 = S("l0a_aVst1"); aAstd = S("l0a_aAstd"); aVln = S("l0a_aVln")
    aLst = S("l0a_aLst"); aLtr = S("l0a_aLtr")
    aTq = S("l0a_aTq"); aTk = S("l0a_aTk"); aTv = S("l0a_aTv")
    aVq = S("l0a_aVq"); aAk = S("l0a_aAk"); aPk = S("l0a_aPk"); aVv = S("l0a_aVv")
    aSt = S("l0a_aSt")
    bLqv = S("l0a_bLqv"); bLk = S("l0a_bLk"); bLx0 = S("l0a_bLx0")
    bTw = S("l0a_bTw"); bAsp = S("l0a_bAsp"); bVmk = S("l0a_bVmk")
    bToj = S("l0a_bToj"); bVep = S("l0a_bVep"); bSx1 = S("l0a_bSx1")

    NT = T // 512
    NI = T // 256
    jts = [2 * it + 2 for it in range(NI)]
    cum0 = [sum(jts[:it]) for it in range(NI)]
    CUMJ = sum(jts)
    dg0 = [2 * it for it in range(NI)]

    with ExitStack() as ph:
        sb = lambda n, s, dt=F32: ph.enter_context(nc.sbuf_tensor(n, s, dt))
        smaskA = sb("l0a_maskA", [128, 256])
        smaskB = sb("l0a_maskB", [128, 256])
        sbpb = sb("l0a_bpb", [128, C])
        seps = sb("l0a_eps", [128, 1])
        sonec = sb("l0a_onec", [128, 1])

        saw = ExitStack()
        sbw = lambda n, s, dt=F32: saw.enter_context(nc.sbuf_tensor(n, s, dt))
        sones16 = sbw("l0a_ones16", [1, 128], BF16)
        sones512 = sbw("l0a_ones512", [1, 512], BF16)
        swcomb = sbw("l0a_wcomb", [128, 8, C], BF16)
        sbcomb = sbw("l0a_bcomb", [1, C], BF16)
        sbq = sbw("l0a_bq", [1, C], BF16)
        sbk = sbw("l0a_bk", [1, C], BF16)
        swq = sbw("l0a_wq", [128, 8, C], BF16)
        swk = sbw("l0a_wk", [128, 8, C], BF16)

        with ExitStack() as pp:
            psb = lambda n, s, dt=F32: pp.enter_context(nc.sbuf_tensor(n, s, dt))
            pps = lambda n, s, dt=F32: pp.enter_context(nc.psum_tensor(n, s, dt))
            swvT = psb("l0a_wvT", [128, 8, C], BF16)
            swp = psb("l0a_wp", [128, 8, C], BF16)
            sbv = psb("l0a_bv", [128, 8], BF16)
            sbpr = psb("l0a_bpr", [1, C])
            sonesf = psb("l0a_onesf", [1, 128])
            ppsA = pps("l0a_ppsA", [128, 512])
            ppsB = pps("l0a_ppsB", [128, 512])
            ppsR = pps("l0a_ppsR", [1, 512])

            blk = pp.enter_context(nc.Block("l0a_prep", no_gpsimd_drain=True))

            @blk.gpsimd
            def _(g):
                fw(g)
                g.dma_start(out=swq[:], in_=d["l0_wq"].rearrange("(k p) c -> p k c", p=128)).then_inc(pL, 16)
                g.dma_start(out=swk[:], in_=d["l0_wk"].rearrange("(k p) c -> p k c", p=128)).then_inc(pL, 16)
                g.dma_start(out=swvT[:], in_=d["l0_wvT"].rearrange("(k p) c -> p k c", p=128)).then_inc(pL, 16)
                g.dma_start(out=swp[:], in_=d["l0_wp"].rearrange("(k p) c -> p k c", p=128)).then_inc(pL, 16)
                g.dma_start(out=sbq[:], in_=d["l0_bq"]).then_inc(pL, 16)
                g.dma_start(out=sbk[:], in_=d["l0_bk"]).then_inc(pL, 16)
                g.dma_start(out=sbv[:], in_=d["l0_bv128"]).then_inc(pL, 16)

            @blk.sync
            def _(s):
                fw(s)
                s.dma_start(out=sbpr[:], in_=d["l0_bp"]).then_inc(pLs, 16)
                s.dma_start(out=smaskA[:], in_=d["l0_maskA"]).then_inc(pLs, 16)
                s.dma_start(out=smaskB[:], in_=d["l0_maskB"]).then_inc(pLs, 16)

            @blk.vector
            def _(v):
                fw(v)
                v.memset(sones16[:], 1.0)
                v.memset(sones512[:], 1.0)
                v.memset(seps[:], 1e-5)
                v.memset(sonec[:], 1.0)
                v.memset(sonesf[:], 1.0).then_inc(pV, 1)
                for g_i in range(16):
                    v.wait_ge(pTwc, g_i + 1)
                    kk, hh = divmod(g_i, 2)
                    src = ppsA if g_i % 2 == 0 else ppsB
                    v.tensor_copy(swcomb[:, kk, hh * 512:(hh + 1) * 512], src[:]).then_inc(pVwc, 1)
                for hh in range(2):
                    v.wait_ge(pTbp, hh + 1)
                    src = ppsA if hh == 0 else ppsB
                    v.tensor_copy(sbpb[:, hh * 512:(hh + 1) * 512], src[:]).then_inc(pVbp, 1)

            @blk.tensor
            def _(t):
                t.wait_ge(pL, 16 * 7)
                t.wait_ge(pV, 1)
                for g_i in range(16):
                    kk, hh = divmod(g_i, 2)
                    dst = ppsA if g_i % 2 == 0 else ppsB
                    if g_i >= 2:
                        t.wait_ge(pVwc, g_i - 1)
                    for m in range(8):
                        ins = t.matmul(dst[:], swvT[:, m, kk * 128:(kk + 1) * 128],
                                       swp[:, m, hh * 512:(hh + 1) * 512],
                                       start=(m == 0), stop=(m == 7))
                    ins.then_inc(pTwc, 1)
                for hh in range(2):
                    if hh >= 1:
                        t.wait_ge(pAbc, hh)
                    for m in range(8):
                        ins = t.matmul(ppsR[:], sbv[:, m:m + 1],
                                       swp[:, m, hh * 512:(hh + 1) * 512],
                                       start=(m == 0), stop=(m == 7))
                    ins.then_inc(pTbc, 1)
                t.wait_ge(pLs, 48)
                t.wait_ge(pVwc, 16)
                for hh in range(2):
                    dst = ppsA if hh == 0 else ppsB
                    t.matmul(dst[:], sonesf[:], sbpr[:, hh * 512:(hh + 1) * 512],
                             start=True, stop=True).then_inc(pTbp, 1)

            @blk.scalar
            def _(s):
                fw(s)
                for hh in range(2):
                    s.wait_ge(pTbc, hh + 1)
                    s.copy(sbcomb[:, hh * 512:(hh + 1) * 512], ppsR[:]).then_inc(pAbc, 1)

        with ExitStack() as sa:
            asb = lambda n, s, dt=F32: sa.enter_context(nc.sbuf_tensor(n, s, dt))
            aps = lambda n, s, dt=F32: sa.enter_context(nc.psum_tensor(n, s, dt))
            sx = [asb(f"l0a_sx{i}", [128, 4, C]) for i in range(2)]
            scos = [asb(f"l0a_cos{i}", [128, 4, 512], BF16) for i in range(2)]
            ssin = [asb(f"l0a_sin{i}", [128, 4, 512], BF16) for i in range(2)]
            slnx = [asb(f"l0a_slnx{i}", [128, 4, C], BF16) for i in range(2)]
            slnf = [asb(f"l0a_slnf{i}", [128, 8, 512], BF16) for i in range(2)]
            sqst = [asb(f"l0a_sqst{i}", [128, 8, 512], BF16) for i in range(2)]
            skst = [asb(f"l0a_skst{i}", [128, 8, 512], BF16) for i in range(2)]
            svst = [asb(f"l0a_svst{i}", [128, 4, C], BF16) for i in range(2)]
            skf32 = [asb(f"l0a_skf32{i}", [128, 2, 512]) for i in range(2)]
            sqact = asb("l0a_sqact", [128, C])
            sqdve = asb("l0a_sqdve", [128, C])
            spool = asb("l0a_spool", [128, 512])
            sstat = asb("l0a_sstat", [128, 28])
            sstd = asb("l0a_sstd", [128, 4])
            pq = [aps(f"l0a_pq{i}", [128, 512]) for i in range(4)]
            pk = [aps(f"l0a_pk{i}", [128, 512]) for i in range(2)]
            pv = [aps(f"l0a_pv{i}", [128, 512]) for i in range(2)]

            ssum = sstat[:, 0:4]; ssq = sstat[:, 4:8]; smc = sstat[:, 8:12]
            se2 = sstat[:, 12:16]; svar = sstat[:, 16:20]; srstd = sstat[:, 20:24]
            smsq = sstat[:, 24:28]

            blk = sa.enter_context(nc.Block("l0a_stageA", no_gpsimd_drain=True))

            @blk.sync
            def _(s):
                for it in range(NT):
                    t0 = it * 512
                    if it >= 2:
                        s.wait_ge(aVln, it - 1)
                        s.wait_ge(aAsq, it - 1)
                    s.dma_start(out=sx[it % 2][:],
                                in_=x_in[t0:t0 + 512, :].rearrange("(s p) c -> p s c", p=128)
                                ).then_inc(aLx, 16)
                    s.wait_ge(aVln, it + 1)
                    for kk in range(8):
                        s.dma_start(out=lnx16[kk * T + t0: kk * T + t0 + 512, :].rearrange("(s2 p) c -> p s2 c", p=128),
                                    in_=slnx[it % 2][:, :, kk * 128:(kk + 1) * 128]
                                    ).then_inc(aLst, 16)

            @blk.scalar
            def _(s):
                for it in range(NT):
                    t0 = it * 512
                    s.wait_ge(aLx, 16 * (it + 1))
                    if it >= 1:
                        s.wait_ge(aVst1, it)
                    for ss in range(4):
                        ins = s.activation(sqact[:], sx[it % 2][:, ss, :], AF.Square,
                                           accum_out=ssq[:, ss:ss + 1])
                    ins.then_inc(aAsq, 1)
                    s.wait_ge(aVst1, it + 1)
                    s.activation(sstd[:], svar[:], AF.Sqrt, bias=seps[:], scale=-1.0).then_inc(aAstd, 1)
                    s.wait_ge(aLst, 128 * (it + 1))
                    if it >= 2:
                        s.wait_ge(aTv, 4 * (it - 1))
                    for kk in range(8):
                        s.dma_start_transpose(slnf[it % 2][:, kk, :],
                                              lnx16[kk * T + t0: kk * T + t0 + 512, :]
                                              ).then_inc(aLtr, 16)
                    for p_i in range(4):
                        g = 4 * it + p_i
                        s.wait_ge(aTk, g + 1)
                        if g >= 2:
                            s.wait_ge(aPk, g - 1)
                        s.copy(skf32[g % 2][:, 0, :], pk[0][:])
                        s.copy(skf32[g % 2][:, 1, :], pk[1][:]).then_inc(aAk, 1)

            @blk.vector
            def _(v):
                for it in range(NT):
                    v.wait_ge(aLx, 16 * (it + 1))
                    if it >= 2:
                        v.wait_ge(aLst, 128 * (it - 1))
                    for ss in range(4):
                        v.reduce_sum(ssum[:, ss:ss + 1], sx[it % 2][:, ss, :], axis=AX)
                    v.wait_ge(aAsq, it + 1)
                    v.tensor_scalar_mul(smc[:], ssum[:], 1.0 / C)
                    v.tensor_scalar_mul(se2[:], ssq[:], 1.0 / C)
                    _tnop(v)
                    for ss in range(3):
                        v.scalar_tensor_tensor(svar[:, ss:ss + 1], smc[:, ss:ss + 1],
                                               smc[:, ss:ss + 1], se2[:, ss:ss + 1],
                                               ALU.mult, ALU.subtract)
                    ins = v.scalar_tensor_tensor(svar[:, 3:4], smc[:, 3:4],
                                                 smc[:, 3:4], se2[:, 3:4],
                                                 ALU.mult, ALU.subtract)
                    ins.then_inc(aVst1, 1)
                    v.wait_ge(aAstd, it + 1)
                    v.reciprocal(srstd[:], sstd[:])
                    _tnop(v)
                    for ss in range(4):
                        ins = v.tensor_scalar(slnx[it % 2][:, ss, :], sx[it % 2][:, ss, :],
                                              smc[:, ss:ss + 1], srstd[:, ss:ss + 1],
                                              ALU.subtract, ALU.mult)
                    ins.then_inc(aVln, 1)
                    for p_i in range(4):
                        g = 4 * it + p_i
                        v.wait_ge(aTq, g + 1)
                        v.wait_ge(aLt, 32 * (it + 1))
                        if it >= 2:
                            v.wait_ge(aSt, 48 * (it - 1))
                        cosp = scos[it % 2][:, p_i, :]
                        sinp = ssin[it % 2][:, p_i, :]
                        qs = sqst[it % 2]
                        z0 = pq[2 * (g % 2)]
                        z1 = pq[2 * (g % 2) + 1]
                        v.tensor_mul(sqdve[:, 0:512], cosp, z0[:])
                        v.tensor_mul(sqdve[:, 512:1024], sinp, z1[:])
                        v.tensor_add(qs[:, p_i, :], sqdve[:, 0:512], sqdve[:, 512:1024])
                        v.tensor_mul(sqdve[:, 0:512], cosp, z1[:])
                        v.tensor_mul(sqdve[:, 512:1024], sinp, z0[:])
                        v.tensor_sub(qs[:, p_i + 4, :], sqdve[:, 0:512], sqdve[:, 512:1024]).then_inc(aVq, 1)
                    for p_i in range(4):
                        gv = 4 * it + p_i
                        if it >= 2:
                            v.wait_ge(aSt, 48 * (it - 1))
                        v.wait_ge(aTv, gv + 1)
                        v.tensor_copy(svst[it % 2][:, p_i, 0:512], pv[0][:])
                        v.tensor_copy(svst[it % 2][:, p_i, 512:1024], pv[1][:]).then_inc(aVv, 1)

            @blk.gpsimd
            def _(g):
                for it in range(NT):
                    t0 = it * 512
                    if it >= 2:
                        g.wait_ge(aVq, 4 * (it - 1))
                        g.wait_ge(aPk, 4 * (it - 1))
                    g.dma_start(out=scos[it % 2][:],
                                in_=d["cosfm"][:, t0:t0 + 512].rearrange("(j p) t -> p j t", p=128)
                                ).then_inc(aLt, 16)
                    g.dma_start(out=ssin[it % 2][:],
                                in_=d["sinfm"][:, t0:t0 + 512].rearrange("(j p) t -> p j t", p=128)
                                ).then_inc(aLt, 16)
                    for p_i in range(4):
                        g_i = 4 * it + p_i
                        g.wait_ge(aAk, g_i + 1)
                        if it >= 2:
                            g.wait_ge(aSt, 48 * (it - 1))
                        cosp = scos[it % 2][:, p_i, :]
                        sinp = ssin[it % 2][:, p_i, :]
                        ks = skst[it % 2]
                        z0 = skf32[g_i % 2][:, 0, :]
                        z1 = skf32[g_i % 2][:, 1, :]
                        g.tensor_mul(ks[:, p_i, :], cosp, z0)
                        g.tensor_mul(spool[:], sinp, z1)
                        g.tensor_add(ks[:, p_i, :], ks[:, p_i, :], spool[:])
                        g.tensor_mul(ks[:, p_i + 4, :], cosp, z1)
                        g.tensor_mul(spool[:], sinp, z0)
                        g.tensor_sub(ks[:, p_i + 4, :], ks[:, p_i + 4, :], spool[:]).then_inc(aPk, 1)
                    g.wait_ge(aVq, 4 * (it + 1))
                    g.dma_start(out=qfm.rearrange("(k p) t -> p k t", p=128)[:, :, t0:t0 + 512],
                                in_=sqst[it % 2][:]).then_inc(aSt, 16)
                    g.wait_ge(aPk, 4 * (it + 1))
                    g.dma_start(out=kfm.rearrange("(k p) t -> p k t", p=128)[:, :, t0:t0 + 512],
                                in_=skst[it % 2][:]).then_inc(aSt, 16)
                    g.wait_ge(aVv, 4 * (it + 1))
                    g.dma_start(out=vtm[t0:t0 + 512, :].rearrange("(s p) c -> p s c", p=128),
                                in_=svst[it % 2][:]).then_inc(aSt, 16)

            @blk.tensor
            def _(t):
                for it in range(NT):
                    t.wait_ge(aLtr, 128 * (it + 1))
                    lnf = slnf[it % 2]
                    for p_i in range(4):
                        g_i = 4 * it + p_i
                        if g_i >= 2:
                            t.wait_ge(aVq, g_i - 1)
                        for half, co in ((0, p_i), (1, p_i + 4)):
                            dst = pq[2 * (g_i % 2) + half]
                            t.matmul(dst[:], sbq[:, co * 128:(co + 1) * 128], sones512[:],
                                     start=True, stop=False)
                            for kk in range(8):
                                ins = t.matmul(dst[:], swq[:, kk, co * 128:(co + 1) * 128],
                                               lnf[:, kk, :], start=False, stop=(kk == 7))
                        ins.then_inc(aTq, 1)
                    for p_i in range(4):
                        g_i = 4 * it + p_i
                        if g_i >= 1:
                            t.wait_ge(aAk, g_i)
                        for half, co in ((0, p_i), (1, p_i + 4)):
                            dst = pk[half]
                            t.matmul(dst[:], sbk[:, co * 128:(co + 1) * 128], sones512[:],
                                     start=True, stop=False)
                            for kk in range(8):
                                ins = t.matmul(dst[:], swk[:, kk, co * 128:(co + 1) * 128],
                                               lnf[:, kk, :], start=False, stop=(kk == 7))
                        ins.then_inc(aTk, 1)
                    for sub in range(4):
                        gv = 4 * it + sub
                        if gv >= 1:
                            t.wait_ge(aVv, gv)
                        for half in range(2):
                            dst = pv[half]
                            t.matmul(dst[:], sones16[:], sbcomb[:, half * 512:(half + 1) * 512],
                                     start=True, stop=False)
                            for kk in range(8):
                                ins = t.matmul(dst[:], lnf[:, kk, sub * 128:(sub + 1) * 128],
                                               swcomb[:, kk, half * 512:(half + 1) * 512],
                                               start=False, stop=(kk == 7))
                        ins.then_inc(aTv, 1)
        saw.close()

        with ExitStack() as sbk_:
            asb = lambda n, s, dt=F32: sbk_.enter_context(nc.sbuf_tensor(n, s, dt))
            aps = lambda n, s, dt=F32: sbk_.enter_context(nc.psum_tensor(n, s, dt))
            q_sb = asb("l0a_qsb", [128, 8, T], BF16)
            v_sb = asb("l0a_vsb", [128, T // 128, C], BF16)
            sk = [asb(f"l0a_sk{i}", [128, 8, 256], BF16) for i in range(2)]
            sew = [asb(f"l0a_sew{i}", [128, 256]) for i in range(2)]
            swei = [asb(f"l0a_swei{i}", [128, 256], BF16) for i in range(2)]
            snl = [asb(f"l0a_snl{i}", [128, 256]) for i in range(2)]
            sx0 = [asb(f"l0a_sx0_{i}", [128, 2, C]) for i in range(2)]
            sx1 = [asb(f"l0a_sx1_{i}", [128, 2, C]) for i in range(2)]
            pw = [aps(f"l0a_pw{i}", [128, 256]) for i in range(2)]
            po = [aps(f"l0a_po{i}", [128, 512]) for i in range(4)]

            blk = sbk_.enter_context(nc.Block("l0a_stageB", no_gpsimd_drain=True))

            @blk.sync
            def _(s):
                s.wait_ge(aSt, 48 * NT)
                s.dma_start(out=q_sb[:], in_=qfm.rearrange("(k p) t -> p k t", p=128)).then_inc(bLqv, 16)
                s.dma_start(out=v_sb[:], in_=vtm.rearrange("(j p) c -> p j c", p=128)).then_inc(bLqv, 16)
                for it in range(NI):
                    i0 = it * 256
                    if it >= 2:
                        s.wait_ge(bTw, cum0[it - 1])
                    s.dma_start(out=sk[it % 2][:],
                                in_=kfm.rearrange("(k p) t -> p k t", p=128)[:, :, i0:i0 + 256]
                                ).then_inc(bLk, 16)

            @blk.gpsimd
            def _(g):
                for it in range(NI):
                    i0 = it * 256
                    if it >= 2:
                        g.wait_ge(bVep, it - 1)
                    g.dma_start(out=sx0[it % 2][:],
                                in_=x_in[i0:i0 + 256, :].rearrange("(s p) c -> p s c", p=128)
                                ).then_inc(bLx0, 16)
                    g.wait_ge(bVep, it + 1)
                    g.dma_start(out=x_out[i0:i0 + 256, :].rearrange("(s p) c -> p s c", p=128),
                                in_=sx1[it % 2][:]).then_inc(bSx1, 16)
                g.wait_ge(bSx1, 16 * NI)

            @blk.tensor
            def _(t):
                t.wait_ge(bLqv, 32)
                for it in range(NI):
                    njt = jts[it]
                    t.wait_ge(bLk, 16 * (it + 1))
                    for jt in range(njt + 1):
                        if jt < njt:
                            cj = cum0[it] + jt
                            if cj >= 2:
                                t.wait_ge(bAsp, cj - 1)
                            for kk in range(8):
                                ins = t.matmul(pw[cj % 2][:], q_sb[:, kk, jt * 128:(jt + 1) * 128],
                                               sk[it % 2][:, kk, :], start=(kk == 0), stop=(kk == 7))
                            ins.then_inc(bTw, 1)
                        if jt >= 1:
                            jo = jt - 1
                            cj = cum0[it] + jo
                            t.wait_ge(bAsp, cj + 1)
                            if jo in (2 * it, 2 * it + 1):
                                t.wait_ge(bVmk, dg0[it] + (jo - 2 * it) + 1)
                            if jo == 0 and it >= 1:
                                t.wait_ge(bVep, it)
                            for sub in range(2):
                                for half in range(2):
                                    ins = t.matmul(po[2 * sub + half][:],
                                                   swei[cj % 2][:, sub * 128:(sub + 1) * 128],
                                                   v_sb[:, jo, half * 512:(half + 1) * 512],
                                                   start=(jo == 0), stop=(jo == njt - 1))
                            ins.then_inc(bToj, 1)

            @blk.scalar
            def _(s):
                for it in range(NI):
                    for jt in range(jts[it]):
                        cj = cum0[it] + jt
                        s.wait_ge(bTw, cj + 1)
                        if cj >= 2:
                            s.wait_ge(bToj, cj - 1)
                        s.activation(sew[cj % 2][:], pw[cj % 2][:], AF.Exp, scale=1.0 / 32.0)
                        if jt in (2 * it, 2 * it + 1):
                            dcnt = dg0[it] + (jt - 2 * it)
                            if dcnt >= 2:
                                s.wait_ge(bVmk, dcnt - 1)
                            s.activation(snl[dcnt % 2][:], sew[cj % 2][:], AF.Ln, bias=sonec[:]).then_inc(bAsp, 1)
                        else:
                            s.activation(swei[cj % 2][:], sew[cj % 2][:], AF.Ln, bias=sonec[:]).then_inc(bAsp, 1)

            @blk.vector
            def _(v):
                for it in range(NI):
                    for dd in range(2):
                        jt = 2 * it + dd
                        cj = cum0[it] + jt
                        dcnt = dg0[it] + dd
                        v.wait_ge(bAsp, cj + 1)
                        v.tensor_mul(swei[cj % 2][:], snl[dcnt % 2][:],
                                     smaskA[:] if dd == 0 else smaskB[:]).then_inc(bVmk, 1)
                    v.wait_ge(bToj, cum0[it] + jts[it])
                    v.wait_ge(bLx0, 16 * (it + 1))
                    if it >= 2:
                        v.wait_ge(bSx1, 16 * (it - 1))
                    for sub in range(2):
                        for half in range(2):
                            dst = sx1[it % 2][:, sub, half * 512:(half + 1) * 512]
                            v.tensor_add(dst, po[2 * sub + half][:], sbpb[:, half * 512:(half + 1) * 512])
                            ins = v.tensor_add(dst, dst, sx0[it % 2][:, sub, half * 512:(half + 1) * 512])
                    ins.then_inc(bVep, 1)
    return [(bSx1, 16 * NI)]


def phase_l0_ffn(env, x_in, x_out, fence):
    nc = env.nc
    d = env.d
    E = 8
    NT = T // 512

    f2cm = nc.dram_tensor("l0f_f2cm", [8 * T, 128], BF16, kind="Internal").ap()
    if DEV_DUMP:
        dbg_sc = nc.dram_tensor("dbg_sc", [128, T // 128, 8], F32, kind="ExternalOutput").ap()
        dbg_rt = nc.dram_tensor("dbg_rt", [128, T // 128, 8], F32, kind="ExternalOutput").ap()
        dbg_tr = nc.dram_tensor("dbg_tr", [128, T // 128, 8], F32, kind="ExternalOutput").ap()
        dbg_f2 = nc.dram_tensor("dbg_f2", [8 * T, 128], BF16, kind="ExternalOutput").ap()
        dbg_xa = nc.dram_tensor("dbg_xa", [T, C], F32, kind="ExternalOutput").ap()
        dbg_st = nc.dram_tensor("dbg_st", [128, T // 512, 32], F32, kind="ExternalOutput").ap()

    S = env.sem
    c0Lx = S("l0f_c0Lx"); c0A1 = S("l0f_c0A1"); c0V1 = S("l0f_c0V1")
    c0A1s = S("l0f_c0A1s"); c0Vf = S("l0f_c0Vf"); c0A2 = S("l0f_c0A2")
    c0V2 = S("l0f_c0V2"); c0A2s = S("l0f_c0A2s"); c0Vln = S("l0f_c0Vln")
    c0Sf2 = S("l0f_c0Sf2"); c0Sxa = S("l0f_c0Sxa")
    c0Ttr = S("l0f_c0Ttr"); c0Vtr = S("l0f_c0Vtr"); c0Tsc = S("l0f_c0Tsc")
    c0Vam = S("l0f_c0Vam"); c0Aal = S("l0f_c0Aal"); c0Vrt = S("l0f_c0Vrt")
    c0Lc = S("l0f_c0Lc")
    fLw = S("l0f_fLw"); fLw2 = S("l0f_fLw2"); fVw = S("l0f_fVw"); fTbc = S("l0f_fTbc")
    fAb = S("l0f_fAb"); fTtr2 = S("l0f_fTtr2")
    fVb = S("l0f_fVb"); fTh = S("l0f_fTh"); fAh = S("l0f_fAh")
    fTy = S("l0f_fTy"); fVy = S("l0f_fVy")
    fLf2 = S("l0f_fLf2"); fLxa = S("l0f_fLxa"); fSxa = S("l0f_fSxa")
    fLb = S("l0f_fLb"); fLbg = S("l0f_fLbg")

    def fw(eng):
        for sem_, val_ in fence:
            eng.wait_ge(sem_, val_)

    with ExitStack() as ph:
        sb = lambda n, s, dt=F32: ph.enter_context(nc.sbuf_tensor(n, s, dt))
        sroute = sb("l0f_route", [128, T // 128, E])
        seps2 = sb("l0f_eps", [128, 1])

        with ExitStack() as f0:
            asb = lambda n, s, dt=F32: f0.enter_context(nc.sbuf_tensor(n, s, dt))
            aps = lambda n, s, dt=F32: f0.enter_context(nc.psum_tensor(n, s, dt))
            sx1f = [asb(f"l0f_sx1f{i}", [128, 4, C]) for i in range(2)]
            sfeat = [asb(f"l0f_sfeat{i}", [128, 4, C]) for i in range(2)]
            sf2 = [asb(f"l0f_sf2_{i}", [128, 4, C], BF16) for i in range(2)]
            sxai = [asb(f"l0f_sxai{i}", [128, 4, C]) for i in range(2)]
            strx = [asb(f"l0f_strx{i}", [128, 128]) for i in range(2)]
            scent = asb("l0f_cent", [128, 8, E])
            sidf = asb("l0f_idf", [128, 128])
            ssc = asb("l0f_ssc", [128, E])
            samax = asb("l0f_amax", [128, 1])
            salpha = asb("l0f_alpha", [128, 1])
            swm = asb("l0f_wm", [128, E])
            sqact = asb("l0f_sqact", [128, C])
            sqd = asb("l0f_sqd", [128, C])
            sstat = asb("l0f_sstat", [128, 28])
            sstd = asb("l0f_sstd", [128, 4])
            if DEV_DUMP:
                sdbg_sc = asb("l0f_dbgsc", [128, T // 128, 8])
                sdbg_tr = asb("l0f_dbgtr", [128, T // 128, 8])
                sdbg_st = asb("l0f_dbgst", [128, T // 512, 32])
            ptr = [aps(f"l0f_ptr{i}", [128, 128]) for i in range(2)]
            psc = aps("l0f_psc", [128, E])

            ssum = sstat[:, 0:4]; ssq = sstat[:, 4:8]; smc = sstat[:, 8:12]
            se2 = sstat[:, 12:16]; svar = sstat[:, 16:20]; srstd = sstat[:, 20:24]
            smsq = sstat[:, 24:28]

            blk = f0.enter_context(nc.Block("l0f_F0", no_gpsimd_drain=True))

            @blk.sync
            def _(s):
                fw(s)
                s.dma_start(out=scent[:], in_=d["l0f_centT"].rearrange("(k p) e -> p k e", p=128),
                            allow_slow_non_contiguous=True).then_inc(c0Lc, 16)
                s.dma_start(out=sidf[:], in_=d["idf32"]).then_inc(c0Lc, 16)
                for it in range(NT):
                    t0 = it * 512
                    if it >= 2:
                        s.wait_ge(c0Vln, it - 1)
                        s.wait_ge(c0A1, it - 1)
                    s.dma_start(out=sx1f[it % 2][:],
                                in_=x_in[t0:t0 + 512, :].rearrange("(s p) c -> p s c", p=128)
                                ).then_inc(c0Lx, 16)
                    s.wait_ge(c0Vln, it + 1)
                    for kk in range(8):
                        s.dma_start(out=f2cm[kk * T + t0: kk * T + t0 + 512, :].rearrange("(s2 p) c -> p s2 c", p=128),
                                    in_=sf2[it % 2][:, :, kk * 128:(kk + 1) * 128]
                                    ).then_inc(c0Sf2, 16)

            @blk.gpsimd
            def _(g):
                fw(g)
                for it in range(NT):
                    t0 = it * 512
                    g.wait_ge(c0Vln, it + 1)
                    g.dma_start(out=x_out[t0:t0 + 512, :].rearrange("(s p) c -> p s c", p=128),
                                in_=sxai[it % 2][:]).then_inc(c0Sxa, 16)
                if DEV_DUMP:
                    g.wait_ge(c0Vrt, 4 * NT)
                    g.wait_ge(c0Sf2, 128 * NT)
                    g.wait_ge(c0Sxa, 16 * NT)
                    g.dma_start(out=dbg_sc, in_=sdbg_sc[:]).then_inc(c0Sxa, 16)
                    g.dma_start(out=dbg_rt, in_=sroute[:]).then_inc(c0Sxa, 16)
                    g.dma_start(out=dbg_tr, in_=sdbg_tr[:]).then_inc(c0Sxa, 16)
                    g.dma_start(out=dbg_f2, in_=f2cm).then_inc(c0Sxa, 16)
                    g.dma_start(out=dbg_xa, in_=x_out).then_inc(c0Sxa, 16)
                    g.dma_start(out=dbg_st, in_=sdbg_st[:]).then_inc(c0Sxa, 16)
                    g.wait_ge(c0Sxa, 16 * NT + 96)

            @blk.scalar
            def _(s):
                fw(s)
                for it in range(NT):
                    s.wait_ge(c0Lx, 16 * (it + 1))
                    if it >= 1:
                        s.wait_ge(c0V2, it)
                    for ss in range(4):
                        ins = s.activation(sqact[:], sx1f[it % 2][:, ss, :], AF.Square,
                                           accum_out=ssq[:, ss:ss + 1])
                    ins.then_inc(c0A1, 1)
                    s.wait_ge(c0V1, it + 1)
                    s.activation(sstd[:], svar[:], AF.Sqrt, bias=seps2[:], scale=-1.0).then_inc(c0A1s, 1)
                    s.wait_ge(c0V2, it + 1)
                    s.activation(sstd[:], svar[:], AF.Sqrt, bias=seps2[:], scale=-1.0).then_inc(c0A2s, 1)
                    for su in range(4):
                        s.wait_ge(c0Vam, 4 * it + su + 1)
                        s.activation(salpha[:], samax[:], AF.Sigmoid).then_inc(c0Aal, 1)

            @blk.vector
            def _(v):
                fw(v)
                v.memset(seps2[:], 1e-5)
                for it in range(NT):
                    xb = sx1f[it % 2]
                    fb = sfeat[it % 2]
                    v.wait_ge(c0Lx, 16 * (it + 1))
                    if it >= 2:
                        v.wait_ge(c0Tsc, 4 * (it - 1))
                        v.wait_ge(c0Sf2, 128 * (it - 1))
                        v.wait_ge(c0Sxa, 16 * (it - 1))
                    for ss in range(4):
                        v.reduce_sum(ssum[:, ss:ss + 1], xb[:, ss, :], axis=AX)
                    v.wait_ge(c0A1, it + 1)
                    v.tensor_scalar_mul(smc[:], ssum[:], 1.0 / C)
                    v.tensor_scalar_mul(se2[:], ssq[:], 1.0 / C)
                    _tnop(v)
                    for ss in range(3):
                        v.scalar_tensor_tensor(svar[:, ss:ss + 1], smc[:, ss:ss + 1],
                                               smc[:, ss:ss + 1], se2[:, ss:ss + 1],
                                               ALU.mult, ALU.subtract)
                    ins = v.scalar_tensor_tensor(svar[:, 3:4], smc[:, 3:4],
                                                 smc[:, 3:4], se2[:, 3:4],
                                                 ALU.mult, ALU.subtract)
                    ins.then_inc(c0V1, 1)
                    v.wait_ge(c0A1s, it + 1)
                    v.reciprocal(srstd[:], sstd[:])
                    _tnop(v)
                    for ss in range(4):
                        ins = v.tensor_scalar(fb[:, ss, :], xb[:, ss, :],
                                              smc[:, ss:ss + 1], srstd[:, ss:ss + 1],
                                              ALU.subtract, ALU.mult)
                    ins.then_inc(c0Vf, 1)
                    for ss in range(4):
                        v.tensor_add(sxai[it % 2][:, ss, :], xb[:, ss, :], fb[:, ss, :])
                    for ss in range(4):
                        v.reduce_sum(ssum[:, ss:ss + 1], fb[:, ss, :], axis=AX)
                        v.tensor_mul(sqd[:], fb[:, ss, :], fb[:, ss, :])
                        v.reduce_sum(ssq[:, ss:ss + 1], sqd[:], axis=AX)
                    v.tensor_scalar_mul(smc[:], ssum[:], 1.0 / C)
                    v.tensor_scalar_mul(se2[:], ssq[:], 1.0 / C)
                    _tnop(v)
                    for ss in range(3):
                        v.scalar_tensor_tensor(svar[:, ss:ss + 1], smc[:, ss:ss + 1],
                                               smc[:, ss:ss + 1], se2[:, ss:ss + 1],
                                               ALU.mult, ALU.subtract)
                    ins = v.scalar_tensor_tensor(svar[:, 3:4], smc[:, 3:4],
                                                 smc[:, 3:4], se2[:, 3:4],
                                                 ALU.mult, ALU.subtract)
                    ins.then_inc(c0V2, 1)
                    v.wait_ge(c0A2s, it + 1)
                    v.reciprocal(srstd[:], sstd[:])
                    _tnop(v)
                    if DEV_DUMP:
                        v.tensor_copy(sdbg_st[:, it, 0:28], sstat[:])
                        v.tensor_copy(sdbg_st[:, it, 28:32], sstd[:])
                    for ss in range(4):
                        ins = v.tensor_scalar(sf2[it % 2][:, ss, :], fb[:, ss, :],
                                              smc[:, ss:ss + 1], srstd[:, ss:ss + 1],
                                              ALU.subtract, ALU.mult)
                    ins.then_inc(c0Vln, 1)
                    for su in range(4):
                        gidx = 4 * it + su
                        for kk in range(8):
                            v.wait_ge(c0Ttr, 8 * gidx + kk + 1)
                            v.tensor_copy(strx[kk % 2][:], ptr[kk % 2][:]).then_inc(c0Vtr, 1)
                        v.wait_ge(c0Tsc, gidx + 1)
                        v.tensor_copy(ssc[:], psc[:])
                        _tnop(v)
                        if DEV_DUMP:
                            v.tensor_copy(sdbg_sc[:, gidx, :], ssc[:])
                            v.tensor_copy(sdbg_tr[:, gidx, :], strx[1][:, 0:8])
                        v.reduce_max(samax[:], ssc[:], axis=AX).then_inc(c0Vam, 1)
                        v.wait_ge(c0Aal, gidx + 1)
                        v.tensor_scalar(swm[:], ssc[:], samax[:], None, ALU.is_equal)
                        _tnop(v)
                        v.tensor_scalar(sroute[:, gidx, :], swm[:], salpha[:], None,
                                        ALU.mult).then_inc(c0Vrt, 1)

            @blk.tensor
            def _(t):
                fw(t)
                t.wait_ge(c0Lc, 32)
                for it in range(NT):
                    t.wait_ge(c0Vf, it + 1)
                    for su in range(4):
                        gidx = 4 * it + su
                        if gidx >= 1:
                            t.wait_ge(c0Vrt, gidx)
                        for kk in range(8):
                            if 8 * gidx + kk >= 2:
                                t.wait_ge(c0Vtr, 8 * gidx + kk - 1)
                            t.transpose(ptr[kk % 2][:], sfeat[it % 2][:, su, kk * 128:(kk + 1) * 128],
                                        sidf[:]).then_inc(c0Ttr, 1)
                            t.wait_ge(c0Vtr, 8 * gidx + kk + 1)
                            ins = t.matmul(psc[:], strx[kk % 2][:], scent[:, kk, :],
                                           start=(kk == 0), stop=(kk == 7))
                        ins.then_inc(c0Tsc, 1)

        NIT = T // 256
        with ExitStack() as f1:
            asb = lambda n, s, dt=F32: f1.enter_context(nc.sbuf_tensor(n, s, dt))
            aps = lambda n, s, dt=F32: f1.enter_context(nc.psum_tensor(n, s, dt))
            sW1 = asb("l0f_w1", [128, 8, 4096], BF16)
            sW2 = asb("l0f_w2", [128, 32, C], BF16)
            sh = asb("l0f_h", [128, 32, 256], BF16)
            sf2t = [asb(f"l0f_f2t{i}", [128, 8, 256], BF16) for i in range(2)]
            sxa = [asb(f"l0f_sxa{i}", [128, 2, C]) for i in range(2)]
            sgam = asb("l0f_gam", [128, E])
            sbet = asb("l0f_bet", [128, E], BF16)
            sb1c = asb("l0f_b1c", [128, 32])
            sb2r = asb("l0f_b2r", [1, C], BF16)
            sbias = asb("l0f_bias", [128, 32])
            sbwrow = asb("l0f_bwrow", [1, 4096], BF16)
            sidf1 = asb("l0f_id16", [128, 128], BF16)
            sones16 = asb("l0f_ones16", [1, 128], BF16)
            ph_ = [aps(f"l0f_ph{i}", [128, 256])[:] for i in range(2)]
            po = [aps(f"l0f_po{i}", [128, 512]) for i in range(4)]
            prow_t = aps("l0f_prow", [1, 512])
            prow = [prow_t, prow_t]
            pcol = aps("l0f_pcol", [128, 64], BF16)

            blk = f1.enter_context(nc.Block("l0f_F1", no_gpsimd_drain=True))
            F0FENCE = [(c0Sf2, 128 * NT), (c0Sxa, 16 * NT), (c0Vrt, 4 * NT)]

            def f0w(eng):
                for sem_, val_ in F0FENCE:
                    eng.wait_ge(sem_, val_)

            @blk.vector
            def _(v):
                f0w(v)
                v.memset(sones16[:], 1.0)
                for e in range(E):
                    v.wait_ge(fLb, 32 * (e + 1))
                    v.wait_ge(fTtr2, e + 1)
                    v.tensor_add(sbias[:], pcol[:, 0:64:2], sb1c[:]).then_inc(fVb, 1)
                    for q in range(4):
                        v.wait_ge(fTbc, 8 * e + 2 * (q + 1))
                        v.wait_ge(fLw, 64 * e + 16 * (q + 1))
                        for kk in range(8):
                            ins = v.tensor_scalar_mul(sW1[:, kk, q * 1024:(q + 1) * 1024],
                                                      sW1[:, kk, q * 1024:(q + 1) * 1024],
                                                      sgam[:, kk:kk + 1])
                        ins.then_inc(fVw, 1)
                    for it in range(NIT):
                        n = NIT * e + it
                        v.wait_ge(fTy, 4 * n + 4)
                        v.wait_ge(fLxa, 16 * (n + 1))
                        for sub in range(2):
                            mcol = sroute[:, 2 * it + sub, e:e + 1]
                            for half in range(2):
                                ins = v.scalar_tensor_tensor(
                                    sxa[it % 2][:, sub, half * 512:(half + 1) * 512],
                                    po[2 * sub + half][:], mcol,
                                    sxa[it % 2][:, sub, half * 512:(half + 1) * 512],
                                    ALU.mult, ALU.add)
                        ins.then_inc(fVy, 1)

            @blk.gpsimd
            def _(g):
                f0w(g)
                for e in range(E):
                    if e >= 1:
                        g.wait_ge(fTbc, 8 * e)
                        g.wait_ge(fTy, 4 * (NIT * e - 1) + 1)
                    g.dma_start(out=sbet[:], in_=d["l0f_beta128"][e * 128:(e + 1) * 128, :]).then_inc(fLbg, 16)
                    g.dma_start(out=sb2r[:], in_=d["l0f_b2"][e:e + 1, :]).then_inc(fLbg, 16)
                    if e == 0:
                        g.dma_start(out=sidf1[:], in_=d["idf32"]).then_inc(fLbg, 16)
                    for q in range(4):
                        if e >= 1:
                            g.wait_ge(fTh, 32 * (NIT * e - 1) + 8 * (q + 1))
                        g.dma_start(out=sW1[:, :, q * 1024:(q + 1) * 1024],
                                    in_=d["l0f_w1T"][e * C:(e + 1) * C, :].rearrange("(k p) f -> p k f", p=128)[:, :, q * 1024:(q + 1) * 1024]
                                    ).then_inc(fLw, 16)
                    for q in range(4):
                        if e >= 1:
                            g.wait_ge(fTy, 4 * (NIT * e - 1) + q + 1)
                        g.dma_start(out=sW2[:, 8 * q:8 * (q + 1), :],
                                    in_=d["l0f_w2T"][e * 4096 + 1024 * q: e * 4096 + 1024 * (q + 1), :].rearrange("(k p) dd -> p k dd", p=128)
                                    ).then_inc(fLw2, 16)

            @blk.sync
            def _(s):
                f0w(s)
                NTOT = NIT * E

                def f2load(n):
                    ee, ii = divmod(n, NIT)
                    t0 = ii * 256
                    if n >= 2:
                        s.wait_ge(fTh, 32 * (n - 1))
                    for kk in range(8):
                        s.dma_start_transpose(sf2t[ii % 2][:, kk, :],
                                              f2cm[kk * T + t0: kk * T + t0 + 256, :]
                                              ).then_inc(fLf2, 16)

                for e in range(E):
                    if e >= 1:
                        s.wait_ge(fVw, 4 * e)
                        s.wait_ge(fVb, e)
                    s.dma_start(out=sgam[:], in_=d["l0f_gam128"][e * 128:(e + 1) * 128, :]).then_inc(fLb, 16)
                    s.dma_start(out=sb1c[:], in_=d["l0f_b1c128"][e * 128:(e + 1) * 128, :]).then_inc(fLb, 16)
                    for it in range(NIT):
                        n = NIT * e + it
                        t0 = it * 256
                        if e == 0 and it == 0:
                            f2load(0)
                            f2load(1)
                        if n >= 1:
                            s.wait_ge(fSxa, 16 * n)
                        s.dma_start(out=sxa[it % 2][:],
                                    in_=x_out[t0:t0 + 256, :].rearrange("(s p) c -> p s c", p=128)
                                    ).then_inc(fLxa, 16)
                        if n + 2 < NTOT:
                            f2load(n + 2)
                        s.wait_ge(fVy, n + 1)
                        s.dma_start(out=x_out[t0:t0 + 256, :].rearrange("(s p) c -> p s c", p=128),
                                    in_=sxa[it % 2][:]).then_inc(fSxa, 16)
                s.wait_ge(fSxa, 16 * NIT * E)

            @blk.tensor
            def _(t):
                f0w(t)
                for e in range(E):
                    t.wait_ge(fLb, 32 * (e + 1))
                    t.wait_ge(fLbg, 32 * e + 48)
                    for fg in range(8):
                        q = fg // 2
                        t.wait_ge(fLw, 64 * e + 16 * (q + 1))
                        if 8 * e + fg >= 1:
                            t.wait_ge(fAb, 8 * e + fg)
                        for kk in range(8):
                            ins = t.matmul(prow[fg % 2][:], sbet[:, kk:kk + 1],
                                           sW1[:, kk, fg * 512:(fg + 1) * 512],
                                           start=(kk == 0), stop=(kk == 7))
                        ins.then_inc(fTbc, 1)
                    t.wait_ge(fAb, 8 * (e + 1))
                    if e >= 1:
                        t.wait_ge(fVb, e)
                    for fc in range(32):
                        ins = t.transpose(pcol[:, 2 * fc:2 * fc + 1], sbwrow[0:1, fc * 128:(fc + 1) * 128],
                                          sidf1[0:1, 0:1])
                    ins.then_inc(fTtr2, 1)
                    for it in range(NIT):
                        n = NIT * e + it
                        t.wait_ge(fLf2, 128 * (n + 1))
                        for fc in range(32):
                            q = fc // 8
                            if fc % 8 == 0:
                                t.wait_ge(fVw, 4 * e + q + 1)
                            if 32 * n + fc >= 2:
                                t.wait_ge(fAh, 32 * n + fc - 1)
                            for kk in range(8):
                                ins = t.matmul(ph_[fc % 2], sW1[:, kk, fc * 128:(fc + 1) * 128],
                                               sf2t[it % 2][:, kk, :],
                                               start=(kk == 0), stop=(kk == 7))
                            ins.then_inc(fTh, 1)
                        if n >= 1:
                            t.wait_ge(fVy, n)
                        for sub in range(2):
                            for half in range(2):
                                t.matmul(po[2 * sub + half][:], sones16[:],
                                         sb2r[:, half * 512:(half + 1) * 512],
                                         start=True, stop=False)
                        for fc in range(32):
                            t.wait_ge(fAh, 32 * n + fc + 1)
                            if fc % 8 == 0:
                                t.wait_ge(fLw2, 64 * e + 16 * (fc // 8 + 1))
                            for sub in range(2):
                                for half in range(2):
                                    ins = t.matmul(po[2 * sub + half][:],
                                                   sh[:, fc, sub * 128:(sub + 1) * 128],
                                                   sW2[:, fc, half * 512:(half + 1) * 512],
                                                   start=False, stop=(fc == 31))
                            if fc % 8 == 7:
                                ins.then_inc(fTy, 1)

            @blk.scalar
            def _(s):
                f0w(s)
                for e in range(E):
                    for fg in range(8):
                        s.wait_ge(fTbc, 8 * e + fg + 1)
                        s.copy(sbwrow[0:1, fg * 512:(fg + 1) * 512], prow[fg % 2][:]).then_inc(fAb, 1)
                    for it in range(NIT):
                        n = NIT * e + it
                        s.wait_ge(fVb, e + 1)
                        for fc in range(32):
                            s.wait_ge(fTh, 32 * n + fc + 1)
                            s.activation(sh[:, fc, :], ph_[fc % 2], AF.Relu,
                                         bias=sbias[:, fc:fc + 1]).then_inc(fAh, 1)
    return [(fSxa, 16 * (T // 256) * E)]


def phase_l1_attn(env, x_in, x_out, fence):
    nc = env.nc
    d = env.d
    NT = T // 512
    NP = 8

    ln3cm = nc.dram_tensor("l1a_ln3cm", [8 * T, 128], BF16, kind="Internal").ap()
    qfm3 = nc.dram_tensor("l1a_qfm", [C, T], BF16, kind="Internal").ap()
    xb3 = nc.dram_tensor("l1a_xb", [T, C], F32, kind="Internal").ap()

    S = env.sem
    gLx = S("l1a_gLx"); gAsq = S("l1a_gAsq"); gVst = S("l1a_gVst")
    gAstd = S("l1a_gAstd"); gVln = S("l1a_gVln"); gVxb = S("l1a_gVxb")
    gSln = S("l1a_gSln"); gSxb = S("l1a_gSxb")
    hLtr = S("l1a_hLtr"); hLw = S("l1a_hLw"); hLwg = S("l1a_hLwg")
    hTq = S("l1a_hTq"); hAe = S("l1a_hAe"); hTst = S("l1a_hTst")
    hVr = S("l1a_hVr"); hAsd = S("l1a_hAsd"); hTb = S("l1a_hTb")
    hVn = S("l1a_hVn"); hTtr = S("l1a_hTtr"); hVtr = S("l1a_hVtr")
    hVrc = S("l1a_hVrc")
    hSq = S("l1a_hSq"); hTkv = S("l1a_hTkv"); hVkv = S("l1a_hVkv")
    iLq = S("l1a_iLq"); iLqs = S("l1a_iLqs"); iLtr = S("l1a_iLtr"); iLxb = S("l1a_iLxb")
    iTat = S("l1a_iTat"); iVat = S("l1a_iVat"); iTp1 = S("l1a_iTp1")
    iAh = S("l1a_iAh"); iTp2 = S("l1a_iTp2"); iVep = S("l1a_iVep")
    iSx3 = S("l1a_iSx3")

    def fw(eng):
        for sem_, val_ in fence:
            eng.wait_ge(sem_, val_)

    with ExitStack() as ph:
        sb = lambda n, s, dt=F32: ph.enter_context(nc.sbuf_tensor(n, s, dt))
        seps3 = sb("l1a_eps", [128, 1])
        skv16 = sb("l1a_skv16", [128, 8, 128], BF16)

        with ExitStack() as g0:
            asb = lambda n, s, dt=F32: g0.enter_context(nc.sbuf_tensor(n, s, dt))
            sx = [asb(f"l1a_sx{i}", [128, 4, C]) for i in range(2)]
            slnf = [asb(f"l1a_slnf{i}", [128, 4, C]) for i in range(2)]
            sln16 = [asb(f"l1a_sln16{i}", [128, 4, C], BF16) for i in range(2)]
            sxb = [asb(f"l1a_sxb{i}", [128, 4, C]) for i in range(2)]
            sqact = asb("l1a_sqact", [128, C])
            sstat = asb("l1a_sstat", [128, 28])
            sstd = asb("l1a_sstd", [128, 4])
            ssum = sstat[:, 0:4]; ssq = sstat[:, 4:8]; smc = sstat[:, 8:12]
            se2 = sstat[:, 12:16]; svar = sstat[:, 16:20]; srstd = sstat[:, 20:24]

            blk = g0.enter_context(nc.Block("l1a_G0", no_gpsimd_drain=True))

            @blk.sync
            def _(s):
                fw(s)
                for it in range(NT):
                    t0 = it * 512
                    if it >= 2:
                        s.wait_ge(gVln, it - 1)
                        s.wait_ge(gAsq, it - 1)
                    s.dma_start(out=sx[it % 2][:],
                                in_=x_in[t0:t0 + 512, :].rearrange("(s p) c -> p s c", p=128)
                                ).then_inc(gLx, 16)
                    s.wait_ge(gVln, it + 1)
                    for kk in range(8):
                        s.dma_start(out=ln3cm[kk * T + t0: kk * T + t0 + 512, :].rearrange("(s2 p) c -> p s2 c", p=128),
                                    in_=sln16[it % 2][:, :, kk * 128:(kk + 1) * 128]
                                    ).then_inc(gSln, 16)

            @blk.gpsimd
            def _(g):
                fw(g)
                for it in range(NT):
                    t0 = it * 512
                    g.wait_ge(gVxb, it + 1)
                    g.dma_start(out=xb3[t0:t0 + 512, :].rearrange("(s p) c -> p s c", p=128),
                                in_=sxb[it % 2][:]).then_inc(gSxb, 16)

            @blk.scalar
            def _(s):
                fw(s)
                for it in range(NT):
                    s.wait_ge(gLx, 16 * (it + 1))
                    if it >= 1:
                        s.wait_ge(gVst, it)
                    for ss in range(4):
                        ins = s.activation(sqact[:], sx[it % 2][:, ss, :], AF.Square,
                                           accum_out=ssq[:, ss:ss + 1])
                    ins.then_inc(gAsq, 1)
                    s.wait_ge(gVst, it + 1)
                    s.activation(sstd[:], svar[:], AF.Sqrt, bias=seps3[:], scale=-1.0).then_inc(gAstd, 1)

            @blk.vector
            def _(v):
                fw(v)
                v.memset(seps3[:], 1e-5)
                for it in range(NT):
                    xb = sx[it % 2]
                    v.wait_ge(gLx, 16 * (it + 1))
                    if it >= 2:
                        v.wait_ge(gSln, 128 * (it - 1))
                        v.wait_ge(gSxb, 16 * (it - 1))
                    for ss in range(4):
                        v.reduce_sum(ssum[:, ss:ss + 1], xb[:, ss, :], axis=AX)
                    v.wait_ge(gAsq, it + 1)
                    v.tensor_scalar_mul(smc[:], ssum[:], 1.0 / C)
                    v.tensor_scalar_mul(se2[:], ssq[:], 1.0 / C)
                    _tnop(v)
                    for ss in range(4):
                        ins = v.scalar_tensor_tensor(svar[:, ss:ss + 1], smc[:, ss:ss + 1],
                                                     smc[:, ss:ss + 1], se2[:, ss:ss + 1],
                                                     ALU.mult, ALU.subtract)
                    ins.then_inc(gVst, 1)
                    v.wait_ge(gAstd, it + 1)
                    v.reciprocal(srstd[:], sstd[:])
                    _tnop(v)
                    for ss in range(4):
                        v.tensor_scalar(slnf[it % 2][:, ss, :], xb[:, ss, :],
                                        smc[:, ss:ss + 1], srstd[:, ss:ss + 1],
                                        ALU.subtract, ALU.mult)
                    for ss in range(4):
                        ins = v.tensor_copy(sln16[it % 2][:, ss, :], slnf[it % 2][:, ss, :])
                    ins.then_inc(gVln, 1)
                    for ss in range(4):
                        ins = v.tensor_add(sxb[it % 2][:, ss, :], xb[:, ss, :], slnf[it % 2][:, ss, :])
                    ins.then_inc(gVxb, 1)

        with ExitStack() as g1:
            asb = lambda n, s, dt=F32: g1.enter_context(nc.sbuf_tensor(n, s, dt))
            aps = lambda n, s, dt=F32: g1.enter_context(nc.psum_tensor(n, s, dt))
            sktm = asb("l1a_sktm", [128, T // 128, C], BF16)
            svtm = asb("l1a_svtm", [128, T // 128, C], BF16)
            sqkvw = asb("l1a_qkvw", [128, 8, 3072], BF16)
            sqkvb = asb("l1a_sqkvb", [128, 24])
            slnxf = [asb("l1a_lnxf", [128, 8, 512], BF16)] * 2
            sk16st = [asb("l1a_k16st", [128, 2, 512], BF16)]
            sq16h = asb("l1a_sq16", [128, 4, 512], BF16)
            sjunk = asb("l1a_sjunk", [128, 64])
            smk = asb("l1a_smk", [128, 64])
            smv = asb("l1a_smv", [128, 64])
            sqk = asb("l1a_sqk", [128, 64])
            sqv = asb("l1a_sqv", [128, 64])
            sms = asb("l1a_sms", [128, 128])
            snegv = asb("l1a_snegv", [128, 128])
            sstd2 = asb("l1a_sstd2", [128, 128])
            srstd2 = asb("l1a_srstd2", [128, 128])
            sgain = asb("l1a_sgain", [128, 4, 1024], BF16)
            skvacc = asb("l1a_kvacc", [128, 8, 128])
            sksc = None
            sid16 = asb("l1a_id16", [128, 128], BF16)
            stemp = asb("l1a_stemp", [128, 8])
            satnb = sq16h[:].rearrange("p a b -> p (a b)")[:, 0:1024].rearrange("p (q e) -> p q e", q=8)
            pqkv = [aps(f"l1a_pqkv{i}", [128, 512]) for i in range(2)]
            ptr = [aps(f"l1a_ptr{i}", [128, 128], BF16) for i in range(2)]
            pkv = [aps(f"l1a_pkv{i}", [128, 128]) for i in range(2)]

            blk = g1.enter_context(nc.Block("l1a_G1", no_gpsimd_drain=True))
            G0F = [(gSln, 128 * NT), (gSxb, 16 * NT)]

            def g0w(eng):
                for sem_, val_ in G0F:
                    eng.wait_ge(sem_, val_)

            @blk.gpsimd
            def _(g):
                g0w(g)
                g.dma_start(out=sqkvw[:], in_=d["l1a_qkvwT"].rearrange("(k p) r -> p k r", p=128)).then_inc(hLwg, 16)
                g.dma_start(out=sid16[:], in_=d["idf32"]).then_inc(hLwg, 16)
                g.dma_start(out=sgain[:], in_=d["l1a_gaint"].rearrange("(q p) c -> p q c", p=128)).then_inc(hLwg, 16)
                def gains(g, it_):
                    for sub in range(4):
                        tcg = 4 * it_ + sub
                        g.wait_ge(hVn, tcg + 1)
                        g.tensor_mul(sktm[:, tcg, :], sktm[:, tcg, :], sgain[:, 0, :])
                        g.tensor_add(sktm[:, tcg, :], sktm[:, tcg, :], sgain[:, 1, :])
                        g.tensor_mul(svtm[:, tcg, :], svtm[:, tcg, :], sgain[:, 2, :])
                        ins = g.tensor_add(svtm[:, tcg, :], svtm[:, tcg, :], sgain[:, 3, :])
                        ins.then_inc(hVr, 1)

                for it in range(NT):
                    t0 = it * 512
                    for hf in range(2):
                        g.wait_ge(hAe, 24 * it + 4 * (hf + 1))
                        g.dma_start(out=qfm3.rearrange("(k p) t -> p k t", p=128)[:, 4 * hf:4 * (hf + 1), t0:t0 + 512],
                                    in_=sq16h[:]).then_inc(hSq, 16)
                    if it >= 1:
                        gains(g, it - 1)
                gains(g, NT - 1)
                g.wait_ge(hSq, 32 * NT)
                g.dma_start(out=satnb, in_=d["l1a_atnb"].rearrange("p (q e) -> p q e", q=8)).then_inc(hLwg, 16)

            @blk.sync
            def _(s):
                g0w(s)
                s.dma_start(out=sqkvb[:], in_=d["l1a_qkvb128"]).then_inc(hLw, 16)
                s.dma_start(out=stemp[:], in_=d["l1a_temp128"]).then_inc(hLw, 16)
                for it in range(NT):
                    t0 = it * 512
                    if it >= 1:
                        s.wait_ge(hTq, 24 * it)
                    for kk in range(8):
                        s.dma_start_transpose(slnxf[it % 2][:, kk, :],
                                              ln3cm[kk * T + t0: kk * T + t0 + 512, :]
                                              ).then_inc(hLtr, 16)

            def kv_mms(t, it_):
                t.wait_ge(hVr, 4 * (it_ + 1))
                for p in range(NP):
                    kvcnt = 8 * it_ + p
                    if kvcnt >= 2:
                        t.wait_ge(hVrc, kvcnt - 1)
                    for s4 in range(4):
                        tc = 4 * it_ + s4
                        ins = t.matmul(pkv[p % 2][:], sktm[:, tc, p * 128:(p + 1) * 128],
                                       svtm[:, tc, p * 128:(p + 1) * 128],
                                       start=(s4 == 0), stop=(s4 == 3))
                    ins.then_inc(hTkv, 1)

            @blk.tensor
            def _(t):
                g0w(t)
                t.wait_ge(hLwg, 48)
                t.wait_ge(hLw, 32)
                for it in range(NT):
                    t.wait_ge(hLtr, 128 * (it + 1))
                    if it >= 2:
                        kv_mms(t, it - 2)
                    lnf = slnxf[it % 2]
                    for rc in range(24):
                        cnt = 24 * it + rc
                        if cnt >= 2:
                            t.wait_ge(hAe, cnt - 1)
                        for kk in range(8):
                            ins = t.matmul(pqkv[cnt % 2][:], sqkvw[:, kk, rc * 128:(rc + 1) * 128],
                                           lnf[:, kk, :], start=(kk == 0), stop=(kk == 7))
                        ins.then_inc(hTq, 1)
                        if rc >= 8:
                            pcnt = it * 16 + (rc - 8)
                            t.wait_ge(hAe, cnt + 1)
                            for sub in range(4):
                                tcnt = 4 * pcnt + sub
                                if tcnt >= 2:
                                    t.wait_ge(hVtr, tcnt - 1)
                                t.transpose(ptr[tcnt % 2][:], sk16st[0][:, pcnt % 2, sub * 128:(sub + 1) * 128],
                                            sid16[:]).then_inc(hTtr, 1)
                kv_mms(t, NT - 2)
                kv_mms(t, NT - 1)

            @blk.scalar
            def _(s):
                g0w(s)
                for it in range(NT):
                    for rc in range(24):
                        cnt = 24 * it + rc
                        s.wait_ge(hTq, cnt + 1)
                        if rc < 8:
                            if rc % 4 == 0 and (it + rc) > 0:
                                s.wait_ge(hSq, 32 * it + (16 if rc == 4 else 0))
                            s.activation(sq16h[:, rc % 4, :], pqkv[cnt % 2][:], AF.Identity,
                                         bias=sqkvb[:, rc:rc + 1]).then_inc(hAe, 1)
                        else:
                            pcnt = it * 16 + (rc - 8)
                            if pcnt >= 2:
                                s.wait_ge(hTtr, 4 * (pcnt - 1))
                            s.activation(sk16st[0][:, pcnt % 2, :], pqkv[cnt % 2][:], AF.Identity,
                                         bias=sqkvb[:, rc:rc + 1]).then_inc(hAe, 1)
                    s.wait_ge(hVtr, 64 * (it + 1))
                    for sub in range(4):
                        tcg = 4 * it + sub
                        c0 = sub * 16
                        for h in range(16):
                            s.activation(sjunk[:], sktm[:, tcg, h * 64:(h + 1) * 64],
                                         AF.Square, accum_out=sqk[:, c0 + h:c0 + h + 1])
                        for h in range(16):
                            ins = s.activation(sjunk[:], svtm[:, tcg, h * 64:(h + 1) * 64],
                                               AF.Square, accum_out=sqv[:, c0 + h:c0 + h + 1])
                        ins.then_inc(hTst, 1)
                    s.wait_ge(hTb, it + 1)
                    s.activation(sstd2[:], snegv[:], AF.Sqrt, bias=seps3[:], scale=-1.0).then_inc(hAsd, 1)

            def kv_adds(v, it_):
                for p in range(NP):
                    kvcnt = 8 * it_ + p
                    v.wait_ge(hTkv, kvcnt + 1)
                    if it_ == 0:
                        ins = v.tensor_copy(skvacc[:, p, :], pkv[p % 2][:])
                    else:
                        ins = v.tensor_add(skvacc[:, p, :], skvacc[:, p, :], pkv[p % 2][:])
                    ins.then_inc(hVrc, 1)

            @blk.vector
            def _(v):
                g0w(v)
                v.wait_ge(hLwg, 48)
                for it in range(NT):
                    if it >= 2:
                        kv_adds(v, it - 2)
                    for rc in range(8, 24):
                        pcnt = it * 16 + (rc - 8)
                        is_v = rc >= 16
                        col0 = (rc - 8 if not is_v else rc - 16) * 128
                        dst = svtm if is_v else sktm
                        for sub in range(4):
                            tcnt = 4 * pcnt + sub
                            v.wait_ge(hTtr, tcnt + 1)
                            ins = v.tensor_copy(dst[:, 4 * it + sub, col0:col0 + 128],
                                                ptr[tcnt % 2][:])
                            ins.then_inc(hVtr, 1)
                    for sub in range(4):
                        tcg = 4 * it + sub
                        c0 = sub * 16
                        v.reduce_sum(smk[:, c0:c0 + 16], sktm[:, tcg, :].rearrange("p (h e) -> p h e", h=16), axis=AX)
                        v.reduce_sum(smv[:, c0:c0 + 16], svtm[:, tcg, :].rearrange("p (h e) -> p h e", h=16), axis=AX)
                    v.wait_ge(hTst, 4 * (it + 1))
                    v.tensor_scalar_mul(smk[:], smk[:], 1.0 / 64.0)
                    v.tensor_scalar_mul(smv[:], smv[:], 1.0 / 64.0)
                    v.tensor_scalar_mul(sqk[:], sqk[:], 1.0 / 64.0)
                    v.tensor_scalar_mul(sqv[:], sqv[:], 1.0 / 64.0)
                    _tnop(v)
                    v.tensor_mul(sms[:, 0:64], smk[:], smk[:])
                    v.tensor_mul(sms[:, 64:128], smv[:], smv[:])
                    v.tensor_sub(snegv[:, 0:64], sms[:, 0:64], sqk[:])
                    ins = v.tensor_sub(snegv[:, 64:128], sms[:, 64:128], sqv[:])
                    ins.then_inc(hTb, 1)
                    v.wait_ge(hAsd, it + 1)
                    v.reciprocal(srstd2[:], sstd2[:])
                    _tnop(v)
                    for sub in range(4):
                        tcg = 4 * it + sub
                        c0 = sub * 16
                        for h in range(16):
                            v.tensor_scalar(sktm[:, tcg, h * 64:(h + 1) * 64],
                                            sktm[:, tcg, h * 64:(h + 1) * 64],
                                            smk[:, c0 + h:c0 + h + 1], srstd2[:, c0 + h:c0 + h + 1],
                                            ALU.subtract, ALU.mult)
                        for h in range(15):
                            v.tensor_scalar(svtm[:, tcg, h * 64:(h + 1) * 64],
                                            svtm[:, tcg, h * 64:(h + 1) * 64],
                                            smv[:, c0 + h:c0 + h + 1], srstd2[:, 64 + c0 + h:64 + c0 + h + 1],
                                            ALU.subtract, ALU.mult)
                        ins = v.tensor_scalar(svtm[:, tcg, 15 * 64:16 * 64],
                                              svtm[:, tcg, 15 * 64:16 * 64],
                                              smv[:, c0 + 15:c0 + 16], srstd2[:, 64 + c0 + 15:64 + c0 + 16],
                                              ALU.subtract, ALU.mult)
                        ins.then_inc(hVn, 1)
                kv_adds(v, NT - 2)
                kv_adds(v, NT - 1)
                v.wait_ge(hLwg, 64)
                for p in range(NP):
                    v.tensor_scalar(sms[:], skvacc[:, p, :], stemp[:, p:p + 1], 1.0 / T,
                                    ALU.mult, ALU.mult)
                    _tnop(v)
                    ins = v.tensor_add(skv16[:, p, :], sms[:], satnb[:, p, :])
                    ins.then_inc(hVkv, 1)

        with ExitStack() as g2:
            asb = lambda n, s, dt=F32: g2.enter_context(nc.sbuf_tensor(n, s, dt))
            aps = lambda n, s, dt=F32: g2.enter_context(nc.psum_tensor(n, s, dt))
            sp1w = asb("l1a_p1w", [128, 8, C], BF16)
            sp2w = asb("l1a_p2w", [128, 8, C], BF16)
            sp1b = asb("l1a_sp1b", [128, 8])
            sp2b = asb("l1a_p2br", [1, C], BF16)
            sones16 = asb("l1a_ones16", [1, 128], BF16)
            sq16g = [asb(f"l1a_q16g{i}", [128, 8, 512], BF16) for i in range(2)]
            slnxg = [asb(f"l1a_lnxg{i}", [128, 8, 512], BF16) for i in range(2)]
            satt2 = asb("l1a_satt2", [128, 8, 512], BF16)
            sh1 = asb("l1a_sh1", [128, 8, 512], BF16)
            sxb2 = [asb(f"l1a_sxb2{i}", [128, 4, C]) for i in range(2)]
            sx3 = [asb(f"l1a_sx3{i}", [128, 4, C]) for i in range(2)]
            pat = [aps(f"l1a_pat{i}", [128, 512]) for i in range(2)]
            pp1 = [aps(f"l1a_pp1{i}", [128, 512]) for i in range(2)]
            po = [aps(f"l1a_po{i}", [128, 512]) for i in range(2)]

            blk = g2.enter_context(nc.Block("l1a_G2", no_gpsimd_drain=True))
            G1F = [(hVkv, NP), (hSq, 32 * NT)]

            def g1w(eng):
                for sem_, val_ in G1F:
                    eng.wait_ge(sem_, val_)

            @blk.gpsimd
            def _(g):
                g1w(g)
                g.dma_start(out=sp1w[:], in_=d["l1a_p1wT"].rearrange("(k p) r -> p k r", p=128)).then_inc(iLq, 16)
                g.dma_start(out=sp2w[:], in_=d["l1a_p2wT"].rearrange("(k p) r -> p k r", p=128)).then_inc(iLq, 16)
                g.dma_start(out=sp2b[:], in_=d["l1a_p2b"]).then_inc(iLq, 16)
                for it in range(NT):
                    t0 = it * 512
                    if it >= 2:
                        g.wait_ge(iVep, 4 * (it - 1))
                    g.dma_start(out=sxb2[it % 2][:],
                                in_=xb3[t0:t0 + 512, :].rearrange("(s p) c -> p s c", p=128)
                                ).then_inc(iLxb, 16)
                    g.wait_ge(iVep, 4 * (it + 1))
                    g.dma_start(out=x_out[t0:t0 + 512, :].rearrange("(s p) c -> p s c", p=128),
                                in_=sx3[it % 2][:]).then_inc(iSx3, 16)
                g.wait_ge(iSx3, 16 * NT)

            @blk.sync
            def _(s):
                g1w(s)
                s.dma_start(out=sp1b[:], in_=d["l1a_p1b128"]).then_inc(iLqs, 16)
                for it in range(NT):
                    t0 = it * 512
                    if it >= 2:
                        s.wait_ge(iTat, 8 * (it - 1))
                        s.wait_ge(iVat, 8 * (it - 1))
                    s.dma_start(out=sq16g[it % 2][:],
                                in_=qfm3.rearrange("(k p) t -> p k t", p=128)[:, :, t0:t0 + 512]
                                ).then_inc(iLqs, 16)
                    for kk in range(8):
                        s.dma_start_transpose(slnxg[it % 2][:, kk, :],
                                              ln3cm[kk * T + t0: kk * T + t0 + 512, :]
                                              ).then_inc(iLtr, 16)

            @blk.vector
            def _(v):
                g1w(v)
                v.memset(sones16[:], 1.0)
                for it in range(NT):
                    for p_i in range(NP):
                        acnt = 8 * it + p_i
                        v.wait_ge(iTat, acnt + 1)
                        if it >= 1:
                            v.wait_ge(iTp1, 8 * it)
                        v.wait_ge(iLtr, 128 * (it + 1))
                        ins = v.tensor_add(satt2[:, p_i, :], pat[acnt % 2][:],
                                           slnxg[it % 2][:, p_i, :])
                        ins.then_inc(iVat, 1)
                    v.wait_ge(iLxb, 16 * (it + 1))
                    if it >= 2:
                        v.wait_ge(iSx3, 16 * (it - 1))
                    for sub in range(4):
                        ocnt = 4 * it + sub
                        v.wait_ge(iTp2, ocnt + 1)
                        for half in range(2):
                            ins = v.tensor_add(sx3[it % 2][:, sub, half * 512:(half + 1) * 512],
                                               po[half][:],
                                               sxb2[it % 2][:, sub, half * 512:(half + 1) * 512])
                        ins.then_inc(iVep, 1)

            @blk.tensor
            def _(t):
                g1w(t)
                t.wait_ge(iLq, 48)
                t.wait_ge(iLqs, 16)
                for it in range(NT):
                    t.wait_ge(iLqs, 16 * (it + 2))
                    for p_i in range(NP):
                        acnt = 8 * it + p_i
                        if acnt >= 2:
                            t.wait_ge(iVat, acnt - 1)
                        dst = pat[acnt % 2]
                        t.matmul(dst[0:64, :], skv16[0:64, p_i, 0:64],
                                 sq16g[it % 2][0:64, p_i, :], start=True, stop=True)
                        t.matmul(dst[64:128, :], skv16[64:128, p_i, 64:128],
                                 sq16g[it % 2][64:128, p_i, :], start=True, stop=True,
                                 tile_position=(64, 64)).then_inc(iTat, 1)
                    for rc in range(8):
                        pcnt = 8 * it + rc
                        t.wait_ge(iVat, 8 * (it + 1))
                        if pcnt >= 2:
                            t.wait_ge(iAh, pcnt - 1)
                        for kk in range(8):
                            ins = t.matmul(pp1[pcnt % 2][:], sp1w[:, kk, rc * 128:(rc + 1) * 128],
                                           satt2[:, kk, :], start=(kk == 0), stop=(kk == 7))
                        ins.then_inc(iTp1, 1)
                    for sub in range(4):
                        ocnt = 4 * it + sub
                        t.wait_ge(iAh, 8 * (it + 1))
                        if ocnt >= 1:
                            t.wait_ge(iVep, ocnt)
                        for half in range(2):
                            t.matmul(po[half][:], sones16[:], sp2b[:, half * 512:(half + 1) * 512],
                                     start=True, stop=False)
                            for kk in range(8):
                                ins = t.matmul(po[half][:], sh1[:, kk, sub * 128:(sub + 1) * 128],
                                               sp2w[:, kk, half * 512:(half + 1) * 512],
                                               start=False, stop=(kk == 7))
                        ins.then_inc(iTp2, 1)

            @blk.scalar
            def _(s):
                g1w(s)
                for it in range(NT):
                    for rc in range(8):
                        pcnt = 8 * it + rc
                        s.wait_ge(iTp1, pcnt + 1)
                        s.activation(sh1[:, rc, :], pp1[pcnt % 2][:], AF.Gelu,
                                     bias=sp1b[:, rc:rc + 1]).then_inc(iAh, 1)
    return [(iSx3, 16 * NT)]


def phase_l1_ffn(env, x_in, x_out, fence):
    nc = env.nc
    d = env.d
    NT = T // 512
    NP = 8
    E = 8
    NI = T // 512

    lnacm = nc.dram_tensor("l1f_lnacm", [8 * T, 128], BF16, kind="Internal").ap()
    lnbcm = nc.dram_tensor("l1f_lnbcm", [8 * T, 128], BF16, kind="Internal").ap()
    q4fm = nc.dram_tensor("l1f_q4fm", [C, T], BF16, kind="Internal").ap()
    f4 = nc.dram_tensor("l1f_f4", [T, C], F32, kind="Internal").ap()

    S = env.sem
    jLx = S("l1f_jLx"); jAsq = S("l1f_jAsq"); jVst = S("l1f_jVst")
    jAstd = S("l1f_jAstd"); jV2 = S("l1f_jV2"); jA2s = S("l1f_jA2s")
    jVf = S("l1f_jVf"); jVln = S("l1f_jVln"); jSln = S("l1f_jSln"); jSf4 = S("l1f_jSf4")
    kLtr = S("l1f_kLtr"); kLw = S("l1f_kLw"); kTq = S("l1f_kTq")
    kAe = S("l1f_kAe"); kSq = S("l1f_kSq"); kTtr = S("l1f_kTtr"); kVtr = S("l1f_kVtr")
    lSq = S("l1f_lSq"); lTs = S("l1f_lTs"); lAe = S("l1f_lAe")
    lVmk = S("l1f_lVmk"); lTav = S("l1f_lTav"); lVr0 = S("l1f_lVr0")
    lVrr = S("l1f_lVrr")
    lTbr = S("l1f_lTbr"); lVav = S("l1f_lVav"); lTcp = S("l1f_lTcp")
    lVep = S("l1f_lVep"); lLf4 = S("l1f_lLf4"); lSa = S("l1f_lSa")
    lLw = S("l1f_lLw"); lLwg = S("l1f_lLwg")
    nLa = S("l1f_nLa"); nLx3 = S("l1f_nLx3"); nAsq = S("l1f_nAsq")
    nVst = S("l1f_nVst"); nAstd = S("l1f_nAstd"); nVln = S("l1f_nVln")
    nSlb = S("l1f_nSlb"); nTtr = S("l1f_nTtr"); nVtr = S("l1f_nVtr")
    nTlg = S("l1f_nTlg"); nAex = S("l1f_nAex"); nVrt = S("l1f_nVrt")
    nVxa = S("l1f_nVxa"); nSxa = S("l1f_nSxa"); nLw = S("l1f_nLw"); nLwg = S("l1f_nLwg")
    mLw = S("l1f_mLw"); mTh = S("l1f_mTh"); mAh = S("l1f_mAh")
    mTy = S("l1f_mTy"); mVy = S("l1f_mVy"); mLf2 = S("l1f_mLf2")
    mLxa = S("l1f_mLxa"); mSxa = S("l1f_mSxa")

    def fw(eng):
        for sem_, val_ in fence:
            eng.wait_ge(sem_, val_)

    with ExitStack() as ph:
        psb = lambda n, s, dt=F32: ph.enter_context(nc.sbuf_tensor(n, s, dt))
        seps4 = psb("l1f_eps", [128, 1])
        sroute4 = psb("l1f_route", [128, T // 128, E])

        with ExitStack() as h0:
            asb = lambda n, s, dt=F32: h0.enter_context(nc.sbuf_tensor(n, s, dt))
            sx = [asb(f"l1f_sx{i}", [128, 4, C]) for i in range(2)]
            sfeat = [asb(f"l1f_sft{i}", [128, 4, C]) for i in range(2)]
            sln16 = [asb(f"l1f_sln16{i}", [128, 4, C], BF16) for i in range(2)]
            sln1b = asb("l1f_sln1b", [128, C])
            sqact = asb("l1f_sqact", [128, C])
            sqd = asb("l1f_sqd", [128, C])
            sstat = asb("l1f_sstat", [128, 28])
            sstd = asb("l1f_sstd", [128, 4])
            ssum = sstat[:, 0:4]; ssq = sstat[:, 4:8]; smc = sstat[:, 8:12]
            se2 = sstat[:, 12:16]; svar = sstat[:, 16:20]; srstd = sstat[:, 20:24]

            blk = h0.enter_context(nc.Block("l1f_H0", no_gpsimd_drain=True))

            @blk.sync
            def _(s):
                fw(s)
                s.dma_start(out=sln1b[:], in_=d["l1f_ln1b"]).then_inc(jSln, 16)
                for it in range(NT):
                    t0 = it * 512
                    if it >= 2:
                        s.wait_ge(jVln, it - 1)
                        s.wait_ge(jAsq, it - 1)
                    s.dma_start(out=sx[it % 2][:],
                                in_=x_in[t0:t0 + 512, :].rearrange("(s p) c -> p s c", p=128)
                                ).then_inc(jLx, 16)
                    s.wait_ge(jVln, it + 1)
                    for kk in range(8):
                        s.dma_start(out=lnacm[kk * T + t0: kk * T + t0 + 512, :].rearrange("(s2 p) c -> p s2 c", p=128),
                                    in_=sln16[it % 2][:, :, kk * 128:(kk + 1) * 128]
                                    ).then_inc(jSln, 16)

            @blk.gpsimd
            def _(g):
                fw(g)
                for it in range(NT):
                    t0 = it * 512
                    g.wait_ge(jVf, it + 1)
                    g.dma_start(out=f4[t0:t0 + 512, :].rearrange("(s p) c -> p s c", p=128),
                                in_=sfeat[it % 2][:]).then_inc(jSf4, 16)

            @blk.scalar
            def _(s):
                fw(s)
                for it in range(NT):
                    s.wait_ge(jLx, 16 * (it + 1))
                    if it >= 1:
                        s.wait_ge(jVst, it)
                    for ss in range(4):
                        ins = s.activation(sqact[:], sx[it % 2][:, ss, :], AF.Square,
                                           accum_out=ssq[:, ss:ss + 1])
                    ins.then_inc(jAsq, 1)
                    s.wait_ge(jVst, it + 1)
                    s.activation(sstd[:], svar[:], AF.Sqrt, bias=seps4[:], scale=-1.0).then_inc(jAstd, 1)
                    s.wait_ge(jV2, it + 1)
                    s.activation(sstd[:], svar[:], AF.Sqrt, bias=seps4[:], scale=-1.0).then_inc(jA2s, 1)

            @blk.vector
            def _(v):
                fw(v)
                v.memset(seps4[:], 1e-5)
                for it in range(NT):
                    xb = sx[it % 2]
                    fb = sfeat[it % 2]
                    v.wait_ge(jLx, 16 * (it + 1))
                    if it >= 2:
                        v.wait_ge(jSf4, 16 * (it - 1))
                        v.wait_ge(jSln, 16 + 128 * (it - 1))
                    for ss in range(4):
                        v.reduce_sum(ssum[:, ss:ss + 1], xb[:, ss, :], axis=AX)
                    v.wait_ge(jAsq, it + 1)
                    v.tensor_scalar_mul(smc[:], ssum[:], 1.0 / C)
                    v.tensor_scalar_mul(se2[:], ssq[:], 1.0 / C)
                    _tnop(v)
                    for ss in range(4):
                        ins = v.scalar_tensor_tensor(svar[:, ss:ss + 1], smc[:, ss:ss + 1],
                                                     smc[:, ss:ss + 1], se2[:, ss:ss + 1],
                                                     ALU.mult, ALU.subtract)
                    ins.then_inc(jVst, 1)
                    v.wait_ge(jAstd, it + 1)
                    v.reciprocal(srstd[:], sstd[:])
                    _tnop(v)
                    for ss in range(4):
                        ins = v.tensor_scalar(fb[:, ss, :], xb[:, ss, :],
                                              smc[:, ss:ss + 1], srstd[:, ss:ss + 1],
                                              ALU.subtract, ALU.mult)
                    ins.then_inc(jVf, 1)
                    for ss in range(4):
                        v.reduce_sum(ssum[:, ss:ss + 1], fb[:, ss, :], axis=AX)
                        v.tensor_mul(sqd[:], fb[:, ss, :], fb[:, ss, :])
                        v.reduce_sum(ssq[:, ss:ss + 1], sqd[:], axis=AX)
                    v.tensor_scalar_mul(smc[:], ssum[:], 1.0 / C)
                    v.tensor_scalar_mul(se2[:], ssq[:], 1.0 / C)
                    _tnop(v)
                    for ss in range(4):
                        ins = v.scalar_tensor_tensor(svar[:, ss:ss + 1], smc[:, ss:ss + 1],
                                                     smc[:, ss:ss + 1], se2[:, ss:ss + 1],
                                                     ALU.mult, ALU.subtract)
                    ins.then_inc(jV2, 1)
                    v.wait_ge(jA2s, it + 1)
                    v.reciprocal(srstd[:], sstd[:])
                    _tnop(v)
                    v.wait_ge(jSln, 16)
                    for ss in range(4):
                        v.tensor_scalar(sqd[:], fb[:, ss, :],
                                        smc[:, ss:ss + 1], srstd[:, ss:ss + 1],
                                        ALU.subtract, ALU.mult)
                        ins = v.tensor_mul(sln16[it % 2][:, ss, :], sqd[:], sln1b[:])
                    ins.then_inc(jVln, 1)

        with ExitStack() as h1o:
            osb = lambda n, s, dt=F32: h1o.enter_context(nc.sbuf_tensor(n, s, dt))
            sk_fm = osb("l1f_skfm", [128, 8, T], BF16)
            sv_ext = osb("l1f_svext", [128, T // 128, 16, 65], BF16)
            ssel2b4 = osb("l1f_sel2b", [2, 128])
            sid16 = osb("l1f_id16", [128, 128], BF16)

            with ExitStack() as h1a:
                asb = lambda n, s, dt=F32: h1a.enter_context(nc.sbuf_tensor(n, s, dt))
                aps = lambda n, s, dt=F32: h1a.enter_context(nc.psum_tensor(n, s, dt))
                scattn = asb("l1f_scattn", [128, 8, 3072], BF16)
                slnxf = [asb("l1f_lnxf", [128, 8, 512], BF16)] * 2
                sq16h = asb("l1f_sq16", [128, 4, 512], BF16)
                sv16 = [asb(f"l1f_sv16{i}", [128, 512], BF16) for i in range(2)]
                pqkv = [aps(f"l1f_pqkv{i}", [128, 512]) for i in range(2)]
                ptr = [aps(f"l1f_ptr{i}", [128, 128], BF16) for i in range(2)]

                blk = h1a.enter_context(nc.Block("l1f_H1a", no_gpsimd_drain=True))
                H0F = [(jSln, 16 + 128 * NT), (jSf4, 16 * NT)]

                def h0w(eng):
                    for sem_, val_ in H0F:
                        eng.wait_ge(sem_, val_)

                @blk.gpsimd
                def _(g):
                    h0w(g)
                    g.dma_start(out=scattn[:], in_=d["l1f_cattn"].rearrange("(k p) r -> p k r", p=128)).then_inc(kLw, 16)
                    g.dma_start(out=sid16[:], in_=d["idf32"]).then_inc(kLw, 16)
                    g.dma_start(out=ssel2b4[:], in_=d["l1a_sel2b"]).then_inc(kLw, 16)
                    for it in range(NT):
                        t0 = it * 512
                        for hf in range(2):
                            g.wait_ge(kAe, 24 * it + 4 * (hf + 1))
                            g.dma_start(out=q4fm.rearrange("(k p) t -> p k t", p=128)[:, 4 * hf:4 * (hf + 1), t0:t0 + 512],
                                        in_=sq16h[:]).then_inc(kSq, 16)

                @blk.sync
                def _(s):
                    h0w(s)
                    for it in range(NT):
                        t0 = it * 512
                        if it >= 1:
                            s.wait_ge(kTq, 24 * it)
                        for kk in range(8):
                            s.dma_start_transpose(slnxf[it % 2][:, kk, :],
                                                  lnacm[kk * T + t0: kk * T + t0 + 512, :]
                                                  ).then_inc(kLtr, 16)

                @blk.tensor
                def _(t):
                    h0w(t)
                    t.wait_ge(kLw, 48)
                    for it in range(NT):
                        t.wait_ge(kLtr, 128 * (it + 1))
                        lnf = slnxf[it % 2]
                        for rc in range(24):
                            cnt = 24 * it + rc
                            if cnt >= 2:
                                t.wait_ge(kAe, cnt - 1)
                            for kk in range(8):
                                ins = t.matmul(pqkv[cnt % 2][:], scattn[:, kk, rc * 128:(rc + 1) * 128],
                                               lnf[:, kk, :], start=(kk == 0), stop=(kk == 7))
                            ins.then_inc(kTq, 1)
                            if rc >= 16:
                                pcnt = it * 8 + (rc - 16)
                                t.wait_ge(kAe, cnt + 1)
                                for sub in range(4):
                                    tcnt = 4 * pcnt + sub
                                    if tcnt >= 2:
                                        t.wait_ge(kVtr, tcnt - 1)
                                    t.transpose(ptr[tcnt % 2][:], sv16[pcnt % 2][:, sub * 128:(sub + 1) * 128],
                                                sid16[:]).then_inc(kTtr, 1)

                @blk.scalar
                def _(s):
                    h0w(s)
                    for it in range(NT):
                        for rc in range(24):
                            cnt = 24 * it + rc
                            s.wait_ge(kTq, cnt + 1)
                            if rc < 8:
                                if rc % 4 == 0 and (it + rc) > 0:
                                    s.wait_ge(kSq, 32 * it + (16 if rc == 4 else 0))
                                s.copy(sq16h[:, rc % 4, :], pqkv[cnt % 2][:]).then_inc(kAe, 1)
                            elif rc < 16:
                                s.copy(sk_fm[:, rc - 8, it * 512:(it + 1) * 512],
                                       pqkv[cnt % 2][:]).then_inc(kAe, 1)
                            else:
                                pcnt = it * 8 + (rc - 16)
                                if pcnt >= 2:
                                    s.wait_ge(kTtr, 4 * (pcnt - 1))
                                s.copy(sv16[pcnt % 2][:], pqkv[cnt % 2][:]).then_inc(kAe, 1)

                @blk.vector
                def _(v):
                    h0w(v)
                    v.memset(sv_ext[:], 1.0)
                    for it in range(NT):
                        for pp in range(8):
                            pcnt = it * 8 + pp
                            for sub in range(4):
                                tcnt = 4 * pcnt + sub
                                v.wait_ge(kTtr, tcnt + 1)
                                ins = v.tensor_copy(
                                    sv_ext[:, 4 * it + sub, 2 * pp:2 * pp + 2, 0:64],
                                    ptr[tcnt % 2][:].rearrange("p (h e) -> p h e", h=2))
                                ins.then_inc(kVtr, 1)

            with ExitStack() as h1b:
                asb = lambda n, s, dt=F32: h1b.enter_context(nc.sbuf_tensor(n, s, dt))
                aps = lambda n, s, dt=F32: h1b.enter_context(nc.psum_tensor(n, s, dt))
                scproj = asb("l1f_scproj", [128, 8, C], BF16)
                smask = asb("l1f_smasks", [128, 4, 1024], BF16)
                sq4 = [asb("l1f_sq4", [128, 8, 512], BF16)] * 2
                spt = asb("l1f_spt", [128, 8, 512], BF16)
                sew = asb("l1f_sew", [128, 2, 1024], BF16)
                savp = asb("l1f_savp", [128, 512])
                srr0 = asb("l1f_srr0", [1, 512])
                srr1 = asb("l1f_srr1", [1, 512])
                sselB = asb("l1f_selB", [1, 128])
                savn = asb("l1f_savn", [128, 8, 512], BF16)
                sf4t = [asb("l1f_sf4t", [128, 4, C])] * 2
                psS = [aps(f"l1f_psS{i}", [128, 1024]) for i in range(2)]
                pav = [aps(f"l1f_pav{i}", [128, 512]) for i in range(2)]
                pbr = aps("l1f_pbr", [128, 512])
                pcp = aps("l1f_pcp", [128, 512])

                blk = h1b.enter_context(nc.Block("l1f_H1b", no_gpsimd_drain=True))
                H1AF = [(kSq, 32 * NT), (kVtr, 32 * NT), (kAe, 24 * NT)]

                def h1aw(eng):
                    for sem_, val_ in H1AF:
                        eng.wait_ge(sem_, val_)

                njts = [4 * (I + 1) for I in range(NI)]
                gcum = []
                gtot = 0
                for I in range(NI):
                    row = []
                    for p in range(NP):
                        row.append(gtot)
                        gtot += njts[I]
                    gcum.append(row)
                PGT = NI * NP
                SUBT = 4 * NI

                @blk.sync
                def _(s):
                    h1aw(s)
                    s.dma_start(out=sselB[:], in_=d["l1a_sel2b"][1:2, :]).then_inc(lLw, 16)
                    for I in range(NI):
                        if I >= 1:
                            s.wait_ge(lTs, gcum[I - 1][0] + njts[I - 1] * NP)
                        s.dma_start(out=sq4[I % 2][:],
                                    in_=q4fm.rearrange("(k p) t -> p k t", p=128)[:, :, I * 512:(I + 1) * 512]
                                    ).then_inc(lSq, 16)

                @blk.gpsimd
                def _(g):
                    h1aw(g)
                    g.dma_start(out=scproj[:], in_=d["l1f_cproj"].rearrange("(k p) r -> p k r", p=128)).then_inc(lLwg, 16)
                    g.dma_start(out=smask[:], in_=d["l1f_masks"].rearrange("p (s c) -> p s c", s=4)).then_inc(lLwg, 16)
                    for I in range(NI):
                        t0 = I * 512
                        if I >= 1:
                            g.wait_ge(lSa, 16 * I)
                        g.dma_start(out=sf4t[I % 2][:],
                                    in_=f4[t0:t0 + 512, :].rearrange("(s p) c -> p s c", p=128)
                                    ).then_inc(lLf4, 16)
                        g.wait_ge(lVep, 4 * (I + 1))
                        g.dma_start(out=f4[t0:t0 + 512, :].rearrange("(s p) c -> p s c", p=128),
                                    in_=sf4t[I % 2][:]).then_inc(lSa, 16)
                    g.wait_ge(lSa, 16 * NI)

                @blk.tensor
                def _(t):
                    h1aw(t)
                    t.wait_ge(lLw, 16)
                    t.wait_ge(lLwg, 32)
                    pend = []

                    def emit_pbr(t, pgx):
                        t.wait_ge(lVrr, pgx + 1)
                        t.matmul(pbr[:], ssel2b4[0:1, :], srr0[:], start=True, stop=False)
                        t.matmul(pbr[:], sselB[:], srr1[:], start=False, stop=True).then_inc(lTbr, 1)

                    for I in range(NI):
                        t.wait_ge(lSq, 16 * (I + 1))
                        njt = njts[I]
                        for p in range(NP):
                            pg = 8 * I + p
                            g0 = gcum[I][p]

                            def do_av(jt):
                                g_i = g0 + jt
                                t.wait_ge(lAe, g_i + 1)
                                if jt >= 4 * I:
                                    t.wait_ge(lVmk, 8 * pg + 2 * (jt - 4 * I + 1))
                                if jt == 0 and pg >= 1:
                                    t.wait_ge(lVr0, pg)
                                t.matmul(pav[0][0:65, :], sv_ext[:, jt, 2 * p, :],
                                         spt[:, 2 * (g_i % 4), :], start=(jt == 0), stop=(jt == njt - 1))
                                t.matmul(pav[1][0:65, :], sv_ext[:, jt, 2 * p + 1, :],
                                         spt[:, 2 * (g_i % 4) + 1, :],
                                         start=(jt == 0), stop=(jt == njt - 1)).then_inc(lTav, 1)

                            for jt in range(njt):
                                g_i = g0 + jt
                                if g_i >= 2:
                                    t.wait_ge(lAe, g_i - 1)
                                t.matmul(psS[g_i % 2][:, 0:512], sk_fm[0:64, p, jt * 128:(jt + 1) * 128],
                                         sq4[I % 2][0:64, p, :], start=True, stop=True)
                                t.matmul(psS[g_i % 2][:, 512:1024], sk_fm[64:128, p, jt * 128:(jt + 1) * 128],
                                         sq4[I % 2][64:128, p, :], start=True, stop=True,
                                         tile_position=(64, 0)).then_inc(lTs, 1)
                                if jt >= 1:
                                    do_av(jt - 1)
                                if jt == 1 and pend:
                                    emit_pbr(t, pend.pop())
                            do_av(njt - 1)
                            pend.append(pg)
                        while pend:
                            emit_pbr(t, pend.pop())
                        t.wait_ge(lVav, 8 * (I + 1))
                        for sub in range(4):
                            scnt = 4 * I + sub
                            if scnt >= 1:
                                t.wait_ge(lVep, scnt)
                            for half in range(2):
                                for p in range(NP):
                                    ins = t.matmul(pcp[:] if half == 0 else pbr[:],
                                                   savn[:, p, sub * 128:(sub + 1) * 128],
                                                   scproj[:, p, half * 512:(half + 1) * 512],
                                                   start=(p == 0), stop=(p == 7))
                            ins.then_inc(lTcp, 1)

                @blk.scalar
                def _(s):
                    h1aw(s)
                    for I in range(NI):
                        njt = njts[I]
                        for p in range(NP):
                            pg = 8 * I + p
                            g0 = gcum[I][p]
                            for jt in range(njt):
                                g_i = g0 + jt
                                s.wait_ge(lTs, g_i + 1)
                                if g_i >= 4:
                                    s.wait_ge(lTav, g_i - 3)
                                diag = jt >= 4 * I
                                if diag:
                                    if jt - 4 * I >= 2:
                                        s.wait_ge(lVmk, 8 * (8 * I + p) + 2 * (jt - 4 * I - 1))
                                    ins = s.activation(sew[:, g_i % 2, :], psS[g_i % 2][:], AF.Exp, scale=0.125)
                                else:
                                    ins = s.activation(spt[:, 2 * (g_i % 4):2 * (g_i % 4) + 2, :],
                                                       psS[g_i % 2][:], AF.Exp, scale=0.125)
                                ins.then_inc(lAe, 1)

                @blk.vector
                def _(v):
                    h1aw(v)
                    v.wait_ge(lLwg, 32)
                    pendv = []

                    def emit_savn(v, pp_, ppg_):
                        v.wait_ge(lTbr, ppg_ + 1)
                        ins = v.tensor_mul(savn[:, pp_, :], savp[:], pbr[:])
                        ins.then_inc(lVav, 1)

                    for I in range(NI):
                        njt = njts[I]
                        for p in range(NP):
                            g0 = gcum[I][p]
                            for jt in range(njt):
                                g_i = g0 + jt
                                if jt >= 4 * I:
                                    v.wait_ge(lAe, g_i + 1)
                                    mk = smask[:, jt - 4 * I, :].rearrange("p (h c) -> p h c", h=2)
                                    ins = v.tensor_mul(spt[:, 2 * (g_i % 4):2 * (g_i % 4) + 2, :],
                                                       sew[:, g_i % 2, :].rearrange("p (h c) -> p h c", h=2), mk)
                                    ins.then_inc(lVmk, 2)
                                if jt == 1 and pendv:
                                    emit_savn(v, *pendv.pop())
                            pg = 8 * I + p
                            v.wait_ge(lTav, g0 + njt)
                            v.tensor_copy(savp[0:64, :], pav[0][0:64, :])
                            v.tensor_copy(savp[64:128, :], pav[1][0:64, :])
                            v.tensor_copy(srr0[:], pav[0][64:65, :])
                            ins = v.tensor_copy(srr1[:], pav[1][64:65, :])
                            ins.then_inc(lVr0, 1)
                            _tnop(v)
                            v.reciprocal(srr0[:], srr0[:])
                            ins = v.reciprocal(srr1[:], srr1[:])
                            ins.then_inc(lVrr, 1)
                            pendv.append((p, pg))
                        while pendv:
                            emit_savn(v, *pendv.pop())
                        v.wait_ge(lLf4, 16 * (I + 1))
                        for sub in range(4):
                            scnt = 4 * I + sub
                            v.wait_ge(lTcp, scnt + 1)
                            v.tensor_add(sf4t[I % 2][:, sub, 0:512], pcp[:],
                                         sf4t[I % 2][:, sub, 0:512])
                            ins = v.tensor_add(sf4t[I % 2][:, sub, 512:1024], pbr[:],
                                               sf4t[I % 2][:, sub, 512:1024])
                            ins.then_inc(lVep, 1)
        _phase_l1_ffn_tail(env, x_in, x_out, f4, lnbcm, sroute4, seps4,
                           [(lSa, 16 * NI)],
                           dict(nLa=nLa, nLx3=nLx3, nAsq=nAsq, nVst=nVst, nAstd=nAstd,
                                nVln=nVln, nSlb=nSlb, nTtr=nTtr, nVtr=nVtr, nTlg=nTlg,
                                nAex=nAex, nVrt=nVrt, nVxa=nVxa, nSxa=nSxa, nLw=nLw, nLwg=nLwg,
                                mLw=mLw, mTh=mTh, mAh=mAh, mTy=mTy, mVy=mVy,
                                mLf2=mLf2, mLxa=mLxa, mSxa=mSxa))
    return [(mSxa, 16 * (T // 256) * E)]


def _phase_l1_ffn_tail(env, x_in, x_out, f4, lnbcm, sroute4, seps4, fence, sm):
    nc = env.nc
    d = env.d
    NT = T // 512
    E = 8
    NIT = T // 256
    nLa = sm["nLa"]; nLx3 = sm["nLx3"]; nAsq = sm["nAsq"]; nVst = sm["nVst"]
    nAstd = sm["nAstd"]; nVln = sm["nVln"]; nSlb = sm["nSlb"]; nTtr = sm["nTtr"]
    nVtr = sm["nVtr"]; nTlg = sm["nTlg"]; nAex = sm["nAex"]; nVrt = sm["nVrt"]
    nVxa = sm["nVxa"]; nSxa = sm["nSxa"]; nLw = sm["nLw"]; nLwg = sm["nLwg"]
    mLw = sm["mLw"]; mTh = sm["mTh"]; mAh = sm["mAh"]; mTy = sm["mTy"]
    mVy = sm["mVy"]; mLf2 = sm["mLf2"]; mLxa = sm["mLxa"]; mSxa = sm["mSxa"]

    with ExitStack() as h2:
        asb = lambda n, s, dt=F32: h2.enter_context(nc.sbuf_tensor(n, s, dt))
        aps = lambda n, s, dt=F32: h2.enter_context(nc.psum_tensor(n, s, dt))
        sa = [asb(f"l1f_h2a{i}", [128, 4, C]) for i in range(2)]
        sx3t = [asb(f"l1f_h2x{i}", [128, 4, C]) for i in range(2)]
        slnb = [asb(f"l1f_h2l{i}", [128, 4, C], BF16) for i in range(2)]
        sxai = [asb(f"l1f_h2xa{i}", [128, 4, C]) for i in range(2)]
        sln2b = asb("l1f_sln2b", [128, C])
        srw16 = asb("l1f_srw16", [128, 8, E], BF16)
        sid16b = asb("l1f_id16b", [128, 128], BF16)
        strx = [asb(f"l1f_h2trx{i}", [128, 128], BF16) for i in range(2)]
        sqact = asb("l1f_h2sq", [128, C])
        sqd = asb("l1f_h2sqd", [128, C])
        sstat = asb("l1f_h2st", [128, 28])
        sstd = asb("l1f_h2sd", [128, 4])
        slg = asb("l1f_h2lg", [128, E])
        srr = asb("l1f_h2rr", [128, 8])
        swm = asb("l1f_h2wm", [128, 2, E])
        ptrb = [aps(f"l1f_h2ptr{i}", [128, 128], BF16) for i in range(2)]
        plg = aps("l1f_h2plg", [128, E])
        ssum = sstat[:, 0:4]; ssq = sstat[:, 4:8]; smc = sstat[:, 8:12]
        se2 = sstat[:, 12:16]; svar = sstat[:, 16:20]; srstd = sstat[:, 20:24]

        blk = h2.enter_context(nc.Block("l1f_H2", no_gpsimd_drain=True))

        def fw2(eng):
            for sem_, val_ in fence:
                eng.wait_ge(sem_, val_)

        @blk.sync
        def _(s):
            fw2(s)
            s.dma_start(out=sln2b[:], in_=d["l1f_ln2b"]).then_inc(nLw, 16)
            for it in range(NT):
                t0 = it * 512
                if it >= 2:
                    s.wait_ge(nVln, it - 1)
                    s.wait_ge(nAsq, it - 1)
                s.dma_start(out=sa[it % 2][:],
                            in_=f4[t0:t0 + 512, :].rearrange("(s p) c -> p s c", p=128)
                            ).then_inc(nLa, 16)
                s.wait_ge(nVln, it + 1)
                for kk in range(8):
                    s.dma_start(out=lnbcm[kk * T + t0: kk * T + t0 + 512, :].rearrange("(s2 p) c -> p s2 c", p=128),
                                in_=slnb[it % 2][:, :, kk * 128:(kk + 1) * 128]
                                ).then_inc(nSlb, 16)

        @blk.gpsimd
        def _(g):
            fw2(g)
            g.dma_start(out=srw16[:], in_=d["l1f_router"].rearrange("(k p) e -> p k e", p=128),
                        allow_slow_non_contiguous=True).then_inc(nLwg, 16)
            g.dma_start(out=sid16b[:], in_=d["idf32"]).then_inc(nLwg, 16)
            for it in range(NT):
                t0 = it * 512
                if it >= 2:
                    g.wait_ge(nVxa, it - 1)
                g.dma_start(out=sx3t[it % 2][:],
                            in_=x_in[t0:t0 + 512, :].rearrange("(s p) c -> p s c", p=128)
                            ).then_inc(nLx3, 16)
                g.wait_ge(nVxa, it + 1)
                g.dma_start(out=x_out[t0:t0 + 512, :].rearrange("(s p) c -> p s c", p=128),
                            in_=sxai[it % 2][:]).then_inc(nSxa, 16)

        @blk.scalar
        def _(s):
            fw2(s)
            for it in range(NT):
                s.wait_ge(nLa, 16 * (it + 1))
                if it >= 1:
                    s.wait_ge(nVst, it)
                for ss in range(4):
                    ins = s.activation(sqact[:], sa[it % 2][:, ss, :], AF.Square,
                                       accum_out=ssq[:, ss:ss + 1])
                ins.then_inc(nAsq, 1)
                s.wait_ge(nVst, it + 1)
                s.activation(sstd[:], svar[:], AF.Sqrt, bias=seps4[:], scale=-1.0).then_inc(nAstd, 1)
                for su in range(4):
                    gidx = 4 * it + su
                    s.wait_ge(nVrt, 3 * gidx + 1)
                    s.activation(srr[:, 3:4], srr[:, 1:2], AF.Exp,
                                 bias=srr[:, 2:3]).then_inc(nAex, 1)

        @blk.vector
        def _(v):
            fw2(v)
            for it in range(NT):
                ab = sa[it % 2]
                v.wait_ge(nLa, 16 * (it + 1))
                if it >= 2:
                    v.wait_ge(nSlb, 128 * (it - 1))
                    v.wait_ge(nSxa, 16 * (it - 1))
                    v.wait_ge(nTlg, 4 * (it - 1))
                for ss in range(4):
                    v.reduce_sum(ssum[:, ss:ss + 1], ab[:, ss, :], axis=AX)
                v.wait_ge(nAsq, it + 1)
                v.tensor_scalar_mul(smc[:], ssum[:], 1.0 / C)
                v.tensor_scalar_mul(se2[:], ssq[:], 1.0 / C)
                _tnop(v)
                for ss in range(4):
                    ins = v.scalar_tensor_tensor(svar[:, ss:ss + 1], smc[:, ss:ss + 1],
                                                 smc[:, ss:ss + 1], se2[:, ss:ss + 1],
                                                 ALU.mult, ALU.subtract)
                ins.then_inc(nVst, 1)
                v.wait_ge(nAstd, it + 1)
                v.reciprocal(srstd[:], sstd[:])
                _tnop(v)
                v.wait_ge(nLw, 16)
                for ss in range(4):
                    v.tensor_scalar(sqd[:], ab[:, ss, :],
                                    smc[:, ss:ss + 1], srstd[:, ss:ss + 1],
                                    ALU.subtract, ALU.mult)
                    ins = v.tensor_mul(slnb[it % 2][:, ss, :], sqd[:], sln2b[:])
                ins.then_inc(nVln, 1)
                v.wait_ge(nLx3, 16 * (it + 1))
                for ss in range(4):
                    ins = v.tensor_add(sxai[it % 2][:, ss, :], sx3t[it % 2][:, ss, :],
                                       ab[:, ss, :])
                ins.then_inc(nVxa, 1)
                for su in range(4):
                    gidx = 4 * it + su
                    for kk in range(8):
                        tcnt = 8 * gidx + kk
                        v.wait_ge(nTtr, tcnt + 1)
                        v.tensor_copy(strx[tcnt % 2][:], ptrb[tcnt % 2][:]).then_inc(nVtr, 1)
                    v.wait_ge(nTlg, gidx + 1)
                    v.tensor_copy(slg[:], plg[:])
                    _tnop(v)
                    v.reduce_max(srr[:, 0:1], slg[:], axis=AX)
                    _tnop(v)
                    v.tensor_scalar(swm[:, 0, :], slg[:], srr[:, 0:1], None, ALU.is_equal)
                    v.tensor_scalar(srr[:, 2:3], srr[:, 0:1], -1.0, None, ALU.mult)
                    _tnop(v)
                    v.scalar_tensor_tensor(sqd[:, 0:E], swm[:, 0, :], -1e30, slg[:],
                                           ALU.mult, ALU.add)
                    _tnop(v)
                    ins = v.reduce_max(srr[:, 1:2], sqd[:, 0:E], axis=AX)
                    ins.then_inc(nVrt, 1)
                    _tnop(v)
                    v.tensor_scalar(swm[:, 1, :], sqd[:, 0:E], srr[:, 1:2], None, ALU.is_equal)
                    v.wait_ge(nAex, gidx + 1)
                    v.tensor_scalar(srr[:, 4:5], srr[:, 3:4], 1.0, None, ALU.add)
                    _tnop(v)
                    v.reciprocal(srr[:, 5:6], srr[:, 4:5])
                    _tnop(v)
                    v.tensor_mul(srr[:, 6:7], srr[:, 3:4], srr[:, 5:6])
                    _tnop(v)
                    v.tensor_scalar(sqd[:, 0:E], swm[:, 0, :], srr[:, 5:6], None, ALU.mult)
                    _tnop(v)
                    ins = v.scalar_tensor_tensor(sroute4[:, gidx, :], swm[:, 1, :],
                                                 srr[:, 6:7], sqd[:, 0:E],
                                                 ALU.mult, ALU.add)
                    ins.then_inc(nVrt, 2)

        @blk.tensor
        def _(t):
            fw2(t)
            t.wait_ge(nLwg, 32)
            for it in range(NT):
                t.wait_ge(nVln, it + 1)
                for su in range(4):
                    gidx = 4 * it + su
                    for kk in range(8):
                        tcnt = 8 * gidx + kk
                        if tcnt >= 2:
                            t.wait_ge(nVtr, tcnt - 1)
                        t.transpose(ptrb[tcnt % 2][:], slnb[it % 2][:, su, kk * 128:(kk + 1) * 128],
                                    sid16b[:]).then_inc(nTtr, 1)
                        t.wait_ge(nVtr, tcnt + 1)
                        ins = t.matmul(plg[:], strx[tcnt % 2][:], srw16[:, kk, :],
                                       start=(kk == 0), stop=(kk == 7))
                    ins.then_inc(nTlg, 1)


    with ExitStack() as h3:
        asb = lambda n, s, dt=F32: h3.enter_context(nc.sbuf_tensor(n, s, dt))
        aps = lambda n, s, dt=F32: h3.enter_context(nc.psum_tensor(n, s, dt))
        sW1m = [asb(f"l1f_w1m{i}", [128, 8, 2048], BF16) for i in range(2)]
        sW2m = [asb(f"l1f_w2m{i}", [128, 16, C], BF16) for i in range(2)]
        sh = asb("l1f_h3h", [128, 16, 256], BF16)
        sf2t = [asb(f"l1f_h3f{i}", [128, 8, 256], BF16) for i in range(2)]
        sxa = [asb(f"l1f_h3xa{i}", [128, 2, C]) for i in range(2)]
        ph_ = [aps(f"l1f_h3ph{i}", [128, 256]) for i in range(2)]
        po = [aps(f"l1f_h3po{i}", [128, 512]) for i in range(4)]

        blk = h3.enter_context(nc.Block("l1f_H3", no_gpsimd_drain=True))
        H2F = [(nSlb, 128 * NT), (nSxa, 16 * NT), (nVrt, 12 * NT)]

        def h2w(eng):
            for sem_, val_ in H2F:
                eng.wait_ge(sem_, val_)

        @blk.gpsimd
        def _(g):
            h2w(g)
            for e in range(E):
                if e >= 2:
                    g.wait_ge(mTy, NIT * (e - 1))
                g.dma_start(out=sW1m[e % 2][:],
                            in_=d["l1f_w1"][:, e * 2048:(e + 1) * 2048].rearrange("(k p) f -> p k f", p=128)
                            ).then_inc(mLw, 16)
                g.dma_start(out=sW2m[e % 2][:],
                            in_=d["l1f_w2"][e * 2048:(e + 1) * 2048, :].rearrange("(k p) dd -> p k dd", p=128)
                            ).then_inc(mLw, 16)

        @blk.sync
        def _(s):
            h2w(s)
            NTOT = NIT * E

            def f2load(n):
                ee, ii = divmod(n, NIT)
                t0 = ii * 256
                if n >= 2:
                    s.wait_ge(mTh, 16 * (n - 1))
                for kk in range(8):
                    s.dma_start_transpose(sf2t[ii % 2][:, kk, :],
                                          lnbcm[kk * T + t0: kk * T + t0 + 256, :]
                                          ).then_inc(mLf2, 16)

            for e in range(E):
                for it in range(NIT):
                    n = NIT * e + it
                    t0 = it * 256
                    if n == 0:
                        f2load(0)
                        f2load(1)
                    if n >= 1:
                        s.wait_ge(mSxa, 16 * n)
                    s.dma_start(out=sxa[it % 2][:],
                                in_=x_out[t0:t0 + 256, :].rearrange("(s p) c -> p s c", p=128)
                                ).then_inc(mLxa, 16)
                    if n + 2 < NTOT:
                        f2load(n + 2)
                    s.wait_ge(mVy, n + 1)
                    s.dma_start(out=x_out[t0:t0 + 256, :].rearrange("(s p) c -> p s c", p=128),
                                in_=sxa[it % 2][:]).then_inc(mSxa, 16)
            s.wait_ge(mSxa, 16 * NIT * E)

        @blk.tensor
        def _(t):
            h2w(t)
            for e in range(E):
                t.wait_ge(mLw, 32 * (e + 1))
                for it in range(NIT):
                    n = NIT * e + it
                    t.wait_ge(mLf2, 128 * (n + 1))
                    for fc in range(16):
                        if 16 * n + fc >= 2:
                            t.wait_ge(mAh, 16 * n + fc - 1)
                        for kk in range(8):
                            ins = t.matmul(ph_[fc % 2][:], sW1m[e % 2][:, kk, fc * 128:(fc + 1) * 128],
                                           sf2t[it % 2][:, kk, :],
                                           start=(kk == 0), stop=(kk == 7))
                        ins.then_inc(mTh, 1)
                    if n >= 1:
                        t.wait_ge(mVy, n)
                    for fc in range(16):
                        t.wait_ge(mAh, 16 * n + fc + 1)
                        for sub in range(2):
                            for half in range(2):
                                ins = t.matmul(po[2 * sub + half][:],
                                               sh[:, fc, sub * 128:(sub + 1) * 128],
                                               sW2m[e % 2][:, fc, half * 512:(half + 1) * 512],
                                               start=(fc == 0), stop=(fc == 15))
                    ins.then_inc(mTy, 1)

        @blk.scalar
        def _(s):
            h2w(s)
            for e in range(E):
                for it in range(NIT):
                    n = NIT * e + it
                    for fc in range(16):
                        s.wait_ge(mTh, 16 * n + fc + 1)
                        s.activation(sh[:, fc, :], ph_[fc % 2][:], AF.Gelu).then_inc(mAh, 1)

        @blk.vector
        def _(v):
            h2w(v)
            for e in range(E):
                for it in range(NIT):
                    n = NIT * e + it
                    v.wait_ge(mTy, n + 1)
                    v.wait_ge(mLxa, 16 * (n + 1))
                    for sub in range(2):
                        wcol = sroute4[:, 2 * it + sub, e:e + 1]
                        for half in range(2):
                            ins = v.scalar_tensor_tensor(
                                sxa[it % 2][:, sub, half * 512:(half + 1) * 512],
                                po[2 * sub + half][:], wcol,
                                sxa[it % 2][:, sub, half * 512:(half + 1) * 512],
                                ALU.mult, ALU.add)
                    ins.then_inc(mVy, 1)
    return


def _l1f_masks():
    il = np.arange(512)[None, :]
    jl = np.arange(128)[:, None]
    m = np.zeros((128, 4, 2, 512), np.float32)
    for s_ in range(4):
        m[:, s_, 0, :] = (il >= 128 * s_ + jl).astype(np.float32)
        m[:, s_, 1, :] = m[:, s_, 0, :]
    return np.ascontiguousarray(m.reshape(128, 4096))


def _host_consts():
    perm = np.concatenate([np.arange(0, C, 2), np.arange(1, C, 2)])
    tpos = np.arange(T, dtype=np.float64)
    jidx = np.arange(C // 2, dtype=np.float64)
    ang = np.outer(T - 1 - tpos, jidx)
    cosfm = np.cos(ang).T.astype(np.float32).copy()
    sinfm = np.sin(ang).T.astype(np.float32).copy()
    il = np.arange(256)[None, :]
    jl = np.arange(128)[:, None]
    maskA = (il >= jl).astype(np.float32)
    maskB = (il >= 128 + jl).astype(np.float32)
    return perm, cosfm, sinfm, maskA, maskB


def build_program(n_phases=4, detect_races=True):
    nc = bass.Bass("TRN2", target_bir_lowering=False, debug=False, detect_race_conditions=detect_races)
    top = ExitStack()
    env = Env(nc, top)
    d = env.d

    def din(name, shape, dt=F32):
        d[name] = nc.dram_tensor(name, shape, dt, kind="ExternalInput").ap()

    din("x0", [T, C])
    din("l0_wq", [C, C]); din("l0_wk", [C, C])
    din("l0_bq", [1, C]); din("l0_bk", [1, C])
    din("l0_wvT", [C, C]); din("l0_wp", [C, C])
    din("l0_bv128", [128, 8]); din("l0_bp", [1, C])
    din("cosfm", [C // 2, T]); din("sinfm", [C // 2, T])
    din("l0_maskA", [128, 256]); din("l0_maskB", [128, 256])
    din("idf32", [128, 128])
    din("l0f_centT", [C, 8])
    din("l0f_w1T", [8 * C, 4096]); din("l0f_w2T", [8 * 4096, C])
    din("l0f_gam128", [8 * 128, 8]); din("l0f_beta128", [8 * 128, 8])
    din("l0f_b1c128", [8 * 128, 32]); din("l0f_b2", [8, C])
    din("l1a_qkvwT", [C, 3072]); din("l1a_qkvb128", [128, 24])
    din("l1a_sel2a", [128, 2]); din("l1a_sel2b", [2, 128])
    din("l1a_kgb", [128, 4]); din("l1a_temp128", [128, 8])
    din("l1a_gaint", [4 * 128, 1024])
    din("l1a_atnb", [128, C])
    din("l1a_p1wT", [C, C]); din("l1a_p1b128", [128, 8])
    din("l1a_p2wT", [C, C]); din("l1a_p2b", [1, C])
    din("l1f_ln1b", [128, C]); din("l1f_ln2b", [128, C])
    din("l1f_cattn", [C, 3072]); din("l1f_cproj", [C, C])
    din("l1f_router", [C, 8]); din("l1f_masks", [128, 4096])
    din("l1f_w1", [C, 16384]); din("l1f_w2", [16384, C])

    out = nc.dram_tensor("out", [T, C], F32, kind="ExternalOutput").ap()

    streams = [d["x0"]]
    names = ["x1", "x2", "x3"]
    for i in range(n_phases - 1):
        streams.append(nc.dram_tensor(names[i], [T, C], F32, kind="Internal").ap())
    streams.append(out)
    assert len(streams) == n_phases + 1

    fence = []
    fence = phase_l0_attn(env, streams[0], streams[1], fence)
    if n_phases >= 2:
        fence = env.phase_boundary(fence)
        fence = phase_l0_ffn(env, streams[1], streams[2], fence)
    if n_phases >= 3:
        fence = env.phase_boundary(fence)
        fence = phase_l1_attn(env, streams[2], streams[3], fence)
    if n_phases >= 4:
        fence = env.phase_boundary(fence)
        fence = phase_l1_ffn(env, streams[3], streams[4], fence)

    top.close()
    return nc


def prep_in_maps(inputs):
    perm, cosfm, sinfm, maskA, maskB = _host_consts()
    wq = np.ascontiguousarray(inputs["L0_attn_Wq"][:, perm])
    wk = np.ascontiguousarray(inputs["L0_attn_Wk"][:, perm])
    bq = np.ascontiguousarray(inputs["L0_attn_bq"][perm])[None, :]
    bk = np.ascontiguousarray(inputs["L0_attn_bk"][perm])[None, :]
    wvT = np.ascontiguousarray(inputs["L0_attn_Wv"].T)
    wp = np.ascontiguousarray(inputs["L0_attn_Wp"])
    bv128 = np.ascontiguousarray(inputs["L0_attn_bv"].reshape(8, 128).T)
    bp = np.ascontiguousarray(inputs["L0_attn_bp"])[None, :]
    x = inputs["L0_attn_x"]

    idf32 = np.eye(128, dtype=np.float32)
    centT = np.ascontiguousarray(inputs["L0_ffn_centroids"].T)
    w1T = np.ascontiguousarray(inputs["L0_ffn_W1"].transpose(0, 2, 1)).reshape(8 * C, 4096)
    w2T = np.ascontiguousarray(inputs["L0_ffn_W2"].transpose(0, 2, 1)).reshape(8 * 4096, C)
    gam128 = np.ascontiguousarray(inputs["L0_ffn_gamma"].reshape(8, 8, 128).transpose(0, 2, 1)).reshape(8 * 128, 8)
    bet128 = np.ascontiguousarray(inputs["L0_ffn_beta"].reshape(8, 8, 128).transpose(0, 2, 1)).reshape(8 * 128, 8)
    b1c128 = np.ascontiguousarray(inputs["L0_ffn_b1"].reshape(8, 32, 128).transpose(0, 2, 1)).reshape(8 * 128, 32)
    b2 = np.ascontiguousarray(inputs["L0_ffn_b2"])
    qkvwT = np.ascontiguousarray(inputs["L1_attn_qkv_w"].T)
    qkvb128 = np.ascontiguousarray(inputs["L1_attn_qkv_b"].reshape(24, 128).T)
    sel2a = np.zeros((128, 2), np.float32); sel2a[:64, 0] = 1.0; sel2a[64:, 1] = 1.0
    sel2b = np.ascontiguousarray(sel2a.T)
    kgb = np.stack([np.tile(inputs["L1_attn_kn_g"], 2), np.tile(inputs["L1_attn_kn_b"], 2),
                    np.tile(inputs["L1_attn_vn_g"], 2), np.tile(inputs["L1_attn_vn_b"], 2)],
                   axis=1).astype(np.float32)
    gaint = np.ascontiguousarray(np.broadcast_to(
        np.stack([np.tile(inputs["L1_attn_kn_g"], 16), np.tile(inputs["L1_attn_kn_b"], 16),
                  np.tile(inputs["L1_attn_vn_g"], 16), np.tile(inputs["L1_attn_vn_b"], 16)]
                 )[:, None, :], (4, 128, 1024)).reshape(4 * 128, 1024)).astype(np.float32)
    tarr = inputs["L1_attn_temperature"].reshape(16)
    temp128 = np.stack([np.repeat(tarr[2 * p:2 * p + 2], 64) for p in range(8)],
                       axis=1).astype(np.float32)
    atnb = np.zeros((128, C), np.float32)
    ab = inputs["L1_attn_attn_bias"]
    for p in range(8):
        atnb[0:64, p * 128:p * 128 + 64] = ab[2 * p]
        atnb[64:128, p * 128 + 64:p * 128 + 128] = ab[2 * p + 1]
    p1wT = np.ascontiguousarray(inputs["L1_attn_p1_w"].T)
    p1b128 = np.ascontiguousarray(inputs["L1_attn_p1_b"].reshape(8, 128).T)
    p2wT = np.ascontiguousarray(inputs["L1_attn_p2_w"].T)
    p2b = np.ascontiguousarray(inputs["L1_attn_p2_b"])[None, :]
    shared = dict(l0_wq=wq, l0_wk=wk, l0_bq=bq, l0_bk=bk, l0_wvT=wvT, l0_wp=wp,
                  l0_bv128=bv128, l0_bp=bp, cosfm=cosfm, sinfm=sinfm,
                  l0_maskA=maskA, l0_maskB=maskB,
                  idf32=idf32, l0f_centT=centT, l0f_w1T=w1T, l0f_w2T=w2T,
                  l0f_gam128=gam128, l0f_beta128=bet128, l0f_b1c128=b1c128, l0f_b2=b2,
                  l1a_qkvwT=qkvwT, l1a_qkvb128=qkvb128, l1a_sel2a=sel2a, l1a_sel2b=sel2b,
                  l1a_kgb=kgb, l1a_temp128=temp128, l1a_atnb=atnb, l1a_gaint=gaint,
                  l1a_p1wT=p1wT, l1a_p1b128=p1b128, l1a_p2wT=p2wT, l1a_p2b=p2b,
                  l1f_ln1b=np.ascontiguousarray(np.broadcast_to(inputs["L1_ffn_ln1_w"], (128, C))),
                  l1f_ln2b=np.ascontiguousarray(np.broadcast_to(inputs["L1_ffn_ln2_w"], (128, C))),
                  l1f_cattn=np.ascontiguousarray(inputs["L1_ffn_c_attn_w"]),
                  l1f_cproj=np.ascontiguousarray(inputs["L1_ffn_c_proj_w"]),
                  l1f_router=np.ascontiguousarray(inputs["L1_ffn_router_w"]),
                  l1f_masks=_l1f_masks(),
                  l1f_w1=np.ascontiguousarray(inputs["L1_ffn_w1"]),
                  l1f_w2=np.ascontiguousarray(inputs["L1_ffn_w2"]))
    in_maps = []
    for b in range(x.shape[0]):
        m = dict(shared)
        m["x0"] = np.ascontiguousarray(x[b][:T])
        in_maps.append(m)
    return in_maps


def kernel(**inputs):
    in_maps = prep_in_maps(inputs)
    nc = build_program(N_PHASES)
    res = run_bass_kernel_spmd(nc, in_maps, core_ids=list(range(NCORES)))
    return np.stack([r["out"] for r in res.results], axis=0)
